# version 03
# baseline (speedup 1.0000x reference)
import numpy as np
import ml_dtypes
import concourse.bass as bass
import concourse.mybir as mybir
from concourse.bass_utils import run_bass_kernel_spmd
from concourse.ap import AP
from contextlib import ExitStack

F32 = mybir.dt.float32
BF16 = mybir.dt.bfloat16
AX = mybir.AxisListType.X
ALU = mybir.AluOpType
AF = mybir.ActivationFunctionType

BSZ, SEQ, D = 4, 4096, 2048
NH, HD, WIN = 16, 128, 1024
NT_OWN, HALO = 16, 8
NT_EXT = NT_OWN + HALO
RING = 10
NBLK = 9
NCH = 16
SCALE_Q = float(128.0 ** -0.5)
NEG = -1.0e30
ER = 128
GAW = 8 + ER

DEBUG = False


def bf(x):
    return np.asarray(x, dtype=ml_dtypes.bfloat16)



PARANOID = False


class Prog:
    SEMS = ("qls", "qst", "qlf", "qsf", "pe", "dv", "ac", "po")
    DMA_SEMS = ("qls", "qst", "qlf", "qsf")
    OWNER = {"pe": "pe", "dv": "dv", "ac": "ac", "po": "st", "qls": None,
             "qst": None, "qlf": None, "qsf": None}

    def __init__(self):
        self.stream = {k: [] for k in ("pe", "dv", "ac", "ld", "st")}
        self.cnt = {s: 0 for s in self.SEMS}

    def op(self, eng, fn, waits=(), inc=None, drain=False):
        w = []
        for x in waits:
            if x is None:
                continue
            if isinstance(x, list):
                w += [t for t in x if t is not None]
            else:
                w.append(x)
        if PARANOID:
            w += [(s, self.cnt[s]) for s in ("pe", "dv", "ac")]
        tok = None
        amt = 0
        if inc is not None:
            amt = 16 if inc in self.DMA_SEMS else 1
            self.cnt[inc] += amt
            tok = (inc, self.cnt[inc])
        self.stream[eng].append((fn, w, inc, amt, drain))
        return tok

    def load(self, out, in_, waits=()):
        return self.op("ld", lambda e, o=out, i=in_: e.dma_start(out=o, in_=i),
                       waits=waits, inc="qls")

    def store(self, out, in_, waits=()):
        return self.op("st", lambda e, o=out, i=in_: e.dma_start(out=o, in_=i),
                       waits=waits, inc="qst")

    def lfence(self, kern):
        return self.op("ld", lambda e: e.dma_start(out=kern.fence_sb[:],
                                                   in_=kern.fin_d[:]),
                       inc="qlf")

    def sfence(self, kern):
        return self.op("st", lambda e: e.dma_start(out=kern.fout_d[:],
                                                   in_=kern.fence_sb[:]),
                       inc="qsf")

    def mm(self, out, lhsT, rhs, start, stop, waits=(), inc=False):
        return self.op(
            "pe",
            lambda e, o=out, l=lhsT, r=rhs, s0=start, s1=stop: e.matmul(
                o, l, r, start=s0, stop=s1),
            waits=waits, inc="pe" if inc else None)

    def replay(self, ename, eng, sems):
        seen = {}
        for fn, waits, inc, amt, drain in self.stream[ename]:
            best = {}
            for (s, v) in waits:
                if Prog.OWNER[s] == ename:
                    continue
                if best.get(s, 0) < v:
                    best[s] = v
            for s, v in best.items():
                if seen.get(s, 0) >= v:
                    continue
                eng.wait_ge(sems[s], v)
                seen[s] = v
            inst = fn(eng)
            if drain:
                inst = eng.drain()
            if inc is not None:
                inst.then_inc(sems[inc], amt)


class PsumBlock:
    def __init__(self, alloc, banks, waits):
        self.a = alloc
        self.banks = banks
        self.waits = waits
        self.c0 = banks[0] * 512

    def ap(self, col0, col1):
        return self.a.ps[:, self.c0 + col0:self.c0 + col1]

    def free(self, *toks):
        toks = [t for t in toks if t is not None]
        for b in self.banks:
            self.a.tok[b] = toks


class PsumAlloc:
    def __init__(self, ps):
        self.ps = ps
        self.tok = [[] for _ in range(8)]
        self.ptr = 0

    def alloc(self, nbanks):
        if self.ptr + nbanks > 8:
            self.ptr = 0
        b0 = self.ptr
        self.ptr = (self.ptr + nbanks) % 8
        banks = list(range(b0, b0 + nbanks))
        waits = []
        for b in banks:
            waits += self.tok[b]
        return PsumBlock(self, banks, waits)



def _rope_tables(positions):
    ntiles = len(positions) // 128
    inv = 1.0 / (10000.0 ** (np.arange(0, HD, 2, dtype=np.float64) / HD))
    pos = np.asarray(positions, np.float64)
    fr = pos[:, None] * inv[None, :]
    c = np.cos(fr).reshape(ntiles, 128, 64).transpose(1, 0, 2).reshape(
        128, ntiles * 64)
    s = (-np.sin(fr)).reshape(ntiles, 128, 64).transpose(1, 0, 2).reshape(
        128, ntiles * 64)
    return bf(c), bf(s)


def _host_prepare(x, wq, wk, wv, wo, gate_w, lora_A, lora_B):
    qq = np.arange(128)[:, None]
    kk = np.arange(128)[None, :]
    m_w = bf(np.where(kk > qq, 0.0, NEG))
    m_c = bf(np.where(kk <= qq, 0.0, NEG))
    ident = bf(np.eye(128))
    gA = np.ascontiguousarray(
        np.concatenate([np.asarray(gate_w).T.astype(np.float32, copy=False),
                        np.asarray(lora_A).reshape(ER, D).T.astype(
                            np.float32, copy=False)], axis=1))
    bstack = np.ascontiguousarray(
        np.asarray(lora_B).transpose(0, 2, 1).reshape(ER, D).astype(
            np.float32, copy=False))
    shared = dict(wq=np.asarray(wq, np.float32), wk=np.asarray(wk, np.float32),
                  wv=np.asarray(wv, np.float32), wo=np.asarray(wo, np.float32),
                  gA=gA, bstack=bstack, ident=ident, m_w=m_w, m_c=m_c,
                  ident_f32=np.eye(128, dtype=np.float32))
    x = np.asarray(x, np.float32)
    in_maps = []
    for b in range(BSZ):
        for half in range(2):
            if half == 0:
                xe = np.concatenate(
                    [np.zeros((HALO * 128, D), np.float32), x[b, 0:2048]], axis=0)
                ext_pos = np.concatenate([np.zeros(HALO * 128), np.arange(0, 2048)])
            else:
                xe = np.ascontiguousarray(x[b, 1024:4096])
                ext_pos = np.arange(1024, 4096)
            c2h, s2h = _rope_tables(ext_pos)
            npad = np.zeros((128, NT_OWN), np.float32)
            if half == 0:
                for i in range(NT_OWN):
                    npb = max(0, HALO - i)
                    if npb == 0:
                        continue
                    cnt = np.full(128, 128.0 * (npb - 1))
                    cnt += 127.0 - np.arange(128)
                    npad[:, i] = cnt
            m = dict(shared)
            m.update(x_ext=np.ascontiguousarray(xe.reshape(NT_EXT, 128, D)),
                     c2=c2h, s2=s2h, npad=npad,
                     fence_in=np.zeros((128, 16), np.float32))
            in_maps.append(m)
    return in_maps



class Kern:
    def __init__(self):
        self.nc = bass.Bass("TRN2", target_bir_lowering=False)
        self.ctx = ExitStack()
        self.P = Prog()
        self._declare()
        self.pa = PsumAlloc(self.ps_all)

    def _declare(self):
        nc = self.nc
        dt = lambda n, s, d, k: nc.dram_tensor(n, s, d, kind=k).ap()
        self.xd = dt("x_ext", [NT_EXT, 128, D], F32, "ExternalInput")
        self.wqd = dt("wq", [D, D], F32, "ExternalInput")
        self.wkd = dt("wk", [D, D], F32, "ExternalInput")
        self.wvd = dt("wv", [D, D], F32, "ExternalInput")
        self.wod = dt("wo", [D, D], F32, "ExternalInput")
        self.gAd = dt("gA", [D, GAW], F32, "ExternalInput")
        self.bsd = dt("bstack", [ER, D], F32, "ExternalInput")
        self.c2d = dt("c2", [128, NT_EXT * 64], BF16, "ExternalInput")
        self.s2d = dt("s2", [128, NT_EXT * 64], BF16, "ExternalInput")
        self.npd = dt("npad", [128, NT_OWN], F32, "ExternalInput")
        self.idd = dt("ident", [128, 128], BF16, "ExternalInput")
        self.mwd = dt("m_w", [128, 128], BF16, "ExternalInput")
        self.mcd = dt("m_c", [128, 128], BF16, "ExternalInput")
        self.fin_d = dt("fence_in", [128, 16], F32, "ExternalInput")
        self.idfd = dt("ident_f32", [128, 128], F32, "ExternalInput")
        self.fout_d = dt("fence_out", [128, 16], F32, "Internal")
        self.yd = dt("y", [NT_OWN, 128, D], F32, "ExternalOutput")
        SK = "ExternalOutput" if DEBUG else "Internal"
        self.kTs = dt("kT_s", [NT_EXT, 128, D], BF16, SK)
        self.vss = dt("v_s", [NT_EXT, 128, D], BF16, SK)
        self.us_ = dt("u_s", [NT_OWN, 128, D], F32, SK)
        self.unTs = dt("unT_s", [NT_OWN, 128, D], BF16, SK)
        self.qTs = dt("qT_s", [NT_OWN, 128, 1024], BF16, SK)
        self.oTs = dt("oT_s", [NT_OWN, 128, 1024], BF16, SK)

        sb = lambda n, s, d: self.ctx.enter_context(nc.sbuf_tensor(n, s, d))
        self.fence_sb = sb("fence_sb", [128, 16], F32)
        self.id_sb = sb("id_sb", [128, 128], BF16)
        self.idf_sb = sb("idf_sb", [128, 128], F32)
        self.diag = [sb("diag0", [128, 128], BF16), sb("diag1", [128, 128], BF16)]
        self.mw_sb = sb("mw_sb", [128, 128], BF16)
        self.mc_sb = sb("mc_sb", [128, 128], BF16)
        self.np_sb = sb("np_sb", [128, NT_OWN], F32)
        self.c2_sb = sb("c2_sb", [128, NT_EXT * 64], BF16)
        self.s2_sb = sb("s2_sb", [128, NT_EXT * 64], BF16)
        self.gA_sb = sb("gA_sb", [128, NCH * GAW], BF16)
        self.bs_sb = sb("bs_sb", [128, D], BF16)
        self.w1 = sb("w1", [128, NCH * D], BF16)
        self.w2 = sb("w2", [128, NCH * D], BF16)
        self.xu = [sb("xu0", [128, D], F32), sb("xu1", [128, D], F32)]
        self.xn_sb = sb("xn_sb", [128, D], BF16)
        self.xnT_sb = sb("xnT_sb", [128, D], BF16)
        self.un_sb = sb("un_sb", [128, D], BF16)
        self.unT = [sb("unT0", [128, D], BF16), sb("unT1", [128, D], BF16)]
        self.buf_a = sb("buf_a", [128, D], BF16)
        self.buf_b = sb("buf_b", [128, D], BF16)
        self.kvo = sb("kvo", [128, D], BF16)
        self.p_sb = sb("p_sb", [128, NBLK * 128], BF16)
        self.pt_sb = sb("pt_sb", [128, NBLK * 128], BF16)
        self.oT_sb = sb("oT_sb", [128, D], BF16)
        mk2 = lambda n, w: [sb(n + "0", [128, w], F32), sb(n + "1", [128, w], F32)]
        self.st1 = mk2("st1", 1)
        self.ssq = mk2("ssq", 1)
        self.mean = mk2("mean", 1)
        self.nmean = mk2("nmean", 1)
        self.msq = mk2("msq", 1)
        self.var = mk2("var", 1)
        self.lnv = mk2("lnv", 1)
        self.rstd = mk2("rstd", 1)
        self.g8a = mk2("g8a", 8)
        self.g8b = mk2("g8b", 8)
        self.gm1 = mk2("gm1", 1)
        self.gm2 = mk2("gm2", 1)
        self.gdl = mk2("gdl", 1)
        self.ge = mk2("ge", 1)
        self.gden = mk2("gden", 1)
        self.grcp = mk2("grcp", 1)
        self.comb8 = mk2("comb8", 8)
        self.comb_er = [sb("comber", [128, ER], F32)] * 2
        self.ch_sb = [sb("ch0", [128, ER], BF16), sb("ch1", [128, ER], BF16)]
        self.chT_sb = [sb("chT0", [128, 128], BF16), sb("chT1", [128, 128], BF16)]
        self.nm_all = sb("nm_all", [128, 8], F32)
        self.sum_all = sb("sum_all", [128, 8], F32)
        self.enm = sb("enm", [128, 1], F32)
        self.corr = sb("corr", [128, 1], F32)
        self.rcp_s = sb("rcp_s", [128, 1], F32)
        self.ps_all = self.ctx.enter_context(
            nc.psum_tensor("ps_all", [128, 4096], F32))

    def ring_k(self, phase, slot):
        a = self.w2 if phase == 1 else self.w1
        return a[:, slot * 1024:(slot + 1) * 1024]

    def ring_v(self, phase, slot):
        a = self.w2 if phase == 1 else self.w1
        return a[:, (RING + slot) * 1024:(RING + slot + 1) * 1024]

    def _ring_load(self, phase, e, slot, waits):
        P = self.P
        cols = slice(0, 1024) if phase == 1 else slice(1024, 2048)
        P.load(self.ring_k(phase, slot), self.kTs[e][:, cols], waits=waits)
        return P.load(self.ring_v(phase, slot), self.vss[e][:, cols], waits=())

    def wload(self, wdram, dest, waits):
        P = self.P
        for g in range(4):
            src = wdram[g * 512:(g + 1) * 512, :].rearrange(
                "(c p) d -> p c d", p=128)
            dst = dest[:, g * 4 * D:(g + 1) * 4 * D].rearrange(
                "p (c d) -> p c d", c=4)
            P.op("st", lambda e, o=dst, i=src: e.dma_start(out=o, in_=i),
                 waits=waits, inc="qst")
        return P.sfence(self)

    def emit_ln(self, x_src, out_bf, par, ld_tok, out_war, stats_war,
                sq_war=None):
        P = self.P
        s = lambda b: b[par]
        t_s1 = P.op("dv", lambda e: e.reduce_sum(s(self.st1)[:], x_src[:],
                                                 axis=AX),
                    waits=[ld_tok, stats_war], drain=True, inc="dv")
        t_sq = P.op("ac", lambda e: e.activation(
            self.buf_b[:], x_src[:], AF.Square, accum_out=s(self.ssq)[:, 0:1]),
            waits=[ld_tok, stats_war, sq_war], drain=True, inc="ac")
        P.op("st", lambda e: e.tensor_scalar(
            s(self.mean)[:], s(self.st1)[:], 1.0 / D, None, ALU.mult),
            waits=[t_s1, stats_war])
        P.op("st", lambda e: e.tensor_scalar(
            s(self.nmean)[:], s(self.mean)[:], -1.0, None, ALU.mult))
        P.op("st", lambda e: e.tensor_mul(
            s(self.msq)[:], s(self.mean)[:], s(self.mean)[:]))
        P.op("st", lambda e: e.tensor_scalar(
            s(self.var)[:], s(self.ssq)[:], 1.0 / D, 1e-5, ALU.mult, ALU.add),
            waits=[t_sq])
        t_po = P.op("st", lambda e: e.tensor_sub(
            s(self.var)[:], s(self.var)[:], s(self.msq)[:]), inc="po")
        P.op("ac", lambda e: e.activation(
            s(self.lnv)[:], s(self.var)[:, 0:1], AF.Ln),
            waits=[t_po], drain=True)
        t_r = P.op("ac", lambda e: e.activation(
            s(self.rstd)[:], s(self.lnv)[:, 0:1], AF.Exp, scale=-0.5),
            drain=True, inc="ac")
        t_ap = P.op("dv", lambda e: e.tensor_scalar(
            out_bf[:], x_src[:], s(self.nmean)[:, 0:1], s(self.rstd)[:, 0:1],
            ALU.add, ALU.mult), waits=[t_r, t_po, out_war], drain=True, inc="dv")
        return t_ap, t_sq

    def emit_transposes(self, src_bf, dst_bf, src_tok, dst_war):
        P = self.P
        pe_last = None
        blks = []
        for half in range(2):
            blk = self.pa.alloc(2)
            t_mm = None
            for c in range(8):
                cc = half * 8 + c
                t_mm = P.mm(blk.ap(c * 128, (c + 1) * 128),
                            src_bf[:, cc * 128:(cc + 1) * 128], self.id_sb[:],
                            True, True,
                            waits=([src_tok, self.t_id] + blk.waits)
                            if c == 0 else (),
                            inc=(c == 7))
            blks.append((blk, t_mm))
            pe_last = t_mm
        for half, (blk, t_mm) in enumerate(blks):
            P.op("ac", lambda e, h=half, b=blk: e.activation(
                dst_bf[:, h * 1024:(h + 1) * 1024], b.ap(0, 1024), AF.Copy),
                waits=[t_mm, dst_war])
        ev = P.op("ac", lambda e: e.drain(), inc="ac")
        for blk, _ in blks:
            blk.free(ev)
        return pe_last, ev

    def emit_proj(self, lhs_fn, rhs_sb, lhs_tok, rhs_tok, out_bf=None,
                  out_war=None, x_dst=None, x_tok=None, evac_scale=None):
        P = self.P
        pe_last = None
        blks = []
        for half in range(2):
            blk = self.pa.alloc(2)
            t_mm = None
            for c in range(NCH):
                for s2 in range(2):
                    col = half * 1024 + s2 * 512
                    t_mm = P.mm(
                        blk.ap(s2 * 512, (s2 + 1) * 512), lhs_fn(c),
                        rhs_sb[:, c * D + col:c * D + col + 512],
                        start=(c == 0), stop=(c == NCH - 1),
                        waits=([lhs_tok, rhs_tok] + blk.waits)
                        if (c == 0 and s2 == 0) else (),
                        inc=(c == NCH - 1 and s2 == 1))
            blks.append((blk, t_mm))
            pe_last = t_mm
        eng = "dv" if x_dst is not None else "ac"
        for half, (blk, t_mm) in enumerate(blks):
            if x_dst is not None:
                P.op("dv", lambda e, h=half, b=blk: e.tensor_add(
                    x_dst[:, h * 1024:(h + 1) * 1024],
                    x_dst[:, h * 1024:(h + 1) * 1024], b.ap(0, 1024)),
                    waits=[t_mm, x_tok])
            else:
                kw = {} if evac_scale is None else {"scale": evac_scale}
                P.op("ac", lambda e, h=half, b=blk, kw=kw: e.activation(
                    out_bf[:, h * 1024:(h + 1) * 1024], b.ap(0, 1024),
                    AF.Copy, **kw),
                    waits=[t_mm, out_war])
        out_tok = P.op(eng, lambda e: e.drain(), inc=eng)
        for blk, _ in blks:
            blk.free(out_tok)
        return pe_last, out_tok

    def emit_rope(self, src_bf, tile_e, src_tok):
        P = self.P
        off = tile_e * 64
        csl = self.c2_sb[:, off:off + 64]
        ssl = self.s2_sb[:, off:off + 64]
        cbc = AP(csl.tensor, csl.offset,
                 [list(csl.ap[0]), [0, NH], [0, 2], [1, 64]])
        sbc = AP(ssl.tensor, ssl.offset,
                 [list(ssl.ap[0]), [0, NH], [0, 2], [1, 64]])
        sv = src_bf[:]
        rot = AP(sv.tensor, sv.offset + 64,
                 [list(sv.ap[0]), [128, NH], [-64, 2], [1, 64]])
        t4 = lambda t: t[:].rearrange("p (h x j) -> p h x j", h=NH, x=2)
        P.op("dv", lambda e: e.tensor_tensor(
            t4(self.buf_b), t4(src_bf), cbc, ALU.mult),
            waits=[src_tok, self.t_tab])
        P.op("dv", lambda e: e.tensor_tensor(t4(self.xn_sb), rot, sbc, ALU.mult))
        fh = lambda t: AP(t[:].tensor, t[:].offset,
                          [list(t[:].ap[0]), [128, NH], [1, 64]])
        sh = lambda t: AP(t[:].tensor, t[:].offset + 64,
                          [list(t[:].ap[0]), [128, NH], [1, 64]])
        P.op("dv", lambda e: e.tensor_tensor(
            fh(src_bf), fh(self.buf_b), fh(self.xn_sb), ALU.add))
        return P.op("dv", lambda e: e.tensor_tensor(
            sh(src_bf), sh(self.buf_b), sh(self.xn_sb), ALU.subtract),
            drain=True, inc="dv")

    def emit_ffn(self, actT_sb, actT_tok, x_dst, x_tok, par, war):
        P = self.P
        s = lambda b: b[par]
        gblk = self.pa.alloc(1)
        t_g = None
        for c in range(NCH):
            t_g = P.mm(gblk.ap(0, GAW), actT_sb[:, c * 128:(c + 1) * 128],
                       self.gA_sb[:, c * GAW:(c + 1) * GAW],
                       start=(c == 0), stop=(c == NCH - 1),
                       waits=([actT_tok, self.t_gA] + gblk.waits)
                       if c == 0 else (),
                       inc=(c == NCH - 1))
        lg = gblk.ap(0, 8)
        hh = gblk.ap(8, GAW)
        P.op("dv", lambda e: e.reduce_max(s(self.gm1)[:], lg, axis=AX),
             waits=[t_g, war], drain=True)
        P.op("dv", lambda e: e.tensor_single_scalar(
            s(self.g8a)[:], lg, s(self.gm1)[:, 0:1], ALU.is_ge), drain=True)
        P.op("dv", lambda e: e.scalar_tensor_tensor(
            s(self.g8b)[:], s(self.g8a)[:], NEG, lg, ALU.mult, ALU.add),
            drain=True)
        P.op("dv", lambda e: e.reduce_max(s(self.gm2)[:], s(self.g8b)[:], axis=AX),
             drain=True)
        P.op("dv", lambda e: e.tensor_single_scalar(
            s(self.g8b)[:], s(self.g8b)[:], s(self.gm2)[:, 0:1], ALU.is_ge),
            drain=True)
        t_d = P.op("dv", lambda e: e.tensor_sub(
            s(self.gdl)[:], s(self.gm2)[:], s(self.gm1)[:]), drain=True, inc="dv")
        t_e = P.op("ac", lambda e: e.activation(
            s(self.ge)[:], s(self.gdl)[:, 0:1], AF.Exp),
            waits=[t_d], drain=True, inc="ac")
        P.op("dv", lambda e: e.tensor_scalar(
            s(self.gden)[:], s(self.ge)[:], 1.0, None, ALU.add),
            waits=[t_e], drain=True)
        P.op("dv", lambda e: e.reciprocal(s(self.grcp)[:], s(self.gden)[:]),
             drain=True)
        P.op("dv", lambda e: e.scalar_tensor_tensor(
            s(self.comb8)[:], s(self.g8b)[:], s(self.ge)[:, 0:1], s(self.g8a)[:],
            ALU.mult, ALU.add), drain=True)
        P.op("dv", lambda e: e.tensor_single_scalar(
            s(self.comb8)[:], s(self.comb8)[:], s(self.grcp)[:, 0:1], ALU.mult),
            drain=True)
        c8 = s(self.comb8)[:]
        cbc = AP(c8.tensor, c8.offset, [list(c8.ap[0]), [1, 8], [0, 16]])
        P.op("dv", lambda e: e.tensor_copy(
            s(self.comb_er)[:].rearrange("p (x r) -> p x r", x=8), cbc),
            drain=True)
        t_ch = P.op("dv", lambda e: e.tensor_mul(
            s(self.ch_sb)[:], hh, s(self.comb_er)[:]), drain=True, inc="dv")
        gblk.free(t_ch)
        cblk = self.pa.alloc(1)
        t_ct = P.mm(cblk.ap(0, 128), s(self.ch_sb)[:], self.id_sb[:], True, True,
                    waits=[t_ch] + cblk.waits, inc=True)
        t_ce = P.op("dv", lambda e, b=cblk: e.tensor_copy(
            s(self.chT_sb)[:], b.ap(0, 128)), waits=[t_ct], drain=True, inc="dv")
        cblk.free(t_ce)
        blks = []
        for half in range(2):
            blk = self.pa.alloc(2)
            t_mm = None
            for s2 in range(2):
                col = half * 1024 + s2 * 512
                t_mm = P.mm(blk.ap(s2 * 512, (s2 + 1) * 512), s(self.chT_sb)[:],
                            self.bs_sb[:, col:col + 512], True, True,
                            waits=([t_ce] + blk.waits) if s2 == 0 else (),
                            inc=(s2 == 1))
            blks.append((blk, t_mm))
        for half, (blk, t_mm) in enumerate(blks):
            P.op("dv", lambda e, h=half, b=blk: e.tensor_add(
                x_dst[:, h * 1024:(h + 1) * 1024],
                x_dst[:, h * 1024:(h + 1) * 1024], b.ap(0, 1024)),
                waits=[t_mm, x_tok])
        t_add = P.op("dv", lambda e: e.drain(), inc="dv")
        for blk, _ in blks:
            blk.free(t_add)
        return t_add, t_g

    def emit_attn_tile(self, phase, i, qsrc, q_tok, ring_tok, oT_off, oT_war):
        P = self.P
        evs = []
        t_av = None
        for h in range(8):
            sblk = self.pa.alloc(3)
            w0 = [q_tok, ring_tok, self.t_mask, self.t_id, self.t_np,
                  self.p_free] + sblk.waits
            P.mm(sblk.ap(0, 128), self.id_sb[:], self.mw_sb[:], True, False,
                 waits=w0)
            P.mm(sblk.ap(8 * 128, 9 * 128), self.id_sb[:], self.mc_sb[:],
                 True, False)
            t_s = None
            for b in range(NBLK):
                slot = (i + b) % RING
                t_s = P.mm(sblk.ap(b * 128, (b + 1) * 128),
                           qsrc[:, h * 128:(h + 1) * 128],
                           self.ring_k(phase, slot)[:, h * 128:(h + 1) * 128],
                           start=(b not in (0, 8)), stop=True,
                           inc=(b == NBLK - 1))
            t_nm = P.op("dv", lambda e, h=h, b=sblk: e.reduce_max(
                self.nm_all[:, h:h + 1], b.ap(0, NBLK * 128), axis=AX,
                negate=True), waits=[t_s], drain=True, inc="dv")
            P.op("ac", lambda e, h=h, b=sblk: e.activation(
                self.p_sb[:], b.ap(0, NBLK * 128), AF.Exp,
                bias=self.nm_all[:, h:h + 1],
                accum_out=self.sum_all[:, h:h + 1]),
                waits=[t_nm, self.p_free])
            t_exp = P.op("ac", lambda e, h=h: e.activation(
                self.enm[:], self.nm_all[:, h:h + 1], AF.Exp),
                drain=True, inc="ac")
            sblk.free(t_exp)
            P.op("st", lambda e, i=i: e.tensor_mul(
                self.corr[:], self.np_sb[:, i:i + 1], self.enm[:]),
                waits=[t_exp])
            t_corr = P.op("st", lambda e, h=h: e.tensor_sub(
                self.corr[:], self.sum_all[:, h:h + 1], self.corr[:]),
                inc="po")
            t_rcp = P.op("dv", lambda e: e.reciprocal(self.rcp_s[:], self.corr[:]),
                         waits=[t_corr], drain=True, inc="dv")
            rb = self.rcp_s[:]
            rbc = AP(rb.tensor, rb.offset, [list(rb.ap[0]), [0, 128]])
            dg = self.diag[h % 2]
            t_diag = P.op("st", lambda e, d=dg, rbc=rbc: e.tensor_tensor(
                d[:], self.idf_sb[:], rbc, ALU.mult),
                waits=[t_rcp, self.diag_war[h % 2]], inc="po")
            ptblk = self.pa.alloc(3)
            t_pt = None
            for b in range(NBLK):
                t_pt = P.mm(ptblk.ap(b * 128, (b + 1) * 128),
                            self.p_sb[:, b * 128:(b + 1) * 128], dg[:],
                            True, True,
                            waits=([t_diag, self.pt_free] + ptblk.waits)
                            if b == 0 else (),
                            inc=(b == NBLK - 1))
            self.diag_war[h % 2] = t_pt
            self.p_free = t_pt
            if h % 2 == 0:
                t_pte = P.op("ac", lambda e, b=ptblk: e.activation(
                    self.pt_sb[:], b.ap(0, NBLK * 128), AF.Copy),
                    waits=[t_pt], drain=True, inc="ac")
            else:
                t_pte = P.op("dv", lambda e, b=ptblk: e.tensor_copy(
                    self.pt_sb[:], b.ap(0, NBLK * 128)),
                    waits=[t_pt], drain=True, inc="dv")
            ptblk.free(t_pte)
            avblk = self.pa.alloc(1)
            for b in range(NBLK):
                slot = (i + b) % RING
                t_av = P.mm(avblk.ap(0, 128),
                            self.ring_v(phase, slot)[:, h * 128:(h + 1) * 128],
                            self.pt_sb[:, b * 128:(b + 1) * 128],
                            start=(b == 0), stop=(b == NBLK - 1),
                            waits=([t_pte] + avblk.waits) if b == 0 else (),
                            inc=(b == NBLK - 1))
            self.pt_free = t_av
            t_oev = P.op("ac", lambda e, h=h, b=avblk, o=oT_off: e.activation(
                self.oT_sb[:, o + h * 128:o + (h + 1) * 128], b.ap(0, 128),
                AF.Copy), waits=[t_av, oT_war], drain=True, inc="ac")
            avblk.free(t_oev)
            evs.append(t_oev)
        return t_av, evs

    def build(self):
        P = self.P
        P.load(self.fence_sb[:], self.fin_d[:])
        P.load(self.id_sb[:], self.idd[:])
        P.load(self.idf_sb[:], self.idfd[:])
        P.load(self.mw_sb[:], self.mwd[:])
        P.load(self.mc_sb[:], self.mcd[:])
        P.load(self.np_sb[:], self.npd[:])
        P.load(self.c2_sb[:], self.c2d[:])
        P.load(self.s2_sb[:], self.s2d[:])
        self.p_free = None
        self.pt_free = None
        self.diag_war = [None, None]
        xu_free = [[], []]

        src = self.gAd.rearrange("(c p) w -> p c w", p=128)
        dst = self.gA_sb[:].rearrange("p (c w) -> p c w", c=NCH)
        P.op("st", lambda e, o=dst, i=src: e.dma_start(out=o, in_=i),
             inc="qst")
        P.op("st", lambda e: e.dma_start(out=self.bs_sb[:], in_=self.bsd[:]),
             inc="qst")
        self.wload(self.wkd, self.w1, ())
        t_w = self.wload(self.wvd, self.w2, ())
        self.t_gA = t_w
        w1_tok = t_w
        w2_tok = t_w

        x_ld = [None] * NT_EXT
        P.load(self.xu[0][:], self.xd[0])
        P.load(self.xu[1][:], self.xd[1])
        lf0 = P.lfence(self)
        self.t_id = self.t_mask = self.t_np = self.t_tab = lf0
        x_ld[0] = lf0
        x_ld[1] = lf0

        kv_stored = [None] * NT_EXT
        unT_stored = [None] * NT_OWN
        xn_free = None
        xnT_free = None
        un_free = None
        unT_free = [[], []]
        ln_war = [None, None]
        gate_war = [None, None]
        bufa_free = None
        kvo_free = []
        v_pend = None
        w1_last = None
        w2_last = None
        rope_prev = None

        for e in range(NT_EXT):
            sl = e % 2
            t_xn, _ = self.emit_ln(self.xu[sl], self.xn_sb, sl, x_ld[e],
                                   xn_free, ln_war[sl], sq_war=rope_prev)
            t_xnT_pe, t_xnT_ev = self.emit_transposes(
                self.xn_sb, self.xnT_sb, t_xn, xnT_free)
            xn_free = t_xnT_pe
            t_u, t_g = self.emit_ffn(self.xnT_sb, t_xnT_ev, self.xu[sl],
                                     x_ld[e], sl, gate_war[sl])
            xnT_free = t_g
            gate_war[sl] = t_u
            t_usf = None
            if e >= HALO:
                P.store(self.us_[e - HALO], self.xu[sl][:], waits=[t_u])
                t_usf = P.sfence(self)
            if v_pend is not None:
                ve, vsl, vtok = v_pend
                lhsv = lambda c, s=vsl: self.unT[s][:, c * 128:(c + 1) * 128]
                v_pe, t_vev = self.emit_proj(lhsv, self.w2, vtok, w2_tok,
                                             out_bf=self.kvo,
                                             out_war=kvo_free)
                w2_last = v_pe
                unT_free[vsl] = unT_free[vsl] + [v_pe]
                P.store(self.vss[ve], self.kvo[:], waits=[t_vev])
                kv_stored[ve] = P.sfence(self)
                kvo_free = [kv_stored[ve]]
                v_pend = None
            t_un, t_sq2 = self.emit_ln(self.xu[sl], self.un_sb, sl, t_u,
                                       un_free, ln_war[sl], sq_war=rope_prev)
            ln_war[sl] = t_un
            xu_free[sl] = [t_un, t_sq2] + ([t_usf] if t_usf else [])
            if e + 2 < NT_EXT:
                P.load(self.xu[sl][:], self.xd[e + 2], waits=[xu_free[sl]])
                x_ld[e + 2] = P.lfence(self)
            t_unT_pe, t_unT_ev = self.emit_transposes(
                self.un_sb, self.unT[sl], t_un, unT_free[sl])
            un_free = t_unT_pe
            if e >= HALO:
                unT_stored[e - HALO] = P.store(self.unTs[e - HALO],
                                               self.unT[sl][:], waits=[t_unT_ev])
            lhsk = lambda c, s=sl: self.unT[s][:, c * 128:(c + 1) * 128]
            k_pe, t_kev = self.emit_proj(lhsk, self.w1, t_unT_ev, w1_tok,
                                         out_bf=self.buf_a, out_war=bufa_free)
            w1_last = k_pe
            t_kr = self.emit_rope(self.buf_a, e, t_kev)
            rope_prev = t_kr
            t_kt_pe, t_ktev = self.emit_transposes(
                self.buf_a, self.kvo, t_kr, kvo_free)
            bufa_free = t_kt_pe
            P.store(self.kTs[e], self.kvo[:], waits=[t_ktev])
            t_ksf = P.sfence(self)
            kvo_free = [t_ksf]
            unT_free[sl] = [t_unT_pe, k_pe] + \
                ([t_ksf] if e >= HALO else [])
            v_pend = (e, sl, t_unT_ev)
        ve, vsl, vtok = v_pend
        lhsv = lambda c, s=vsl: self.unT[s][:, c * 128:(c + 1) * 128]
        v_pe, t_vev = self.emit_proj(lhsv, self.w2, vtok, w2_tok,
                                     out_bf=self.kvo, out_war=kvo_free)
        w2_last = v_pe
        unT_free[vsl] = unT_free[vsl] + [v_pe]
        P.store(self.vss[ve], self.kvo[:], waits=[t_vev])
        kv_stored[ve] = P.sfence(self)
        kvo_free = [kv_stored[ve]]

        w1_tok = self.wload(self.wqd, self.w1, [w1_last])

        ring_free = [None] * NT_EXT
        ring_f = [None] * NT_EXT
        for e2 in range(HALO):
            self._ring_load(1, e2, e2, [kv_stored[e2], w2_last])
        P.load(self.unT[0][:], self.unTs[0],
               waits=[kv_stored[NT_EXT - 1], unT_free[0]])
        P.load(self.unT[1][:], self.unTs[1], waits=[unT_free[1]])
        lfB1 = P.lfence(self)
        for e2 in range(HALO):
            ring_f[e2] = lfB1
        unTl_f = [None] * NT_OWN
        unTl_f[0] = unTl_f[1] = lfB1

        qT_src_tok = [None] * NT_OWN
        s_last = None
        qT_store = [None] * NT_OWN
        sf1 = [None] * NT_OWN
        attn_prev = None
        self.p_free = None
        self.pt_free = None

        q_pe_by_slot = [None, None]
        for i in range(NT_OWN):
            sl = i % 2
            lhsq = lambda c, s=sl: self.unT[s][:, c * 128:(c + 1) * 128]
            q_pe, t_qev = self.emit_proj(lhsq, self.w1, unTl_f[i], w1_tok,
                                         out_bf=self.buf_a, out_war=bufa_free,
                                         evac_scale=SCALE_Q)
            q_pe_by_slot[sl] = q_pe
            if i + 2 < NT_OWN:
                P.load(self.unT[sl][:], self.unTs[i + 2], waits=[q_pe])
            if i + HALO < NT_EXT:
                self._ring_load(1, i + HALO, (i + HALO) % RING,
                                [ring_free[i - 2] if i >= 2 else None])
            lf1 = P.lfence(self)
            if i + 2 < NT_OWN:
                unTl_f[i + 2] = lf1
            if i + HALO < NT_EXT:
                ring_f[i + HALO] = lf1
            if attn_prev is not None:
                ip = attn_prev
                t_avp, evsp = self.emit_attn_tile(
                    1, ip, self.kvo[:, 0:1024], qT_src_tok[ip],
                    ring_f[ip + HALO], 0, sf1[ip] if ip >= 1 else None)
                ring_free[ip] = t_avp
                P.store(self.oTs[ip], self.oT_sb[:, 0:1024], waits=[evsp[-1]])
                s_last = t_avp
            t_qr = self.emit_rope(self.buf_a, i + HALO, t_qev)
            t_qt_pe, t_qtev = self.emit_transposes(
                self.buf_a, self.kvo, t_qr,
                [s_last, sf1[i - 1] if i >= 1 else kv_stored[NT_EXT - 1]])
            bufa_free = t_qt_pe
            qT_src_tok[i] = t_qtev
            qT_store[i] = P.store(self.qTs[i], self.kvo[:, 1024:2048],
                                  waits=[t_qtev])
            sf1[i] = P.sfence(self)
            attn_prev = i
        ip = attn_prev
        t_avp, evsp = self.emit_attn_tile(1, ip, self.kvo[:, 0:1024],
                                          qT_src_tok[ip], ring_f[ip + HALO],
                                          0, sf1[ip])
        ring_free[ip] = t_avp
        P.store(self.oTs[ip], self.oT_sb[:, 0:1024], waits=[evsp[-1]])
        sf1_final = P.sfence(self)
        b1_ring_last = t_avp
        b1_q_last = q_pe_by_slot[1]

        w2_tok = self.wload(self.wod, self.w2, [b1_ring_last])

        ring_free2 = [None] * NT_EXT
        ring_f2 = [None] * NT_EXT
        for e2 in range(HALO + 1):
            self._ring_load(2, e2, e2, [kv_stored[e2], b1_q_last])
        u_ld = [None] * NT_OWN
        qT_ld = [None] * NT_OWN
        oT_ld = [None] * NT_OWN
        y_stored = [None] * NT_OWN
        P.load(self.xu[0][:], self.us_[0], waits=[xu_free[0], sf1_final])
        P.load(self.xu[1][:], self.us_[1], waits=[xu_free[1]])
        P.load(self.un_sb[:, 0:1024], self.qTs[0], waits=[un_free])
        P.load(self.un_sb[:, 1024:2048], self.qTs[1])
        P.load(self.unT[0][:, 0:1024], self.oTs[0], waits=[q_pe_by_slot[0]])
        P.load(self.unT[1][:, 0:1024], self.oTs[1], waits=[q_pe_by_slot[1]])
        lfB2 = P.lfence(self)
        for e2 in range(HALO + 1):
            ring_f2[e2] = lfB2
        u_ld[0] = u_ld[1] = lfB2
        qT_ld[0] = qT_ld[1] = lfB2
        oT_ld[0] = oT_ld[1] = lfB2

        self.p_free = None
        self.pt_free = None
        oT_war2 = None
        ffn_pend = None
        ln_war2 = [None, None]
        gate_war2 = [None, None]
        self.xn_free2 = None
        self.xnT_free2 = None

        for i in range(NT_OWN):
            sl = i % 2
            if i + 1 + HALO < NT_EXT:
                self._ring_load(2, i + 1 + HALO, (i + 1 + HALO) % RING,
                                [ring_free2[i - 1] if i >= 1 else None])
            if ffn_pend is not None:
                ip2, slp, t_anp = ffn_pend
                t_yf = self._b2_tail(ip2, slp, t_anp, gate_war2)
                y_stored[ip2] = t_yf
                xu_free[slp] = [t_yf]
                if ip2 + 2 < NT_OWN:
                    P.load(self.xu[slp][:], self.us_[ip2 + 2], waits=[t_yf])
                ffn_pend = None
            t_av2, evs2 = self.emit_attn_tile(
                2, i, self.un_sb[:, sl * 1024:(sl + 1) * 1024], qT_ld[i],
                ring_f2[i + HALO], 1024, oT_war2)
            ring_free2[i] = t_av2
            if i + 2 < NT_OWN:
                P.load(self.un_sb[:, sl * 1024:(sl + 1) * 1024],
                       self.qTs[i + 2], waits=[t_av2])
            def lhso(c, s=sl):
                if c < 8:
                    return self.unT[s][:, c * 128:(c + 1) * 128]
                return self.oT_sb[:, 1024 + (c - 8) * 128:1024 + (c - 7) * 128]
            o_pe, t_a = self.emit_proj(lhso, self.w2,
                                       [oT_ld[i], evs2[-1]], w2_tok,
                                       x_dst=self.xu[sl], x_tok=u_ld[i])
            oT_war2 = o_pe
            if i + 2 < NT_OWN:
                P.load(self.unT[sl][:, 0:1024], self.oTs[i + 2], waits=[o_pe])
            lf2 = P.lfence(self)
            if i + 2 < NT_OWN:
                qT_ld[i + 2] = lf2
                oT_ld[i + 2] = lf2
            if i + 1 < NT_OWN:
                u_ld[i + 1] = lf2 if u_ld[i + 1] is None else u_ld[i + 1]
            if i + 1 + HALO < NT_EXT:
                ring_f2[i + 1 + HALO] = lf2
            t_an, _ = self.emit_ln(self.xu[sl], self.xn_sb, sl, t_a,
                                   self.xn_free2, ln_war2[sl])
            ln_war2[sl] = t_an
            ffn_pend = (i, sl, t_an)
        ip2, slp, t_anp = ffn_pend
        y_stored[ip2] = self._b2_tail(ip2, slp, t_anp, gate_war2)

        total = P.cnt["qst"]
        P.op("st", lambda e: e.nop(), waits=[("qst", total)])

        nc = self.nc
        with ExitStack() as bctx:
            sems = {}
            for s in Prog.SEMS:
                sems[s] = bctx.enter_context(nc.semaphore(s))
            blk = bctx.enter_context(nc.Block())

            @blk.tensor
            def _(e):
                P.replay("pe", e, sems)

            @blk.vector
            def _(e):
                P.replay("dv", e, sems)

            @blk.scalar
            def _(e):
                P.replay("ac", e, sems)

            @blk.sync
            def _(e):
                P.replay("ld", e, sems)

            @blk.gpsimd
            def _(e):
                P.replay("st", e, sems)
        self.ctx.close()
        return nc

    def _b2_tail(self, i, sl, t_an, gate_war2):
        P = self.P
        t_anT_pe, t_anT_ev = self.emit_transposes(self.xn_sb, self.xnT_sb,
                                                  t_an, self.xnT_free2)
        self.xn_free2 = t_anT_pe
        t_y, t_g2 = self.emit_ffn(self.xnT_sb, t_anT_ev, self.xu[sl],
                                  t_an, sl, gate_war2[sl])
        self.xnT_free2 = t_g2
        gate_war2[sl] = t_y
        P.store(self.yd[i], self.xu[sl][:], waits=[t_y])
        return P.sfence(self)


def kernel(L0_attn_hidden_states, L0_attn_wq, L0_attn_wk, L0_attn_wv,
           L0_attn_wo, L0_ffn_gate_w, L0_ffn_lora_A, L0_ffn_lora_B):
    x = np.asarray(L0_attn_hidden_states, np.float32)
    in_maps = _host_prepare(x, L0_attn_wq, L0_attn_wk, L0_attn_wv, L0_attn_wo,
                            L0_ffn_gate_w, L0_ffn_lora_A, L0_ffn_lora_B)
    k = Kern()
    nc = k.build()
    res = run_bass_kernel_spmd(nc, in_maps, core_ids=list(range(8)))
    out = np.empty((BSZ, SEQ, D), np.float32)
    for c in range(8):
        b, half = c // 2, c % 2
        y = np.asarray(res.results[c]["y"], np.float32).reshape(2048, D)
        out[b, half * 2048:(half + 1) * 2048, :] = y
    return out


# version 04
# speedup vs baseline: 1.0481x; 1.0481x over previous
import numpy as np
import ml_dtypes
import concourse.bass as bass
import concourse.mybir as mybir
from concourse.bass_utils import run_bass_kernel_spmd
from concourse.ap import AP
from contextlib import ExitStack

F32 = mybir.dt.float32
BF16 = mybir.dt.bfloat16
AX = mybir.AxisListType.X
ALU = mybir.AluOpType
AF = mybir.ActivationFunctionType

BSZ, SEQ, D = 4, 4096, 2048
NH, HD, WIN = 16, 128, 1024
NT_OWN, HALO = 16, 8
NT_EXT = NT_OWN + HALO
RING = 10
NBLK = 9
NCH = 16
SCALE_Q = float(128.0 ** -0.5)
NEG = -1.0e30
ER = 128
GAW = 8 + ER

DEBUG = False


def bf(x):
    return np.asarray(x, dtype=ml_dtypes.bfloat16)



PARANOID = False


class Prog:
    SEMS = ("qls", "qst", "qlf", "qsf", "pe", "dv", "ac", "po")
    DMA_SEMS = ("qls", "qst", "qlf", "qsf")
    OWNER = {"pe": "pe", "dv": "dv", "ac": "ac", "po": "st", "qls": None,
             "qst": None, "qlf": None, "qsf": None}

    def __init__(self):
        self.stream = {k: [] for k in ("pe", "dv", "ac", "ld", "st")}
        self.cnt = {s: 0 for s in self.SEMS}

    def op(self, eng, fn, waits=(), inc=None, drain=False):
        w = []
        for x in waits:
            if x is None:
                continue
            if isinstance(x, list):
                w += [t for t in x if t is not None]
            else:
                w.append(x)
        if PARANOID:
            w += [(s, self.cnt[s]) for s in ("pe", "dv", "ac")]
        tok = None
        amt = 0
        if inc is not None:
            amt = 16 if inc in self.DMA_SEMS else 1
            self.cnt[inc] += amt
            tok = (inc, self.cnt[inc])
        self.stream[eng].append((fn, w, inc, amt, drain))
        return tok

    def load(self, out, in_, waits=()):
        return self.op("ld", lambda e, o=out, i=in_: e.dma_start(out=o, in_=i),
                       waits=waits, inc="qls")

    def store(self, out, in_, waits=()):
        return self.op("st", lambda e, o=out, i=in_: e.dma_start(out=o, in_=i),
                       waits=waits, inc="qst")

    def lfence(self, kern):
        return self.op("ld", lambda e: e.dma_start(out=kern.fence_sb[:],
                                                   in_=kern.fin_d[:]),
                       inc="qlf")

    def sfence(self, kern):
        return self.op("st", lambda e: e.dma_start(out=kern.fout_d[:],
                                                   in_=kern.fence_sb[:]),
                       inc="qsf")

    def mm(self, out, lhsT, rhs, start, stop, waits=(), inc=False):
        return self.op(
            "pe",
            lambda e, o=out, l=lhsT, r=rhs, s0=start, s1=stop: e.matmul(
                o, l, r, start=s0, stop=s1),
            waits=waits, inc="pe" if inc else None)

    def replay(self, ename, eng, sems):
        seen = {}
        for fn, waits, inc, amt, drain in self.stream[ename]:
            best = {}
            for (s, v) in waits:
                if Prog.OWNER[s] == ename:
                    continue
                if best.get(s, 0) < v:
                    best[s] = v
            for s, v in best.items():
                if seen.get(s, 0) >= v:
                    continue
                eng.wait_ge(sems[s], v)
                seen[s] = v
            inst = fn(eng)
            if drain:
                inst = eng.drain()
            if inc is not None:
                inst.then_inc(sems[inc], amt)


class PsumBlock:
    def __init__(self, alloc, banks, waits):
        self.a = alloc
        self.banks = banks
        self.waits = waits
        self.c0 = banks[0] * 512

    def ap(self, col0, col1):
        return self.a.ps[:, self.c0 + col0:self.c0 + col1]

    def free(self, *toks):
        toks = [t for t in toks if t is not None]
        for b in self.banks:
            self.a.tok[b] = toks


class PsumAlloc:
    def __init__(self, ps):
        self.ps = ps
        self.tok = [[] for _ in range(8)]
        self.ptr = 0

    def alloc(self, nbanks):
        if self.ptr + nbanks > 8:
            self.ptr = 0
        b0 = self.ptr
        self.ptr = (self.ptr + nbanks) % 8
        banks = list(range(b0, b0 + nbanks))
        waits = []
        for b in banks:
            waits += self.tok[b]
        return PsumBlock(self, banks, waits)



def _rope_tables(positions):
    ntiles = len(positions) // 128
    inv = 1.0 / (10000.0 ** (np.arange(0, HD, 2, dtype=np.float64) / HD))
    pos = np.asarray(positions, np.float64)
    fr = pos[:, None] * inv[None, :]
    c = np.cos(fr).reshape(ntiles, 128, 64).transpose(1, 0, 2).reshape(
        128, ntiles * 64)
    s = (-np.sin(fr)).reshape(ntiles, 128, 64).transpose(1, 0, 2).reshape(
        128, ntiles * 64)
    return bf(c), bf(s)


def _host_prepare(x, wq, wk, wv, wo, gate_w, lora_A, lora_B):
    qq = np.arange(128)[:, None]
    kk = np.arange(128)[None, :]
    m_w = bf(np.where(kk > qq, 0.0, NEG))
    m_c = bf(np.where(kk <= qq, 0.0, NEG))
    ident = bf(np.eye(128))
    gA = np.ascontiguousarray(
        np.concatenate([np.asarray(gate_w).T.astype(np.float32, copy=False),
                        np.asarray(lora_A).reshape(ER, D).T.astype(
                            np.float32, copy=False)], axis=1))
    bstack = np.ascontiguousarray(
        np.asarray(lora_B).transpose(0, 2, 1).reshape(ER, D).astype(
            np.float32, copy=False))
    shared = dict(wq=np.asarray(wq, np.float32), wk=np.asarray(wk, np.float32),
                  wv=np.asarray(wv, np.float32), wo=np.asarray(wo, np.float32),
                  gA=gA, bstack=bstack, ident=ident, m_w=m_w, m_c=m_c,
                  ident_f32=np.eye(128, dtype=np.float32))
    x = np.asarray(x, np.float32)
    in_maps = []
    for b in range(BSZ):
        for half in range(2):
            if half == 0:
                xe = np.concatenate(
                    [np.zeros((HALO * 128, D), np.float32), x[b, 0:2048]], axis=0)
                ext_pos = np.concatenate([np.zeros(HALO * 128), np.arange(0, 2048)])
            else:
                xe = np.ascontiguousarray(x[b, 1024:4096])
                ext_pos = np.arange(1024, 4096)
            c2h, s2h = _rope_tables(ext_pos)
            npad = np.zeros((128, NT_OWN), np.float32)
            if half == 0:
                for i in range(NT_OWN):
                    npb = max(0, HALO - i)
                    if npb == 0:
                        continue
                    cnt = np.full(128, 128.0 * (npb - 1))
                    cnt += 127.0 - np.arange(128)
                    npad[:, i] = cnt
            m = dict(shared)
            m.update(x_ext=np.ascontiguousarray(xe.reshape(NT_EXT, 128, D)),
                     c2=c2h, s2=s2h, npad=npad,
                     fence_in=np.zeros((128, 16), np.float32))
            in_maps.append(m)
    return in_maps



class Kern:
    def __init__(self):
        self.nc = bass.Bass("TRN2", target_bir_lowering=False)
        self.ctx = ExitStack()
        self.P = Prog()
        self._declare()
        self.pa = PsumAlloc(self.ps_all)

    def _declare(self):
        nc = self.nc
        dt = lambda n, s, d, k: nc.dram_tensor(n, s, d, kind=k).ap()
        self.xd = dt("x_ext", [NT_EXT, 128, D], F32, "ExternalInput")
        self.wqd = dt("wq", [D, D], F32, "ExternalInput")
        self.wkd = dt("wk", [D, D], F32, "ExternalInput")
        self.wvd = dt("wv", [D, D], F32, "ExternalInput")
        self.wod = dt("wo", [D, D], F32, "ExternalInput")
        self.gAd = dt("gA", [D, GAW], F32, "ExternalInput")
        self.bsd = dt("bstack", [ER, D], F32, "ExternalInput")
        self.c2d = dt("c2", [128, NT_EXT * 64], BF16, "ExternalInput")
        self.s2d = dt("s2", [128, NT_EXT * 64], BF16, "ExternalInput")
        self.npd = dt("npad", [128, NT_OWN], F32, "ExternalInput")
        self.idd = dt("ident", [128, 128], BF16, "ExternalInput")
        self.mwd = dt("m_w", [128, 128], BF16, "ExternalInput")
        self.mcd = dt("m_c", [128, 128], BF16, "ExternalInput")
        self.fin_d = dt("fence_in", [128, 16], F32, "ExternalInput")
        self.idfd = dt("ident_f32", [128, 128], F32, "ExternalInput")
        self.fout_d = dt("fence_out", [128, 16], F32, "Internal")
        self.yd = dt("y", [NT_OWN, 128, D], F32, "ExternalOutput")
        SK = "ExternalOutput" if DEBUG else "Internal"
        self.kTs = dt("kT_s", [NT_EXT, 128, D], BF16, SK)
        self.vss = dt("v_s", [NT_EXT, 128, D], BF16, SK)
        self.us_ = dt("u_s", [NT_OWN, 128, D], F32, SK)
        self.unTs = dt("unT_s", [NT_OWN, 128, D], BF16, SK)
        self.qTs = dt("qT_s", [NT_OWN, 128, 1024], BF16, SK)
        self.oTs = dt("oT_s", [NT_OWN, 128, 1024], BF16, SK)

        sb = lambda n, s, d: self.ctx.enter_context(nc.sbuf_tensor(n, s, d))
        self.fence_sb = sb("fence_sb", [128, 16], F32)
        self.id_sb = sb("id_sb", [128, 128], BF16)
        self.idf_sb = sb("idf_sb", [128, 128], F32)
        self.diag = [sb("diag0", [128, 128], BF16), sb("diag1", [128, 128], BF16)]
        self.mw_sb = sb("mw_sb", [128, 128], BF16)
        self.mc_sb = sb("mc_sb", [128, 128], BF16)
        self.np_sb = sb("np_sb", [128, NT_OWN], F32)
        self.c2_sb = sb("c2_sb", [128, NT_EXT * 64], BF16)
        self.s2_sb = sb("s2_sb", [128, NT_EXT * 64], BF16)
        self.gA_sb = sb("gA_sb", [128, NCH * GAW], BF16)
        self.bs_sb = sb("bs_sb", [128, D], BF16)
        self.w1 = sb("w1", [128, NCH * D], BF16)
        self.w2 = sb("w2", [128, NCH * D], BF16)
        self.xu = [sb("xu0", [128, D], F32), sb("xu1", [128, D], F32)]
        self.xn_sb = sb("xn_sb", [128, D], BF16)
        self.xnT_sb = sb("xnT_sb", [128, D], BF16)
        self.un_sb = sb("un_sb", [128, D], BF16)
        self.unT = [sb("unT0", [128, D], BF16), sb("unT1", [128, D], BF16)]
        self.buf_a = sb("buf_a", [128, D], BF16)
        self.buf_b = sb("buf_b", [128, D], BF16)
        self.kvo = sb("kvo", [128, D], BF16)
        self.p_sb = [sb("p_sb0", [128, NBLK * 128], BF16),
                     sb("p_sb1", [128, NBLK * 128], BF16)]
        self.pt_sb = sb("pt_sb", [128, NBLK * 128], BF16)
        self.oT_sb = sb("oT_sb", [128, D], BF16)
        mk2 = lambda n, w: [sb(n + "0", [128, w], F32), sb(n + "1", [128, w], F32)]
        self.st1 = mk2("st1", 1)
        self.ssq = mk2("ssq", 1)
        self.mean = mk2("mean", 1)
        self.nmean = mk2("nmean", 1)
        self.msq = mk2("msq", 1)
        self.var = mk2("var", 1)
        self.lnv = mk2("lnv", 1)
        self.rstd = mk2("rstd", 1)
        self.g8a = mk2("g8a", 8)
        self.g8b = mk2("g8b", 8)
        self.gm1 = mk2("gm1", 1)
        self.gm2 = mk2("gm2", 1)
        self.gdl = mk2("gdl", 1)
        self.ge = mk2("ge", 1)
        self.gden = mk2("gden", 1)
        self.grcp = mk2("grcp", 1)
        self.comb8 = mk2("comb8", 8)
        self.comb_er = [sb("comber", [128, ER], F32)] * 2
        self.ch_sb = [sb("ch0", [128, ER], BF16), sb("ch1", [128, ER], BF16)]
        self.chT_sb = [sb("chT0", [128, 128], BF16), sb("chT1", [128, 128], BF16)]
        self.nm_all = sb("nm_all", [128, 8], F32)
        self.sum_all = sb("sum_all", [128, 8], F32)
        self.enm = sb("enm", [128, 1], F32)
        self.corr = sb("corr", [128, 1], F32)
        self.rcp_s = sb("rcp_s", [128, 1], F32)
        self.ps_all = self.ctx.enter_context(
            nc.psum_tensor("ps_all", [128, 4096], F32))

    def ring_k(self, phase, slot):
        a = self.w2 if phase == 1 else self.w1
        return a[:, slot * 1024:(slot + 1) * 1024]

    def ring_v(self, phase, slot):
        a = self.w2 if phase == 1 else self.w1
        return a[:, (RING + slot) * 1024:(RING + slot + 1) * 1024]

    def _ring_load(self, phase, e, slot, waits):
        P = self.P
        cols = slice(0, 1024) if phase == 1 else slice(1024, 2048)
        P.load(self.ring_k(phase, slot), self.kTs[e][:, cols], waits=waits)
        return P.load(self.ring_v(phase, slot), self.vss[e][:, cols], waits=())

    def wload(self, wdram, dest, waits):
        P = self.P
        for g in range(4):
            src = wdram[g * 512:(g + 1) * 512, :].rearrange(
                "(c p) d -> p c d", p=128)
            dst = dest[:, g * 4 * D:(g + 1) * 4 * D].rearrange(
                "p (c d) -> p c d", c=4)
            P.op("st", lambda e, o=dst, i=src: e.dma_start(out=o, in_=i),
                 waits=waits, inc="qst")
        return P.sfence(self)

    def emit_ln(self, x_src, out_bf, par, ld_tok, out_war, stats_war,
                sq_war=None):
        P = self.P
        s = lambda b: b[par]
        t_s1 = P.op("dv", lambda e: e.reduce_sum(s(self.st1)[:], x_src[:],
                                                 axis=AX),
                    waits=[ld_tok, stats_war], drain=True, inc="dv")
        t_sq = P.op("ac", lambda e: e.activation(
            self.buf_b[:], x_src[:], AF.Square, accum_out=s(self.ssq)[:, 0:1]),
            waits=[ld_tok, stats_war, sq_war], drain=True, inc="ac")
        P.op("st", lambda e: e.tensor_scalar(
            s(self.mean)[:], s(self.st1)[:], 1.0 / D, None, ALU.mult),
            waits=[t_s1, stats_war])
        P.op("st", lambda e: e.tensor_scalar(
            s(self.nmean)[:], s(self.mean)[:], -1.0, None, ALU.mult))
        P.op("st", lambda e: e.tensor_mul(
            s(self.msq)[:], s(self.mean)[:], s(self.mean)[:]))
        P.op("st", lambda e: e.tensor_scalar(
            s(self.var)[:], s(self.ssq)[:], 1.0 / D, 1e-5, ALU.mult, ALU.add),
            waits=[t_sq])
        t_po = P.op("st", lambda e: e.tensor_sub(
            s(self.var)[:], s(self.var)[:], s(self.msq)[:]), inc="po")
        P.op("ac", lambda e: e.activation(
            s(self.lnv)[:], s(self.var)[:, 0:1], AF.Ln),
            waits=[t_po], drain=True)
        t_r = P.op("ac", lambda e: e.activation(
            s(self.rstd)[:], s(self.lnv)[:, 0:1], AF.Exp, scale=-0.5),
            drain=True, inc="ac")
        t_ap = P.op("dv", lambda e: e.tensor_scalar(
            out_bf[:], x_src[:], s(self.nmean)[:, 0:1], s(self.rstd)[:, 0:1],
            ALU.add, ALU.mult), waits=[t_r, t_po, out_war], drain=True, inc="dv")
        return t_ap, t_sq

    def emit_transposes(self, src_bf, dst_bf, src_tok, dst_war):
        P = self.P
        pe_last = None
        blks = []
        for half in range(2):
            blk = self.pa.alloc(2)
            t_mm = None
            for c in range(8):
                cc = half * 8 + c
                t_mm = P.mm(blk.ap(c * 128, (c + 1) * 128),
                            src_bf[:, cc * 128:(cc + 1) * 128], self.id_sb[:],
                            True, True,
                            waits=([src_tok, self.t_id] + blk.waits)
                            if c == 0 else (),
                            inc=(c == 7))
            blks.append((blk, t_mm))
            pe_last = t_mm
        for half, (blk, t_mm) in enumerate(blks):
            P.op("ac", lambda e, h=half, b=blk: e.activation(
                dst_bf[:, h * 1024:(h + 1) * 1024], b.ap(0, 1024), AF.Copy),
                waits=[t_mm, dst_war])
        ev = P.op("ac", lambda e: e.drain(), inc="ac")
        for blk, _ in blks:
            blk.free(ev)
        return pe_last, ev

    def emit_proj(self, lhs_fn, rhs_sb, lhs_tok, rhs_tok, out_bf=None,
                  out_war=None, x_dst=None, x_tok=None, evac_scale=None):
        P = self.P
        pe_last = None
        blks = []
        for half in range(2):
            blk = self.pa.alloc(2)
            t_mm = None
            for c in range(NCH):
                for s2 in range(2):
                    col = half * 1024 + s2 * 512
                    t_mm = P.mm(
                        blk.ap(s2 * 512, (s2 + 1) * 512), lhs_fn(c),
                        rhs_sb[:, c * D + col:c * D + col + 512],
                        start=(c == 0), stop=(c == NCH - 1),
                        waits=([lhs_tok, rhs_tok] + blk.waits)
                        if (c == 0 and s2 == 0) else (),
                        inc=(c == NCH - 1 and s2 == 1))
            blks.append((blk, t_mm))
            pe_last = t_mm
        eng = "dv" if x_dst is not None else "ac"
        for half, (blk, t_mm) in enumerate(blks):
            if x_dst is not None:
                P.op("dv", lambda e, h=half, b=blk: e.tensor_add(
                    x_dst[:, h * 1024:(h + 1) * 1024],
                    x_dst[:, h * 1024:(h + 1) * 1024], b.ap(0, 1024)),
                    waits=[t_mm, x_tok])
            else:
                kw = {} if evac_scale is None else {"scale": evac_scale}
                P.op("ac", lambda e, h=half, b=blk, kw=kw: e.activation(
                    out_bf[:, h * 1024:(h + 1) * 1024], b.ap(0, 1024),
                    AF.Copy, **kw),
                    waits=[t_mm, out_war])
        out_tok = P.op(eng, lambda e: e.drain(), inc=eng)
        for blk, _ in blks:
            blk.free(out_tok)
        return pe_last, out_tok

    def emit_rope(self, src_bf, tile_e, src_tok):
        P = self.P
        off = tile_e * 64
        csl = self.c2_sb[:, off:off + 64]
        ssl = self.s2_sb[:, off:off + 64]
        cbc = AP(csl.tensor, csl.offset,
                 [list(csl.ap[0]), [0, NH], [0, 2], [1, 64]])
        sbc = AP(ssl.tensor, ssl.offset,
                 [list(ssl.ap[0]), [0, NH], [0, 2], [1, 64]])
        sv = src_bf[:]
        rot = AP(sv.tensor, sv.offset + 64,
                 [list(sv.ap[0]), [128, NH], [-64, 2], [1, 64]])
        t4 = lambda t: t[:].rearrange("p (h x j) -> p h x j", h=NH, x=2)
        P.op("dv", lambda e: e.tensor_tensor(
            t4(self.buf_b), t4(src_bf), cbc, ALU.mult),
            waits=[src_tok, self.t_tab])
        P.op("dv", lambda e: e.tensor_tensor(t4(self.xn_sb), rot, sbc, ALU.mult))
        fh = lambda t: AP(t[:].tensor, t[:].offset,
                          [list(t[:].ap[0]), [128, NH], [1, 64]])
        sh = lambda t: AP(t[:].tensor, t[:].offset + 64,
                          [list(t[:].ap[0]), [128, NH], [1, 64]])
        P.op("dv", lambda e: e.tensor_tensor(
            fh(src_bf), fh(self.buf_b), fh(self.xn_sb), ALU.add))
        return P.op("dv", lambda e: e.tensor_tensor(
            sh(src_bf), sh(self.buf_b), sh(self.xn_sb), ALU.subtract),
            drain=True, inc="dv")

    def emit_ffn(self, actT_sb, actT_tok, x_dst, x_tok, par, war):
        P = self.P
        s = lambda b: b[par]
        gblk = self.pa.alloc(1)
        t_g = None
        for c in range(NCH):
            t_g = P.mm(gblk.ap(0, GAW), actT_sb[:, c * 128:(c + 1) * 128],
                       self.gA_sb[:, c * GAW:(c + 1) * GAW],
                       start=(c == 0), stop=(c == NCH - 1),
                       waits=([actT_tok, self.t_gA] + gblk.waits)
                       if c == 0 else (),
                       inc=(c == NCH - 1))
        lg = gblk.ap(0, 8)
        hh = gblk.ap(8, GAW)
        P.op("dv", lambda e: e.reduce_max(s(self.gm1)[:], lg, axis=AX),
             waits=[t_g, war], drain=True)
        P.op("dv", lambda e: e.tensor_single_scalar(
            s(self.g8a)[:], lg, s(self.gm1)[:, 0:1], ALU.is_ge), drain=True)
        P.op("dv", lambda e: e.scalar_tensor_tensor(
            s(self.g8b)[:], s(self.g8a)[:], NEG, lg, ALU.mult, ALU.add),
            drain=True)
        P.op("dv", lambda e: e.reduce_max(s(self.gm2)[:], s(self.g8b)[:], axis=AX),
             drain=True)
        P.op("dv", lambda e: e.tensor_single_scalar(
            s(self.g8b)[:], s(self.g8b)[:], s(self.gm2)[:, 0:1], ALU.is_ge),
            drain=True)
        t_d = P.op("dv", lambda e: e.tensor_sub(
            s(self.gdl)[:], s(self.gm2)[:], s(self.gm1)[:]), drain=True, inc="dv")
        t_e = P.op("ac", lambda e: e.activation(
            s(self.ge)[:], s(self.gdl)[:, 0:1], AF.Exp),
            waits=[t_d], drain=True, inc="ac")
        P.op("dv", lambda e: e.tensor_scalar(
            s(self.gden)[:], s(self.ge)[:], 1.0, None, ALU.add),
            waits=[t_e], drain=True)
        P.op("dv", lambda e: e.reciprocal(s(self.grcp)[:], s(self.gden)[:]),
             drain=True)
        P.op("dv", lambda e: e.scalar_tensor_tensor(
            s(self.comb8)[:], s(self.g8b)[:], s(self.ge)[:, 0:1], s(self.g8a)[:],
            ALU.mult, ALU.add), drain=True)
        P.op("dv", lambda e: e.tensor_single_scalar(
            s(self.comb8)[:], s(self.comb8)[:], s(self.grcp)[:, 0:1], ALU.mult),
            drain=True)
        c8 = s(self.comb8)[:]
        cbc = AP(c8.tensor, c8.offset, [list(c8.ap[0]), [1, 8], [0, 16]])
        P.op("dv", lambda e: e.tensor_copy(
            s(self.comb_er)[:].rearrange("p (x r) -> p x r", x=8), cbc),
            drain=True)
        t_ch = P.op("dv", lambda e: e.tensor_mul(
            s(self.ch_sb)[:], hh, s(self.comb_er)[:]), drain=True, inc="dv")
        gblk.free(t_ch)
        cblk = self.pa.alloc(1)
        t_ct = P.mm(cblk.ap(0, 128), s(self.ch_sb)[:], self.id_sb[:], True, True,
                    waits=[t_ch] + cblk.waits, inc=True)
        t_ce = P.op("dv", lambda e, b=cblk: e.tensor_copy(
            s(self.chT_sb)[:], b.ap(0, 128)), waits=[t_ct], drain=True, inc="dv")
        cblk.free(t_ce)
        blks = []
        for half in range(2):
            blk = self.pa.alloc(2)
            t_mm = None
            for s2 in range(2):
                col = half * 1024 + s2 * 512
                t_mm = P.mm(blk.ap(s2 * 512, (s2 + 1) * 512), s(self.chT_sb)[:],
                            self.bs_sb[:, col:col + 512], True, True,
                            waits=([t_ce] + blk.waits) if s2 == 0 else (),
                            inc=(s2 == 1))
            blks.append((blk, t_mm))
        for half, (blk, t_mm) in enumerate(blks):
            P.op("dv", lambda e, h=half, b=blk: e.tensor_add(
                x_dst[:, h * 1024:(h + 1) * 1024],
                x_dst[:, h * 1024:(h + 1) * 1024], b.ap(0, 1024)),
                waits=[t_mm, x_tok])
        t_add = P.op("dv", lambda e: e.drain(), inc="dv")
        for blk, _ in blks:
            blk.free(t_add)
        return t_add, t_g

    def emit_attn_tile(self, phase, i, qsrc, q_tok, ring_tok, oT_off, oT_war):
        P = self.P
        ps = self.ps_all
        SREG = (ps[:, 0:1152], ps[:, 1536:2688])
        SLACK = (ps[:, 1216:1344], ps[:, 2752:2880])
        PTA = ps[:, 3072:4096]
        AVR = ps[:, 3072:3200]
        claim = []
        for b in range(8):
            claim += self.pa.tok[b]
            self.pa.tok[b] = []
        s_tok = [None] * 8
        exp_tok = [None] * 8
        pte_tok = [None] * 8
        oev_tok = [None] * 8
        pt_tok = [None] * 8
        av_tok = [None] * 8

        def emit_S(h):
            reg = SREG[h % 2]
            w = [q_tok, ring_tok, self.t_mask, self.t_id, self.t_np]
            if h >= 2:
                w += [exp_tok[h - 2], pte_tok[h - 2]]
            else:
                w = w + claim
            P.mm(reg[:, 0:128], self.id_sb[:], self.mw_sb[:], True, False,
                 waits=w)
            P.mm(reg[:, 8 * 128:9 * 128], self.id_sb[:], self.mc_sb[:],
                 True, False)
            t = None
            for b in range(NBLK):
                slot = (i + b) % RING
                t = P.mm(reg[:, b * 128:(b + 1) * 128],
                         qsrc[:, h * 128:(h + 1) * 128],
                         self.ring_k(phase, slot)[:, h * 128:(h + 1) * 128],
                         start=(b not in (0, 8)), stop=True,
                         inc=(b == NBLK - 1))
            s_tok[h] = t

        emit_S(0)
        emit_S(1)
        for h in range(8):
            reg = SREG[h % 2]
            t_nm = P.op("dv", lambda e, h=h, r=reg: e.reduce_max(
                self.nm_all[:, h:h + 1], r, axis=AX, negate=True),
                waits=[s_tok[h]], drain=True, inc="dv")
            P.op("ac", lambda e, h=h, r=reg: e.activation(
                self.p_sb[h % 2][:], r, AF.Exp, bias=self.nm_all[:, h:h + 1],
                accum_out=self.sum_all[:, h:h + 1]),
                waits=[t_nm, pt_tok[h - 2] if h >= 2 else None])
            t_exp = P.op("ac", lambda e, h=h: e.activation(
                self.enm[:], self.nm_all[:, h:h + 1], AF.Exp),
                drain=True, inc="ac")
            exp_tok[h] = t_exp
            P.op("st", lambda e, i=i: e.tensor_mul(
                self.corr[:], self.np_sb[:, i:i + 1], self.enm[:]),
                waits=[t_exp])
            t_corr = P.op("st", lambda e, h=h: e.tensor_sub(
                self.corr[:], self.sum_all[:, h:h + 1], self.corr[:]),
                inc="po")
            t_rcp = P.op("dv", lambda e: e.reciprocal(
                self.rcp_s[:], self.corr[:]),
                waits=[t_corr], drain=True, inc="dv")
            rb = self.rcp_s[:]
            rbc = AP(rb.tensor, rb.offset, [list(rb.ap[0]), [0, 128]])
            dg = self.diag[h % 2]
            t_diag = P.op("st", lambda e, d=dg, rbc=rbc: e.tensor_tensor(
                d[:], self.idf_sb[:], rbc, ALU.mult),
                waits=[t_rcp, self.diag_war[h % 2]], inc="po")
            w0 = [t_diag]
            if h >= 1:
                w0 += [pte_tok[h - 1], oev_tok[h - 1]]
            else:
                w0 = w0 + claim
            t_pta = None
            pp = self.p_sb[h % 2]
            for b in range(8):
                t_pta = P.mm(PTA[:, b * 128:(b + 1) * 128],
                             pp[:, b * 128:(b + 1) * 128], dg[:], True, True,
                             waits=w0 if b == 0 else (), inc=(b == 7))
            t_ptb = P.mm(SLACK[h % 2], pp[:, 8 * 128:9 * 128], dg[:],
                         True, True,
                         waits=[pte_tok[h - 2]] if h >= 2 else (), inc=True)
            pt_tok[h] = t_ptb
            self.diag_war[h % 2] = t_ptb
            eng = "ac" if h % 2 == 0 else "dv"
            if eng == "ac":
                P.op("ac", lambda e: e.activation(
                    self.pt_sb[:, 0:1024], PTA, AF.Copy),
                    waits=[t_pta, av_tok[h - 1] if h >= 1 else None])
                P.op("ac", lambda e, h=h: e.activation(
                    self.pt_sb[:, 1024:1152], SLACK[h % 2], AF.Copy),
                    waits=[t_ptb])
            else:
                P.op("dv", lambda e: e.tensor_copy(
                    self.pt_sb[:, 0:1024], PTA),
                    waits=[t_pta, av_tok[h - 1] if h >= 1 else None])
                P.op("dv", lambda e, h=h: e.tensor_copy(
                    self.pt_sb[:, 1024:1152], SLACK[h % 2]),
                    waits=[t_ptb])
            t_pte = P.op(eng, lambda e: e.drain(), inc=eng)
            pte_tok[h] = t_pte
            if h + 2 < 8:
                emit_S(h + 2)
            t_av = None
            for b in range(NBLK):
                slot = (i + b) % RING
                t_av = P.mm(AVR,
                            self.ring_v(phase, slot)[:, h * 128:(h + 1) * 128],
                            self.pt_sb[:, b * 128:(b + 1) * 128],
                            start=(b == 0), stop=(b == NBLK - 1),
                            waits=[t_pte] if b == 0 else (),
                            inc=(b == NBLK - 1))
            av_tok[h] = t_av
            t_oev = P.op("ac", lambda e, h=h, o=oT_off: e.activation(
                self.oT_sb[:, o + h * 128:o + (h + 1) * 128], AVR, AF.Copy),
                waits=[t_av, oT_war], drain=True, inc="ac")
            oev_tok[h] = t_oev
        for b in (0, 1, 2):
            self.pa.tok[b] = [exp_tok[6], pte_tok[6]]
        for b in (3, 4, 5):
            self.pa.tok[b] = [exp_tok[7], pte_tok[7]]
        for b in (6, 7):
            self.pa.tok[b] = [oev_tok[7], pte_tok[7]]
        self.pa.ptr = 0
        return av_tok[7], oev_tok

    def build(self):
        P = self.P
        P.load(self.fence_sb[:], self.fin_d[:])
        P.load(self.id_sb[:], self.idd[:])
        P.load(self.idf_sb[:], self.idfd[:])
        P.load(self.mw_sb[:], self.mwd[:])
        P.load(self.mc_sb[:], self.mcd[:])
        P.load(self.np_sb[:], self.npd[:])
        P.load(self.c2_sb[:], self.c2d[:])
        P.load(self.s2_sb[:], self.s2d[:])
        self.p_free = None
        self.pt_free = None
        self.diag_war = [None, None]
        xu_free = [[], []]

        src = self.gAd.rearrange("(c p) w -> p c w", p=128)
        dst = self.gA_sb[:].rearrange("p (c w) -> p c w", c=NCH)
        P.op("st", lambda e, o=dst, i=src: e.dma_start(out=o, in_=i),
             inc="qst")
        P.op("st", lambda e: e.dma_start(out=self.bs_sb[:], in_=self.bsd[:]),
             inc="qst")
        self.wload(self.wkd, self.w1, ())
        t_w = self.wload(self.wvd, self.w2, ())
        self.t_gA = t_w
        w1_tok = t_w
        w2_tok = t_w

        x_ld = [None] * NT_EXT
        P.load(self.xu[0][:], self.xd[0])
        P.load(self.xu[1][:], self.xd[1])
        lf0 = P.lfence(self)
        self.t_id = self.t_mask = self.t_np = self.t_tab = lf0
        x_ld[0] = lf0
        x_ld[1] = lf0

        kv_stored = [None] * NT_EXT
        unT_stored = [None] * NT_OWN
        xn_free = None
        xnT_free = None
        un_free = None
        unT_free = [[], []]
        ln_war = [None, None]
        gate_war = [None, None]
        bufa_free = None
        kvo_free = []
        v_pend = None
        w1_last = None
        w2_last = None
        rope_prev = None

        for e in range(NT_EXT):
            sl = e % 2
            t_xn, _ = self.emit_ln(self.xu[sl], self.xn_sb, sl, x_ld[e],
                                   xn_free, ln_war[sl], sq_war=rope_prev)
            t_xnT_pe, t_xnT_ev = self.emit_transposes(
                self.xn_sb, self.xnT_sb, t_xn, xnT_free)
            xn_free = t_xnT_pe
            t_u, t_g = self.emit_ffn(self.xnT_sb, t_xnT_ev, self.xu[sl],
                                     x_ld[e], sl, gate_war[sl])
            xnT_free = t_g
            gate_war[sl] = t_u
            t_usf = None
            if e >= HALO:
                P.store(self.us_[e - HALO], self.xu[sl][:], waits=[t_u])
                t_usf = P.sfence(self)
            if v_pend is not None:
                ve, vsl, vtok = v_pend
                lhsv = lambda c, s=vsl: self.unT[s][:, c * 128:(c + 1) * 128]
                v_pe, t_vev = self.emit_proj(lhsv, self.w2, vtok, w2_tok,
                                             out_bf=self.kvo,
                                             out_war=kvo_free)
                w2_last = v_pe
                unT_free[vsl] = unT_free[vsl] + [v_pe]
                P.store(self.vss[ve], self.kvo[:], waits=[t_vev])
                kv_stored[ve] = P.sfence(self)
                kvo_free = [kv_stored[ve]]
                v_pend = None
            t_un, t_sq2 = self.emit_ln(self.xu[sl], self.un_sb, sl, t_u,
                                       un_free, ln_war[sl], sq_war=rope_prev)
            ln_war[sl] = t_un
            xu_free[sl] = [t_un, t_sq2] + ([t_usf] if t_usf else [])
            if e + 2 < NT_EXT:
                P.load(self.xu[sl][:], self.xd[e + 2], waits=[xu_free[sl]])
                x_ld[e + 2] = P.lfence(self)
            t_unT_pe, t_unT_ev = self.emit_transposes(
                self.un_sb, self.unT[sl], t_un, unT_free[sl])
            un_free = t_unT_pe
            if e >= HALO:
                unT_stored[e - HALO] = P.store(self.unTs[e - HALO],
                                               self.unT[sl][:], waits=[t_unT_ev])
            lhsk = lambda c, s=sl: self.unT[s][:, c * 128:(c + 1) * 128]
            k_pe, t_kev = self.emit_proj(lhsk, self.w1, t_unT_ev, w1_tok,
                                         out_bf=self.buf_a, out_war=bufa_free)
            w1_last = k_pe
            t_kr = self.emit_rope(self.buf_a, e, t_kev)
            rope_prev = t_kr
            t_kt_pe, t_ktev = self.emit_transposes(
                self.buf_a, self.kvo, t_kr, kvo_free)
            bufa_free = t_kt_pe
            P.store(self.kTs[e], self.kvo[:], waits=[t_ktev])
            t_ksf = P.sfence(self)
            kvo_free = [t_ksf]
            unT_free[sl] = [t_unT_pe, k_pe] + \
                ([t_ksf] if e >= HALO else [])
            v_pend = (e, sl, t_unT_ev)
        ve, vsl, vtok = v_pend
        lhsv = lambda c, s=vsl: self.unT[s][:, c * 128:(c + 1) * 128]
        v_pe, t_vev = self.emit_proj(lhsv, self.w2, vtok, w2_tok,
                                     out_bf=self.kvo, out_war=kvo_free)
        w2_last = v_pe
        unT_free[vsl] = unT_free[vsl] + [v_pe]
        P.store(self.vss[ve], self.kvo[:], waits=[t_vev])
        kv_stored[ve] = P.sfence(self)
        kvo_free = [kv_stored[ve]]

        w1_tok = self.wload(self.wqd, self.w1, [w1_last])

        ring_free = [None] * NT_EXT
        ring_f = [None] * NT_EXT
        for e2 in range(HALO):
            self._ring_load(1, e2, e2, [kv_stored[e2], w2_last])
        P.load(self.unT[0][:], self.unTs[0],
               waits=[kv_stored[NT_EXT - 1], unT_free[0]])
        P.load(self.unT[1][:], self.unTs[1], waits=[unT_free[1]])
        lfB1 = P.lfence(self)
        for e2 in range(HALO):
            ring_f[e2] = lfB1
        unTl_f = [None] * NT_OWN
        unTl_f[0] = unTl_f[1] = lfB1

        qT_src_tok = [None] * NT_OWN
        s_last = None
        qT_store = [None] * NT_OWN
        sf1 = [None] * NT_OWN
        attn_prev = None
        self.p_free = None
        self.pt_free = None

        q_pe_by_slot = [None, None]
        for i in range(NT_OWN):
            sl = i % 2
            lhsq = lambda c, s=sl: self.unT[s][:, c * 128:(c + 1) * 128]
            q_pe, t_qev = self.emit_proj(lhsq, self.w1, unTl_f[i], w1_tok,
                                         out_bf=self.buf_a, out_war=bufa_free,
                                         evac_scale=SCALE_Q)
            q_pe_by_slot[sl] = q_pe
            if i + 2 < NT_OWN:
                P.load(self.unT[sl][:], self.unTs[i + 2], waits=[q_pe])
            if i + HALO < NT_EXT:
                self._ring_load(1, i + HALO, (i + HALO) % RING,
                                [ring_free[i - 2] if i >= 2 else None])
            lf1 = P.lfence(self)
            if i + 2 < NT_OWN:
                unTl_f[i + 2] = lf1
            if i + HALO < NT_EXT:
                ring_f[i + HALO] = lf1
            if attn_prev is not None:
                ip = attn_prev
                t_avp, evsp = self.emit_attn_tile(
                    1, ip, self.kvo[:, 0:1024], qT_src_tok[ip],
                    ring_f[ip + HALO], 0, sf1[ip] if ip >= 1 else None)
                ring_free[ip] = t_avp
                P.store(self.oTs[ip], self.oT_sb[:, 0:1024], waits=[evsp[-1]])
                s_last = t_avp
            t_qr = self.emit_rope(self.buf_a, i + HALO, t_qev)
            t_qt_pe, t_qtev = self.emit_transposes(
                self.buf_a, self.kvo, t_qr,
                [s_last, sf1[i - 1] if i >= 1 else kv_stored[NT_EXT - 1]])
            bufa_free = t_qt_pe
            qT_src_tok[i] = t_qtev
            qT_store[i] = P.store(self.qTs[i], self.kvo[:, 1024:2048],
                                  waits=[t_qtev])
            sf1[i] = P.sfence(self)
            attn_prev = i
        ip = attn_prev
        t_avp, evsp = self.emit_attn_tile(1, ip, self.kvo[:, 0:1024],
                                          qT_src_tok[ip], ring_f[ip + HALO],
                                          0, sf1[ip])
        ring_free[ip] = t_avp
        P.store(self.oTs[ip], self.oT_sb[:, 0:1024], waits=[evsp[-1]])
        sf1_final = P.sfence(self)
        b1_ring_last = t_avp
        b1_q_last = q_pe_by_slot[1]

        w2_tok = self.wload(self.wod, self.w2, [b1_ring_last])

        ring_free2 = [None] * NT_EXT
        ring_f2 = [None] * NT_EXT
        for e2 in range(HALO + 1):
            self._ring_load(2, e2, e2, [kv_stored[e2], b1_q_last])
        u_ld = [None] * NT_OWN
        qT_ld = [None] * NT_OWN
        oT_ld = [None] * NT_OWN
        y_stored = [None] * NT_OWN
        P.load(self.xu[0][:], self.us_[0], waits=[xu_free[0], sf1_final])
        P.load(self.xu[1][:], self.us_[1], waits=[xu_free[1]])
        P.load(self.un_sb[:, 0:1024], self.qTs[0], waits=[un_free])
        P.load(self.un_sb[:, 1024:2048], self.qTs[1])
        P.load(self.unT[0][:, 0:1024], self.oTs[0], waits=[q_pe_by_slot[0]])
        P.load(self.unT[1][:, 0:1024], self.oTs[1], waits=[q_pe_by_slot[1]])
        lfB2 = P.lfence(self)
        for e2 in range(HALO + 1):
            ring_f2[e2] = lfB2
        u_ld[0] = u_ld[1] = lfB2
        qT_ld[0] = qT_ld[1] = lfB2
        oT_ld[0] = oT_ld[1] = lfB2

        self.p_free = None
        self.pt_free = None
        oT_war2 = None
        ffn_pend = None
        ln_war2 = [None, None]
        gate_war2 = [None, None]
        self.xn_free2 = None
        self.xnT_free2 = None

        for i in range(NT_OWN):
            sl = i % 2
            if i + 1 + HALO < NT_EXT:
                self._ring_load(2, i + 1 + HALO, (i + 1 + HALO) % RING,
                                [ring_free2[i - 1] if i >= 1 else None])
            if ffn_pend is not None:
                ip2, slp, t_anp = ffn_pend
                t_yf = self._b2_tail(ip2, slp, t_anp, gate_war2)
                y_stored[ip2] = t_yf
                xu_free[slp] = [t_yf]
                if ip2 + 2 < NT_OWN:
                    P.load(self.xu[slp][:], self.us_[ip2 + 2], waits=[t_yf])
                ffn_pend = None
            t_av2, evs2 = self.emit_attn_tile(
                2, i, self.un_sb[:, sl * 1024:(sl + 1) * 1024], qT_ld[i],
                ring_f2[i + HALO], 1024, oT_war2)
            ring_free2[i] = t_av2
            if i + 2 < NT_OWN:
                P.load(self.un_sb[:, sl * 1024:(sl + 1) * 1024],
                       self.qTs[i + 2], waits=[t_av2])
            def lhso(c, s=sl):
                if c < 8:
                    return self.unT[s][:, c * 128:(c + 1) * 128]
                return self.oT_sb[:, 1024 + (c - 8) * 128:1024 + (c - 7) * 128]
            o_pe, t_a = self.emit_proj(lhso, self.w2,
                                       [oT_ld[i], evs2[-1]], w2_tok,
                                       x_dst=self.xu[sl], x_tok=u_ld[i])
            oT_war2 = o_pe
            if i + 2 < NT_OWN:
                P.load(self.unT[sl][:, 0:1024], self.oTs[i + 2], waits=[o_pe])
            lf2 = P.lfence(self)
            if i + 2 < NT_OWN:
                qT_ld[i + 2] = lf2
                oT_ld[i + 2] = lf2
            if i + 1 < NT_OWN:
                u_ld[i + 1] = lf2 if u_ld[i + 1] is None else u_ld[i + 1]
            if i + 1 + HALO < NT_EXT:
                ring_f2[i + 1 + HALO] = lf2
            t_an, _ = self.emit_ln(self.xu[sl], self.xn_sb, sl, t_a,
                                   self.xn_free2, ln_war2[sl])
            ln_war2[sl] = t_an
            ffn_pend = (i, sl, t_an)
        ip2, slp, t_anp = ffn_pend
        y_stored[ip2] = self._b2_tail(ip2, slp, t_anp, gate_war2)

        total = P.cnt["qst"]
        P.op("st", lambda e: e.nop(), waits=[("qst", total)])

        nc = self.nc
        with ExitStack() as bctx:
            sems = {}
            for s in Prog.SEMS:
                sems[s] = bctx.enter_context(nc.semaphore(s))
            blk = bctx.enter_context(nc.Block())

            @blk.tensor
            def _(e):
                P.replay("pe", e, sems)

            @blk.vector
            def _(e):
                P.replay("dv", e, sems)

            @blk.scalar
            def _(e):
                P.replay("ac", e, sems)

            @blk.sync
            def _(e):
                P.replay("ld", e, sems)

            @blk.gpsimd
            def _(e):
                P.replay("st", e, sems)
        self.ctx.close()
        return nc

    def _b2_tail(self, i, sl, t_an, gate_war2):
        P = self.P
        t_anT_pe, t_anT_ev = self.emit_transposes(self.xn_sb, self.xnT_sb,
                                                  t_an, self.xnT_free2)
        self.xn_free2 = t_anT_pe
        t_y, t_g2 = self.emit_ffn(self.xnT_sb, t_anT_ev, self.xu[sl],
                                  t_an, sl, gate_war2[sl])
        self.xnT_free2 = t_g2
        gate_war2[sl] = t_y
        P.store(self.yd[i], self.xu[sl][:], waits=[t_y])
        return P.sfence(self)


def kernel(L0_attn_hidden_states, L0_attn_wq, L0_attn_wk, L0_attn_wv,
           L0_attn_wo, L0_ffn_gate_w, L0_ffn_lora_A, L0_ffn_lora_B):
    x = np.asarray(L0_attn_hidden_states, np.float32)
    in_maps = _host_prepare(x, L0_attn_wq, L0_attn_wk, L0_attn_wv, L0_attn_wo,
                            L0_ffn_gate_w, L0_ffn_lora_A, L0_ffn_lora_B)
    k = Kern()
    nc = k.build()
    res = run_bass_kernel_spmd(nc, in_maps, core_ids=list(range(8)))
    out = np.empty((BSZ, SEQ, D), np.float32)
    for c in range(8):
        b, half = c // 2, c % 2
        y = np.asarray(res.results[c]["y"], np.float32).reshape(2048, D)
        out[b, half * 2048:(half + 1) * 2048, :] = y
    return out


# version 05
# speedup vs baseline: 1.1037x; 1.0530x over previous
import numpy as np
import ml_dtypes
import concourse.bass as bass
import concourse.mybir as mybir
from concourse.bass_utils import run_bass_kernel_spmd
from concourse.ap import AP
from contextlib import ExitStack

F32 = mybir.dt.float32
BF16 = mybir.dt.bfloat16
AX = mybir.AxisListType.X
ALU = mybir.AluOpType
AF = mybir.ActivationFunctionType

BSZ, SEQ, D = 4, 4096, 2048
NH, HD, WIN = 16, 128, 1024
NT_OWN, HALO = 16, 8
NT_EXT = NT_OWN + HALO
RING = 10
NBLK = 9
NCH = 16
SCALE_Q = float(128.0 ** -0.5)
NEG = -1.0e30
ER = 128
GAW = 8 + ER

DEBUG = False


def bf(x):
    return np.asarray(x, dtype=ml_dtypes.bfloat16)



PARANOID = False


class Prog:
    SEMS = ("qls", "qst", "qlf", "qsf", "pe", "dv", "ac", "po")
    DMA_SEMS = ("qls", "qst", "qlf", "qsf")
    OWNER = {"pe": "pe", "dv": "dv", "ac": "ac", "po": "st", "qls": None,
             "qst": None, "qlf": None, "qsf": None}

    def __init__(self):
        self.stream = {k: [] for k in ("pe", "dv", "ac", "ld", "st")}
        self.cnt = {s: 0 for s in self.SEMS}

    def op(self, eng, fn, waits=(), inc=None, drain=False):
        w = []
        for x in waits:
            if x is None:
                continue
            if isinstance(x, list):
                w += [t for t in x if t is not None]
            else:
                w.append(x)
        if PARANOID:
            w += [(s, self.cnt[s]) for s in ("pe", "dv", "ac")]
        tok = None
        amt = 0
        if inc is not None:
            amt = 16 if inc in self.DMA_SEMS else 1
            self.cnt[inc] += amt
            tok = (inc, self.cnt[inc])
        self.stream[eng].append((fn, w, inc, amt, drain))
        return tok

    def load(self, out, in_, waits=()):
        return self.op("ld", lambda e, o=out, i=in_: e.dma_start(out=o, in_=i),
                       waits=waits, inc="qls")

    def store(self, out, in_, waits=()):
        return self.op("st", lambda e, o=out, i=in_: e.dma_start(out=o, in_=i),
                       waits=waits, inc="qst")

    def lfence(self, kern):
        return self.op("ld", lambda e: e.dma_start(out=kern.fence_sb[:],
                                                   in_=kern.fin_d[:]),
                       inc="qlf")

    def sfence(self, kern):
        return self.op("st", lambda e: e.dma_start(out=kern.fout_d[:],
                                                   in_=kern.fence_sb[:]),
                       inc="qsf")

    def mm(self, out, lhsT, rhs, start, stop, waits=(), inc=False):
        return self.op(
            "pe",
            lambda e, o=out, l=lhsT, r=rhs, s0=start, s1=stop: e.matmul(
                o, l, r, start=s0, stop=s1),
            waits=waits, inc="pe" if inc else None)

    def replay(self, ename, eng, sems):
        seen = {}
        for fn, waits, inc, amt, drain in self.stream[ename]:
            best = {}
            for (s, v) in waits:
                if Prog.OWNER[s] == ename:
                    continue
                if best.get(s, 0) < v:
                    best[s] = v
            for s, v in best.items():
                if seen.get(s, 0) >= v:
                    continue
                eng.wait_ge(sems[s], v)
                seen[s] = v
            inst = fn(eng)
            if drain:
                inst = eng.drain()
            if inc is not None:
                inst.then_inc(sems[inc], amt)


class PsumBlock:
    def __init__(self, alloc, banks, waits):
        self.a = alloc
        self.banks = banks
        self.waits = waits
        self.c0 = banks[0] * 512

    def ap(self, col0, col1):
        return self.a.ps[:, self.c0 + col0:self.c0 + col1]

    def free(self, *toks):
        toks = [t for t in toks if t is not None]
        for b in self.banks:
            self.a.tok[b] = toks


class PsumAlloc:
    def __init__(self, ps):
        self.ps = ps
        self.tok = [[] for _ in range(8)]
        self.ptr = 0

    def alloc(self, nbanks):
        if self.ptr + nbanks > 8:
            self.ptr = 0
        b0 = self.ptr
        self.ptr = (self.ptr + nbanks) % 8
        banks = list(range(b0, b0 + nbanks))
        waits = []
        for b in banks:
            waits += self.tok[b]
        return PsumBlock(self, banks, waits)



def _rope_tables(positions):
    ntiles = len(positions) // 128
    inv = 1.0 / (10000.0 ** (np.arange(0, HD, 2, dtype=np.float64) / HD))
    pos = np.asarray(positions, np.float64)
    fr = pos[:, None] * inv[None, :]
    c = np.cos(fr).reshape(ntiles, 128, 64).transpose(1, 0, 2).reshape(
        128, ntiles * 64)
    s = (-np.sin(fr)).reshape(ntiles, 128, 64).transpose(1, 0, 2).reshape(
        128, ntiles * 64)
    return bf(c), bf(s)


def _host_prepare(x, wq, wk, wv, wo, gate_w, lora_A, lora_B):
    qq = np.arange(128)[:, None]
    kk = np.arange(128)[None, :]
    m_w = bf(np.where(kk > qq, 0.0, NEG))
    m_c = bf(np.where(kk <= qq, 0.0, NEG))
    ident = bf(np.eye(128))
    gA = np.ascontiguousarray(
        np.concatenate([np.asarray(gate_w).T.astype(np.float32, copy=False),
                        np.asarray(lora_A).reshape(ER, D).T.astype(
                            np.float32, copy=False)], axis=1))
    bstack = np.ascontiguousarray(
        np.asarray(lora_B).transpose(0, 2, 1).reshape(ER, D).astype(
            np.float32, copy=False))
    shared = dict(wq=np.asarray(wq, np.float32), wk=np.asarray(wk, np.float32),
                  wv=np.asarray(wv, np.float32), wo=np.asarray(wo, np.float32),
                  gA=gA, bstack=bstack, ident=ident, m_w=m_w, m_c=m_c,
                  ident_f32=np.eye(128, dtype=np.float32))
    x = np.asarray(x, np.float32)
    in_maps = []
    for b in range(BSZ):
        for half in range(2):
            if half == 0:
                xe = np.concatenate(
                    [np.zeros((HALO * 128, D), np.float32), x[b, 0:2048]], axis=0)
                ext_pos = np.concatenate([np.zeros(HALO * 128), np.arange(0, 2048)])
            else:
                xe = np.ascontiguousarray(x[b, 1024:4096])
                ext_pos = np.arange(1024, 4096)
            c2h, s2h = _rope_tables(ext_pos)
            npad = np.zeros((128, NT_OWN), np.float32)
            if half == 0:
                for i in range(NT_OWN):
                    npb = max(0, HALO - i)
                    if npb == 0:
                        continue
                    cnt = np.full(128, 128.0 * (npb - 1))
                    cnt += 127.0 - np.arange(128)
                    npad[:, i] = cnt
            m = dict(shared)
            m.update(x_ext=np.ascontiguousarray(xe.reshape(NT_EXT, 128, D)),
                     c2=c2h, s2=s2h, npad=npad,
                     fence_in=np.zeros((128, 16), np.float32))
            in_maps.append(m)
    return in_maps



class Kern:
    def __init__(self):
        self.nc = bass.Bass("TRN2", target_bir_lowering=False)
        self.ctx = ExitStack()
        self.P = Prog()
        self._declare()
        self.pa = PsumAlloc(self.ps_all)

    def _declare(self):
        nc = self.nc
        dt = lambda n, s, d, k: nc.dram_tensor(n, s, d, kind=k).ap()
        self.xd = dt("x_ext", [NT_EXT, 128, D], F32, "ExternalInput")
        self.wqd = dt("wq", [D, D], F32, "ExternalInput")
        self.wkd = dt("wk", [D, D], F32, "ExternalInput")
        self.wvd = dt("wv", [D, D], F32, "ExternalInput")
        self.wod = dt("wo", [D, D], F32, "ExternalInput")
        self.gAd = dt("gA", [D, GAW], F32, "ExternalInput")
        self.bsd = dt("bstack", [ER, D], F32, "ExternalInput")
        self.c2d = dt("c2", [128, NT_EXT * 64], BF16, "ExternalInput")
        self.s2d = dt("s2", [128, NT_EXT * 64], BF16, "ExternalInput")
        self.npd = dt("npad", [128, NT_OWN], F32, "ExternalInput")
        self.idd = dt("ident", [128, 128], BF16, "ExternalInput")
        self.mwd = dt("m_w", [128, 128], BF16, "ExternalInput")
        self.mcd = dt("m_c", [128, 128], BF16, "ExternalInput")
        self.fin_d = dt("fence_in", [128, 16], F32, "ExternalInput")
        self.idfd = dt("ident_f32", [128, 128], F32, "ExternalInput")
        self.fout_d = dt("fence_out", [128, 16], F32, "Internal")
        self.yd = dt("y", [NT_OWN, 128, D], F32, "ExternalOutput")
        SK = "ExternalOutput" if DEBUG else "Internal"
        self.kTs = dt("kT_s", [NT_EXT, 128, D], BF16, SK)
        self.vss = dt("v_s", [NT_EXT, 128, D], BF16, SK)
        self.us_ = dt("u_s", [NT_OWN, 128, D], F32, SK)
        self.unTs = dt("unT_s", [NT_OWN, 128, D], BF16, SK)
        self.qTs = dt("qT_s", [NT_OWN, 128, 1024], BF16, SK)
        self.oTs = dt("oT_s", [NT_OWN, 128, 1024], BF16, SK)

        sb = lambda n, s, d: self.ctx.enter_context(nc.sbuf_tensor(n, s, d))
        self.fence_sb = sb("fence_sb", [128, 16], F32)
        self.id_sb = sb("id_sb", [128, 128], BF16)
        self.idf_sb = sb("idf_sb", [128, 128], F32)
        self.diag = [sb("diag0", [128, 128], BF16), sb("diag1", [128, 128], BF16)]
        self.mw_sb = sb("mw_sb", [128, 128], BF16)
        self.mc_sb = sb("mc_sb", [128, 128], BF16)
        self.np_sb = sb("np_sb", [128, NT_OWN], F32)
        self.c2_sb = sb("c2_sb", [128, NT_EXT * 64], BF16)
        self.s2_sb = sb("s2_sb", [128, NT_EXT * 64], BF16)
        self.gA_sb = sb("gA_sb", [128, NCH * GAW], BF16)
        self.bs_sb = sb("bs_sb", [128, D], BF16)
        self.w1 = sb("w1", [128, NCH * D], BF16)
        self.w2 = sb("w2", [128, NCH * D], BF16)
        self.xu = [sb("xu0", [128, D], F32), sb("xu1", [128, D], F32)]
        self.xn_sb = sb("xn_sb", [128, D], BF16)
        self.xnT_sb = sb("xnT_sb", [128, D], BF16)
        self.un_sb = sb("un_sb", [128, D], BF16)
        self.unT = [sb("unT0", [128, D], BF16), sb("unT1", [128, D], BF16)]
        self.buf_a = sb("buf_a", [128, D], BF16)
        self.buf_b = sb("buf_b", [128, D], BF16)
        self.kvo = sb("kvo", [128, D], BF16)
        self.p_sb = [sb("p_sb0", [128, NBLK * 128], BF16),
                     sb("p_sb1", [128, NBLK * 128], BF16)]
        self.pt_sb = sb("pt_sb", [128, NBLK * 128], BF16)
        self.oT_sb = sb("oT_sb", [128, D], BF16)
        mk2 = lambda n, w: [sb(n + "0", [128, w], F32), sb(n + "1", [128, w], F32)]
        self.st1 = mk2("st1", 1)
        self.ssq = mk2("ssq", 1)
        self.mean = mk2("mean", 1)
        self.nmean = mk2("nmean", 1)
        self.msq = mk2("msq", 1)
        self.var = mk2("var", 1)
        self.lnv = mk2("lnv", 1)
        self.rstd = mk2("rstd", 1)
        self.g8a = mk2("g8a", 8)
        self.g8b = mk2("g8b", 8)
        self.gm1 = mk2("gm1", 1)
        self.gm2 = mk2("gm2", 1)
        self.gdl = mk2("gdl", 1)
        self.ge = mk2("ge", 1)
        self.gden = mk2("gden", 1)
        self.grcp = mk2("grcp", 1)
        self.comb8 = mk2("comb8", 8)
        self.comb_er = [sb("comber", [128, ER], F32)] * 2
        self.ch_sb = [sb("ch0", [128, ER], BF16), sb("ch1", [128, ER], BF16)]
        self.chT_sb = [sb("chT0", [128, 128], BF16), sb("chT1", [128, 128], BF16)]
        self.nm_all = sb("nm_all", [128, 8], F32)
        self.sum_all = sb("sum_all", [128, 8], F32)
        self.enm = sb("enm", [128, 1], F32)
        self.corr = sb("corr", [128, 1], F32)
        self.rcp_s = sb("rcp_s", [128, 1], F32)
        self.ps_all = self.ctx.enter_context(
            nc.psum_tensor("ps_all", [128, 4096], F32))

    def ring_k(self, phase, slot):
        a = self.w2 if phase == 1 else self.w1
        return a[:, slot * 1024:(slot + 1) * 1024]

    def ring_v(self, phase, slot):
        a = self.w2 if phase == 1 else self.w1
        return a[:, (RING + slot) * 1024:(RING + slot + 1) * 1024]

    def _ring_load(self, phase, e, slot, waits):
        P = self.P
        cols = slice(0, 1024) if phase == 1 else slice(1024, 2048)
        P.load(self.ring_k(phase, slot), self.kTs[e][:, cols], waits=waits)
        return P.load(self.ring_v(phase, slot), self.vss[e][:, cols], waits=())

    def wload(self, wdram, dest, waits):
        P = self.P
        for g in range(4):
            src = wdram[g * 512:(g + 1) * 512, :].rearrange(
                "(c p) d -> p c d", p=128)
            dst = dest[:, g * 4 * D:(g + 1) * 4 * D].rearrange(
                "p (c d) -> p c d", c=4)
            P.op("st", lambda e, o=dst, i=src: e.dma_start(out=o, in_=i),
                 waits=waits, inc="qst")
        return P.sfence(self)

    def emit_ln(self, x_src, out_bf, par, ld_tok, out_war, stats_war,
                sq_war=None):
        P = self.P
        s = lambda b: b[par]
        t_s1 = P.op("dv", lambda e: e.reduce_sum(s(self.st1)[:], x_src[:],
                                                 axis=AX),
                    waits=[ld_tok, stats_war], drain=True, inc="dv")
        t_sq = P.op("ac", lambda e: e.activation(
            self.buf_b[:], x_src[:], AF.Square, accum_out=s(self.ssq)[:, 0:1]),
            waits=[ld_tok, stats_war, sq_war], drain=True, inc="ac")
        P.op("st", lambda e: e.tensor_scalar(
            s(self.mean)[:], s(self.st1)[:], 1.0 / D, None, ALU.mult),
            waits=[t_s1, stats_war])
        P.op("st", lambda e: e.tensor_scalar(
            s(self.nmean)[:], s(self.mean)[:], -1.0, None, ALU.mult))
        P.op("st", lambda e: e.tensor_mul(
            s(self.msq)[:], s(self.mean)[:], s(self.mean)[:]))
        P.op("st", lambda e: e.tensor_scalar(
            s(self.var)[:], s(self.ssq)[:], 1.0 / D, 1e-5, ALU.mult, ALU.add),
            waits=[t_sq])
        t_po = P.op("st", lambda e: e.tensor_sub(
            s(self.var)[:], s(self.var)[:], s(self.msq)[:]), inc="po")
        P.op("ac", lambda e: e.activation(
            s(self.lnv)[:], s(self.var)[:, 0:1], AF.Ln),
            waits=[t_po], drain=True)
        t_r = P.op("ac", lambda e: e.activation(
            s(self.rstd)[:], s(self.lnv)[:, 0:1], AF.Exp, scale=-0.5),
            drain=True, inc="ac")
        t_ap = P.op("dv", lambda e: e.tensor_scalar(
            out_bf[:], x_src[:], s(self.nmean)[:, 0:1], s(self.rstd)[:, 0:1],
            ALU.add, ALU.mult), waits=[t_r, t_po, out_war], drain=True, inc="dv")
        return t_ap, t_sq

    def emit_transposes(self, src_bf, dst_bf, src_tok, dst_war):
        P = self.P
        pe_last = None
        blks = []
        for half in range(2):
            blk = self.pa.alloc(2)
            t_mm = None
            for c in range(8):
                cc = half * 8 + c
                t_mm = P.mm(blk.ap(c * 128, (c + 1) * 128),
                            src_bf[:, cc * 128:(cc + 1) * 128], self.id_sb[:],
                            True, True,
                            waits=([src_tok, self.t_id] + blk.waits)
                            if c == 0 else (),
                            inc=(c == 7))
            blks.append((blk, t_mm))
            pe_last = t_mm
        for half, (blk, t_mm) in enumerate(blks):
            P.op("ac", lambda e, h=half, b=blk: e.activation(
                dst_bf[:, h * 1024:(h + 1) * 1024], b.ap(0, 1024), AF.Copy),
                waits=[t_mm, dst_war])
        ev = P.op("ac", lambda e: e.drain(), inc="ac")
        for blk, _ in blks:
            blk.free(ev)
        return pe_last, ev

    def emit_proj(self, lhs_fn, rhs_sb, lhs_tok, rhs_tok, out_bf=None,
                  out_war=None, x_dst=None, x_tok=None, evac_scale=None):
        P = self.P
        pe_last = None
        blks = []
        for half in range(2):
            blk = self.pa.alloc(2)
            t_mm = None
            for c in range(NCH):
                for s2 in range(2):
                    col = half * 1024 + s2 * 512
                    t_mm = P.mm(
                        blk.ap(s2 * 512, (s2 + 1) * 512), lhs_fn(c),
                        rhs_sb[:, c * D + col:c * D + col + 512],
                        start=(c == 0), stop=(c == NCH - 1),
                        waits=([lhs_tok, rhs_tok] + blk.waits)
                        if (c == 0 and s2 == 0) else (),
                        inc=(c == NCH - 1 and s2 == 1))
            blks.append((blk, t_mm))
            pe_last = t_mm
        eng = "dv" if x_dst is not None else "ac"
        for half, (blk, t_mm) in enumerate(blks):
            if x_dst is not None:
                P.op("dv", lambda e, h=half, b=blk: e.tensor_add(
                    x_dst[:, h * 1024:(h + 1) * 1024],
                    x_dst[:, h * 1024:(h + 1) * 1024], b.ap(0, 1024)),
                    waits=[t_mm, x_tok])
            else:
                kw = {} if evac_scale is None else {"scale": evac_scale}
                P.op("ac", lambda e, h=half, b=blk, kw=kw: e.activation(
                    out_bf[:, h * 1024:(h + 1) * 1024], b.ap(0, 1024),
                    AF.Copy, **kw),
                    waits=[t_mm, out_war])
        out_tok = P.op(eng, lambda e: e.drain(), inc=eng)
        for blk, _ in blks:
            blk.free(out_tok)
        return pe_last, out_tok

    def emit_rope(self, src_bf, tile_e, src_tok):
        P = self.P
        off = tile_e * 64
        csl = self.c2_sb[:, off:off + 64]
        ssl = self.s2_sb[:, off:off + 64]
        cbc = AP(csl.tensor, csl.offset,
                 [list(csl.ap[0]), [0, NH], [0, 2], [1, 64]])
        sbc = AP(ssl.tensor, ssl.offset,
                 [list(ssl.ap[0]), [0, NH], [0, 2], [1, 64]])
        sv = src_bf[:]
        rot = AP(sv.tensor, sv.offset + 64,
                 [list(sv.ap[0]), [128, NH], [-64, 2], [1, 64]])
        t4 = lambda t: t[:].rearrange("p (h x j) -> p h x j", h=NH, x=2)
        P.op("dv", lambda e: e.tensor_tensor(
            t4(self.buf_b), t4(src_bf), cbc, ALU.mult),
            waits=[src_tok, self.t_tab])
        P.op("dv", lambda e: e.tensor_tensor(t4(self.xn_sb), rot, sbc, ALU.mult))
        fh = lambda t: AP(t[:].tensor, t[:].offset,
                          [list(t[:].ap[0]), [128, NH], [1, 64]])
        sh = lambda t: AP(t[:].tensor, t[:].offset + 64,
                          [list(t[:].ap[0]), [128, NH], [1, 64]])
        P.op("dv", lambda e: e.tensor_tensor(
            fh(src_bf), fh(self.buf_b), fh(self.xn_sb), ALU.add))
        return P.op("dv", lambda e: e.tensor_tensor(
            sh(src_bf), sh(self.buf_b), sh(self.xn_sb), ALU.subtract),
            drain=True, inc="dv")

    def emit_ffn(self, actT_sb, actT_tok, x_dst, x_tok, par, war):
        P = self.P
        s = lambda b: b[par]
        gblk = self.pa.alloc(1)
        t_g = None
        for c in range(NCH):
            t_g = P.mm(gblk.ap(0, GAW), actT_sb[:, c * 128:(c + 1) * 128],
                       self.gA_sb[:, c * GAW:(c + 1) * GAW],
                       start=(c == 0), stop=(c == NCH - 1),
                       waits=([actT_tok, self.t_gA] + gblk.waits)
                       if c == 0 else (),
                       inc=(c == NCH - 1))
        lg = gblk.ap(0, 8)
        hh = gblk.ap(8, GAW)
        P.op("dv", lambda e: e.reduce_max(s(self.gm1)[:], lg, axis=AX),
             waits=[t_g, war], drain=True)
        P.op("dv", lambda e: e.tensor_single_scalar(
            s(self.g8a)[:], lg, s(self.gm1)[:, 0:1], ALU.is_ge), drain=True)
        P.op("dv", lambda e: e.scalar_tensor_tensor(
            s(self.g8b)[:], s(self.g8a)[:], NEG, lg, ALU.mult, ALU.add),
            drain=True)
        P.op("dv", lambda e: e.reduce_max(s(self.gm2)[:], s(self.g8b)[:], axis=AX),
             drain=True)
        P.op("dv", lambda e: e.tensor_single_scalar(
            s(self.g8b)[:], s(self.g8b)[:], s(self.gm2)[:, 0:1], ALU.is_ge),
            drain=True)
        t_d = P.op("dv", lambda e: e.tensor_sub(
            s(self.gdl)[:], s(self.gm2)[:], s(self.gm1)[:]), drain=True, inc="dv")
        t_e = P.op("ac", lambda e: e.activation(
            s(self.ge)[:], s(self.gdl)[:, 0:1], AF.Exp),
            waits=[t_d], drain=True, inc="ac")
        P.op("dv", lambda e: e.tensor_scalar(
            s(self.gden)[:], s(self.ge)[:], 1.0, None, ALU.add),
            waits=[t_e], drain=True)
        P.op("dv", lambda e: e.reciprocal(s(self.grcp)[:], s(self.gden)[:]),
             drain=True)
        P.op("dv", lambda e: e.scalar_tensor_tensor(
            s(self.comb8)[:], s(self.g8b)[:], s(self.ge)[:, 0:1], s(self.g8a)[:],
            ALU.mult, ALU.add), drain=True)
        P.op("dv", lambda e: e.tensor_single_scalar(
            s(self.comb8)[:], s(self.comb8)[:], s(self.grcp)[:, 0:1], ALU.mult),
            drain=True)
        c8 = s(self.comb8)[:]
        cbc = AP(c8.tensor, c8.offset, [list(c8.ap[0]), [1, 8], [0, 16]])
        P.op("dv", lambda e: e.tensor_copy(
            s(self.comb_er)[:].rearrange("p (x r) -> p x r", x=8), cbc),
            drain=True)
        t_ch = P.op("dv", lambda e: e.tensor_mul(
            s(self.ch_sb)[:], hh, s(self.comb_er)[:]), drain=True, inc="dv")
        gblk.free(t_ch)
        cblk = self.pa.alloc(1)
        t_ct = P.mm(cblk.ap(0, 128), s(self.ch_sb)[:], self.id_sb[:], True, True,
                    waits=[t_ch] + cblk.waits, inc=True)
        t_ce = P.op("dv", lambda e, b=cblk: e.tensor_copy(
            s(self.chT_sb)[:], b.ap(0, 128)), waits=[t_ct], drain=True, inc="dv")
        cblk.free(t_ce)
        blks = []
        for half in range(2):
            blk = self.pa.alloc(2)
            t_mm = None
            for s2 in range(2):
                col = half * 1024 + s2 * 512
                t_mm = P.mm(blk.ap(s2 * 512, (s2 + 1) * 512), s(self.chT_sb)[:],
                            self.bs_sb[:, col:col + 512], True, True,
                            waits=([t_ce] + blk.waits) if s2 == 0 else (),
                            inc=(s2 == 1))
            blks.append((blk, t_mm))
        for half, (blk, t_mm) in enumerate(blks):
            P.op("dv", lambda e, h=half, b=blk: e.tensor_add(
                x_dst[:, h * 1024:(h + 1) * 1024],
                x_dst[:, h * 1024:(h + 1) * 1024], b.ap(0, 1024)),
                waits=[t_mm, x_tok])
        t_add = P.op("dv", lambda e: e.drain(), inc="dv")
        for blk, _ in blks:
            blk.free(t_add)
        return t_add, t_g

    def emit_attn_tile(self, phase, i, qsrc, q_tok, ring_tok, oT_off, oT_war):
        P = self.P
        ps = self.ps_all
        SREG = (ps[:, 0:1152], ps[:, 1536:2688])
        SLACK = (ps[:, 1216:1344], ps[:, 2752:2880])
        PTA = ps[:, 3072:4096]
        AVR = ps[:, 3072:3200]
        claim = []
        for b in range(8):
            claim += self.pa.tok[b]
            self.pa.tok[b] = []
        s_tok = [None] * 8
        exp_tok = [None] * 8
        oev_tok = [None] * 8
        pt_tok = [None] * 8
        av_tok = [None] * 8
        peb_tok = [None] * 8

        def emit_S(h):
            reg = SREG[h % 2]
            w = [q_tok, ring_tok, self.t_mask, self.t_id, self.t_np]
            if h >= 2:
                w += [exp_tok[h - 2]]
                if h >= 4:
                    w += [peb_tok[h - 4]]
            else:
                w = w + claim
            P.mm(reg[:, 0:128], self.id_sb[:], self.mw_sb[:], True, False,
                 waits=w)
            P.mm(reg[:, 8 * 128:9 * 128], self.id_sb[:], self.mc_sb[:],
                 True, False)
            t = None
            for b in range(NBLK):
                slot = (i + b) % RING
                t = P.mm(reg[:, b * 128:(b + 1) * 128],
                         qsrc[:, h * 128:(h + 1) * 128],
                         self.ring_k(phase, slot)[:, h * 128:(h + 1) * 128],
                         start=(b not in (0, 8)), stop=True,
                         inc=(b == NBLK - 1))
            s_tok[h] = t

        emit_S(0)
        emit_S(1)
        pea_tok = [None] * 8
        for h in range(8):
            if h >= 1:
                oev_tok[h - 1] = P.op(
                    "ac", lambda e, h=h, o=oT_off: e.activation(
                        self.oT_sb[:, o + (h - 1) * 128:o + h * 128], AVR,
                        AF.Copy),
                    waits=[av_tok[h - 1], oT_war], drain=True, inc="ac")
            reg = SREG[h % 2]
            t_nm = P.op("dv", lambda e, h=h, r=reg: e.reduce_max(
                self.nm_all[:, h:h + 1], r, axis=AX, negate=True),
                waits=[s_tok[h]], drain=True, inc="dv")
            P.op("ac", lambda e, h=h, r=reg: e.activation(
                self.p_sb[h % 2][:], r, AF.Exp, bias=self.nm_all[:, h:h + 1],
                accum_out=self.sum_all[:, h:h + 1]),
                waits=[t_nm, pt_tok[h - 2] if h >= 2 else None])
            t_exp = P.op("ac", lambda e, h=h: e.activation(
                self.enm[:], self.nm_all[:, h:h + 1], AF.Exp),
                drain=True, inc="ac")
            exp_tok[h] = t_exp
            P.op("st", lambda e, i=i: e.tensor_mul(
                self.corr[:], self.np_sb[:, i:i + 1], self.enm[:]),
                waits=[t_exp])
            t_corr = P.op("st", lambda e, h=h: e.tensor_sub(
                self.corr[:], self.sum_all[:, h:h + 1], self.corr[:]),
                inc="po")
            t_rcp = P.op("dv", lambda e: e.reciprocal(
                self.rcp_s[:], self.corr[:]),
                waits=[t_corr], drain=True, inc="dv")
            rb = self.rcp_s[:]
            rbc = AP(rb.tensor, rb.offset, [list(rb.ap[0]), [0, 128]])
            dg = self.diag[h % 2]
            t_diag = P.op("st", lambda e, d=dg, rbc=rbc: e.tensor_tensor(
                d[:], self.idf_sb[:], rbc, ALU.mult),
                waits=[t_rcp, self.diag_war[h % 2]], inc="po")
            w0 = [t_diag]
            if h >= 1:
                w0 += [pea_tok[h - 1], oev_tok[h - 1]]
            else:
                w0 = w0 + claim
            t_pta = None
            pp = self.p_sb[h % 2]
            for b in range(8):
                t_pta = P.mm(PTA[:, b * 128:(b + 1) * 128],
                             pp[:, b * 128:(b + 1) * 128], dg[:], True, True,
                             waits=w0 if b == 0 else (), inc=(b == 7))
            eng = "ac" if h % 2 == 0 else "dv"
            if eng == "ac":
                P.op("ac", lambda e: e.activation(
                    self.pt_sb[:, 0:1024], PTA, AF.Copy),
                    waits=[t_pta, av_tok[h - 1] if h >= 1 else None])
            else:
                P.op("dv", lambda e: e.tensor_copy(
                    self.pt_sb[:, 0:1024], PTA),
                    waits=[t_pta, av_tok[h - 1] if h >= 1 else None])
            pea_tok[h] = P.op(eng, lambda e: e.drain(), inc=eng)
            if h + 2 < 8:
                emit_S(h + 2)
            t_ptb = P.mm(SLACK[h % 2], pp[:, 8 * 128:9 * 128], dg[:],
                         True, True,
                         waits=[peb_tok[h - 2]] if h >= 2 else (), inc=True)
            pt_tok[h] = t_ptb
            self.diag_war[h % 2] = t_ptb
            if eng == "ac":
                P.op("ac", lambda e, h=h: e.activation(
                    self.pt_sb[:, 1024:1152], SLACK[h % 2], AF.Copy),
                    waits=[t_ptb, av_tok[h - 1] if h >= 1 else None])
            else:
                P.op("dv", lambda e, h=h: e.tensor_copy(
                    self.pt_sb[:, 1024:1152], SLACK[h % 2]),
                    waits=[t_ptb, av_tok[h - 1] if h >= 1 else None])
            peb_tok[h] = P.op(eng, lambda e: e.drain(), inc=eng)
            t_av = None
            for b in range(NBLK):
                slot = (i + b) % RING
                w_av = ()
                if b == 0:
                    w_av = [pea_tok[h]]
                elif b == NBLK - 1:
                    w_av = [peb_tok[h]]
                t_av = P.mm(AVR,
                            self.ring_v(phase, slot)[:, h * 128:(h + 1) * 128],
                            self.pt_sb[:, b * 128:(b + 1) * 128],
                            start=(b == 0), stop=(b == NBLK - 1),
                            waits=w_av, inc=(b == NBLK - 1))
            av_tok[h] = t_av
        oev_tok[7] = P.op("ac", lambda e, o=oT_off: e.activation(
            self.oT_sb[:, o + 7 * 128:o + 8 * 128], AVR, AF.Copy),
            waits=[av_tok[7], oT_war], drain=True, inc="ac")
        for b in (0, 1, 2):
            self.pa.tok[b] = [exp_tok[6], peb_tok[6]]
        for b in (3, 4, 5):
            self.pa.tok[b] = [exp_tok[7], peb_tok[7]]
        for b in (6, 7):
            self.pa.tok[b] = [oev_tok[7], peb_tok[7]]
        self.pa.ptr = 0
        return av_tok[7], oev_tok

    def build(self):
        P = self.P
        P.load(self.fence_sb[:], self.fin_d[:])
        P.load(self.id_sb[:], self.idd[:])
        P.load(self.idf_sb[:], self.idfd[:])
        P.load(self.mw_sb[:], self.mwd[:])
        P.load(self.mc_sb[:], self.mcd[:])
        P.load(self.np_sb[:], self.npd[:])
        P.load(self.c2_sb[:], self.c2d[:])
        P.load(self.s2_sb[:], self.s2d[:])
        self.p_free = None
        self.pt_free = None
        self.diag_war = [None, None]
        xu_free = [[], []]

        src = self.gAd.rearrange("(c p) w -> p c w", p=128)
        dst = self.gA_sb[:].rearrange("p (c w) -> p c w", c=NCH)
        P.op("st", lambda e, o=dst, i=src: e.dma_start(out=o, in_=i),
             inc="qst")
        P.op("st", lambda e: e.dma_start(out=self.bs_sb[:], in_=self.bsd[:]),
             inc="qst")
        self.wload(self.wkd, self.w1, ())
        t_w = self.wload(self.wvd, self.w2, ())
        self.t_gA = t_w
        w1_tok = t_w
        w2_tok = t_w

        x_ld = [None] * NT_EXT
        P.load(self.xu[0][:], self.xd[0])
        P.load(self.xu[1][:], self.xd[1])
        lf0 = P.lfence(self)
        self.t_id = self.t_mask = self.t_np = self.t_tab = lf0
        x_ld[0] = lf0
        x_ld[1] = lf0

        kv_stored = [None] * NT_EXT
        unT_stored = [None] * NT_OWN
        xn_free = None
        xnT_free = None
        un_free = None
        unT_free = [[], []]
        ln_war = [None, None]
        gate_war = [None, None]
        bufa_free = None
        kvo_free = []
        v_pend = None
        w1_last = None
        w2_last = None
        rope_prev = None

        for e in range(NT_EXT):
            sl = e % 2
            t_xn, _ = self.emit_ln(self.xu[sl], self.xn_sb, sl, x_ld[e],
                                   xn_free, ln_war[sl], sq_war=rope_prev)
            t_xnT_pe, t_xnT_ev = self.emit_transposes(
                self.xn_sb, self.xnT_sb, t_xn, xnT_free)
            xn_free = t_xnT_pe
            t_u, t_g = self.emit_ffn(self.xnT_sb, t_xnT_ev, self.xu[sl],
                                     x_ld[e], sl, gate_war[sl])
            xnT_free = t_g
            gate_war[sl] = t_u
            t_usf = None
            if e >= HALO:
                P.store(self.us_[e - HALO], self.xu[sl][:], waits=[t_u])
                t_usf = P.sfence(self)
            if v_pend is not None:
                ve, vsl, vtok = v_pend
                lhsv = lambda c, s=vsl: self.unT[s][:, c * 128:(c + 1) * 128]
                v_pe, t_vev = self.emit_proj(lhsv, self.w2, vtok, w2_tok,
                                             out_bf=self.kvo,
                                             out_war=kvo_free)
                w2_last = v_pe
                unT_free[vsl] = unT_free[vsl] + [v_pe]
                P.store(self.vss[ve], self.kvo[:], waits=[t_vev])
                kv_stored[ve] = P.sfence(self)
                kvo_free = [kv_stored[ve]]
                v_pend = None
            t_un, t_sq2 = self.emit_ln(self.xu[sl], self.un_sb, sl, t_u,
                                       un_free, ln_war[sl], sq_war=rope_prev)
            ln_war[sl] = t_un
            xu_free[sl] = [t_un, t_sq2] + ([t_usf] if t_usf else [])
            if e + 2 < NT_EXT:
                P.load(self.xu[sl][:], self.xd[e + 2], waits=[xu_free[sl]])
                x_ld[e + 2] = P.lfence(self)
            t_unT_pe, t_unT_ev = self.emit_transposes(
                self.un_sb, self.unT[sl], t_un, unT_free[sl])
            un_free = t_unT_pe
            if e >= HALO:
                unT_stored[e - HALO] = P.store(self.unTs[e - HALO],
                                               self.unT[sl][:], waits=[t_unT_ev])
            lhsk = lambda c, s=sl: self.unT[s][:, c * 128:(c + 1) * 128]
            k_pe, t_kev = self.emit_proj(lhsk, self.w1, t_unT_ev, w1_tok,
                                         out_bf=self.buf_a, out_war=bufa_free)
            w1_last = k_pe
            t_kr = self.emit_rope(self.buf_a, e, t_kev)
            rope_prev = t_kr
            t_kt_pe, t_ktev = self.emit_transposes(
                self.buf_a, self.kvo, t_kr, kvo_free)
            bufa_free = t_kt_pe
            P.store(self.kTs[e], self.kvo[:], waits=[t_ktev])
            t_ksf = P.sfence(self)
            kvo_free = [t_ksf]
            unT_free[sl] = [t_unT_pe, k_pe] + \
                ([t_ksf] if e >= HALO else [])
            v_pend = (e, sl, t_unT_ev)
        ve, vsl, vtok = v_pend
        lhsv = lambda c, s=vsl: self.unT[s][:, c * 128:(c + 1) * 128]
        v_pe, t_vev = self.emit_proj(lhsv, self.w2, vtok, w2_tok,
                                     out_bf=self.kvo, out_war=kvo_free)
        w2_last = v_pe
        unT_free[vsl] = unT_free[vsl] + [v_pe]
        P.store(self.vss[ve], self.kvo[:], waits=[t_vev])
        kv_stored[ve] = P.sfence(self)
        kvo_free = [kv_stored[ve]]

        w1_tok = self.wload(self.wqd, self.w1, [w1_last])

        ring_free = [None] * NT_EXT
        ring_f = [None] * NT_EXT
        for e2 in range(HALO):
            self._ring_load(1, e2, e2, [kv_stored[e2], w2_last])
        P.load(self.unT[0][:], self.unTs[0],
               waits=[kv_stored[NT_EXT - 1], unT_free[0]])
        P.load(self.unT[1][:], self.unTs[1], waits=[unT_free[1]])
        lfB1 = P.lfence(self)
        for e2 in range(HALO):
            ring_f[e2] = lfB1
        unTl_f = [None] * NT_OWN
        unTl_f[0] = unTl_f[1] = lfB1

        qT_src_tok = [None] * NT_OWN
        s_last = None
        qT_store = [None] * NT_OWN
        sf1 = [None] * NT_OWN
        attn_prev = None
        self.p_free = None
        self.pt_free = None

        q_pe_by_slot = [None, None]
        for i in range(NT_OWN):
            sl = i % 2
            lhsq = lambda c, s=sl: self.unT[s][:, c * 128:(c + 1) * 128]
            q_pe, t_qev = self.emit_proj(lhsq, self.w1, unTl_f[i], w1_tok,
                                         out_bf=self.buf_a, out_war=bufa_free,
                                         evac_scale=SCALE_Q)
            q_pe_by_slot[sl] = q_pe
            if i + 2 < NT_OWN:
                P.load(self.unT[sl][:], self.unTs[i + 2], waits=[q_pe])
            if i + HALO < NT_EXT:
                self._ring_load(1, i + HALO, (i + HALO) % RING,
                                [ring_free[i - 2] if i >= 2 else None])
            lf1 = P.lfence(self)
            if i + 2 < NT_OWN:
                unTl_f[i + 2] = lf1
            if i + HALO < NT_EXT:
                ring_f[i + HALO] = lf1
            if attn_prev is not None:
                ip = attn_prev
                t_avp, evsp = self.emit_attn_tile(
                    1, ip, self.kvo[:, 0:1024], qT_src_tok[ip],
                    ring_f[ip + HALO], 0, sf1[ip] if ip >= 1 else None)
                ring_free[ip] = t_avp
                P.store(self.oTs[ip], self.oT_sb[:, 0:1024], waits=[evsp[-1]])
                s_last = t_avp
            t_qr = self.emit_rope(self.buf_a, i + HALO, t_qev)
            t_qt_pe, t_qtev = self.emit_transposes(
                self.buf_a, self.kvo, t_qr,
                [s_last, sf1[i - 1] if i >= 1 else kv_stored[NT_EXT - 1]])
            bufa_free = t_qt_pe
            qT_src_tok[i] = t_qtev
            qT_store[i] = P.store(self.qTs[i], self.kvo[:, 1024:2048],
                                  waits=[t_qtev])
            sf1[i] = P.sfence(self)
            attn_prev = i
        ip = attn_prev
        t_avp, evsp = self.emit_attn_tile(1, ip, self.kvo[:, 0:1024],
                                          qT_src_tok[ip], ring_f[ip + HALO],
                                          0, sf1[ip])
        ring_free[ip] = t_avp
        P.store(self.oTs[ip], self.oT_sb[:, 0:1024], waits=[evsp[-1]])
        sf1_final = P.sfence(self)
        b1_ring_last = t_avp
        b1_q_last = q_pe_by_slot[1]

        w2_tok = self.wload(self.wod, self.w2, [b1_ring_last])

        ring_free2 = [None] * NT_EXT
        ring_f2 = [None] * NT_EXT
        for e2 in range(HALO + 1):
            self._ring_load(2, e2, e2, [kv_stored[e2], b1_q_last])
        u_ld = [None] * NT_OWN
        qT_ld = [None] * NT_OWN
        oT_ld = [None] * NT_OWN
        y_stored = [None] * NT_OWN
        P.load(self.xu[0][:], self.us_[0], waits=[xu_free[0], sf1_final])
        P.load(self.xu[1][:], self.us_[1], waits=[xu_free[1]])
        P.load(self.un_sb[:, 0:1024], self.qTs[0], waits=[un_free])
        P.load(self.un_sb[:, 1024:2048], self.qTs[1])
        P.load(self.unT[0][:, 0:1024], self.oTs[0], waits=[q_pe_by_slot[0]])
        P.load(self.unT[1][:, 0:1024], self.oTs[1], waits=[q_pe_by_slot[1]])
        lfB2 = P.lfence(self)
        for e2 in range(HALO + 1):
            ring_f2[e2] = lfB2
        u_ld[0] = u_ld[1] = lfB2
        qT_ld[0] = qT_ld[1] = lfB2
        oT_ld[0] = oT_ld[1] = lfB2

        self.p_free = None
        self.pt_free = None
        oT_war2 = None
        ffn_pend = None
        ln_war2 = [None, None]
        gate_war2 = [None, None]
        self.xn_free2 = None
        self.xnT_free2 = None

        for i in range(NT_OWN):
            sl = i % 2
            if i + 1 + HALO < NT_EXT:
                self._ring_load(2, i + 1 + HALO, (i + 1 + HALO) % RING,
                                [ring_free2[i - 1] if i >= 1 else None])
            if ffn_pend is not None:
                ip2, slp, t_anp = ffn_pend
                t_yf = self._b2_tail(ip2, slp, t_anp, gate_war2)
                y_stored[ip2] = t_yf
                xu_free[slp] = [t_yf]
                if ip2 + 2 < NT_OWN:
                    P.load(self.xu[slp][:], self.us_[ip2 + 2], waits=[t_yf])
                ffn_pend = None
            t_av2, evs2 = self.emit_attn_tile(
                2, i, self.un_sb[:, sl * 1024:(sl + 1) * 1024], qT_ld[i],
                ring_f2[i + HALO], 1024, oT_war2)
            ring_free2[i] = t_av2
            if i + 2 < NT_OWN:
                P.load(self.un_sb[:, sl * 1024:(sl + 1) * 1024],
                       self.qTs[i + 2], waits=[t_av2])
            def lhso(c, s=sl):
                if c < 8:
                    return self.unT[s][:, c * 128:(c + 1) * 128]
                return self.oT_sb[:, 1024 + (c - 8) * 128:1024 + (c - 7) * 128]
            o_pe, t_a = self.emit_proj(lhso, self.w2,
                                       [oT_ld[i], evs2[-1]], w2_tok,
                                       x_dst=self.xu[sl], x_tok=u_ld[i])
            oT_war2 = o_pe
            if i + 2 < NT_OWN:
                P.load(self.unT[sl][:, 0:1024], self.oTs[i + 2], waits=[o_pe])
            lf2 = P.lfence(self)
            if i + 2 < NT_OWN:
                qT_ld[i + 2] = lf2
                oT_ld[i + 2] = lf2
            if i + 1 < NT_OWN:
                u_ld[i + 1] = lf2 if u_ld[i + 1] is None else u_ld[i + 1]
            if i + 1 + HALO < NT_EXT:
                ring_f2[i + 1 + HALO] = lf2
            t_an, _ = self.emit_ln(self.xu[sl], self.xn_sb, sl, t_a,
                                   self.xn_free2, ln_war2[sl])
            ln_war2[sl] = t_an
            ffn_pend = (i, sl, t_an)
        ip2, slp, t_anp = ffn_pend
        y_stored[ip2] = self._b2_tail(ip2, slp, t_anp, gate_war2)

        total = P.cnt["qst"]
        P.op("st", lambda e: e.nop(), waits=[("qst", total)])

        nc = self.nc
        with ExitStack() as bctx:
            sems = {}
            for s in Prog.SEMS:
                sems[s] = bctx.enter_context(nc.semaphore(s))
            blk = bctx.enter_context(nc.Block())

            @blk.tensor
            def _(e):
                P.replay("pe", e, sems)

            @blk.vector
            def _(e):
                P.replay("dv", e, sems)

            @blk.scalar
            def _(e):
                P.replay("ac", e, sems)

            @blk.sync
            def _(e):
                P.replay("ld", e, sems)

            @blk.gpsimd
            def _(e):
                P.replay("st", e, sems)
        self.ctx.close()
        return nc

    def _b2_tail(self, i, sl, t_an, gate_war2):
        P = self.P
        t_anT_pe, t_anT_ev = self.emit_transposes(self.xn_sb, self.xnT_sb,
                                                  t_an, self.xnT_free2)
        self.xn_free2 = t_anT_pe
        t_y, t_g2 = self.emit_ffn(self.xnT_sb, t_anT_ev, self.xu[sl],
                                  t_an, sl, gate_war2[sl])
        self.xnT_free2 = t_g2
        gate_war2[sl] = t_y
        P.store(self.yd[i], self.xu[sl][:], waits=[t_y])
        return P.sfence(self)


def kernel(L0_attn_hidden_states, L0_attn_wq, L0_attn_wk, L0_attn_wv,
           L0_attn_wo, L0_ffn_gate_w, L0_ffn_lora_A, L0_ffn_lora_B):
    x = np.asarray(L0_attn_hidden_states, np.float32)
    in_maps = _host_prepare(x, L0_attn_wq, L0_attn_wk, L0_attn_wv, L0_attn_wo,
                            L0_ffn_gate_w, L0_ffn_lora_A, L0_ffn_lora_B)
    k = Kern()
    nc = k.build()
    res = run_bass_kernel_spmd(nc, in_maps, core_ids=list(range(8)))
    out = np.empty((BSZ, SEQ, D), np.float32)
    for c in range(8):
        b, half = c // 2, c % 2
        y = np.asarray(res.results[c]["y"], np.float32).reshape(2048, D)
        out[b, half * 2048:(half + 1) * 2048, :] = y
    return out


# version 06
# speedup vs baseline: 1.1730x; 1.0629x over previous
import numpy as np
import ml_dtypes
import concourse.bass as bass
import concourse.mybir as mybir
from concourse.bass_utils import run_bass_kernel_spmd
from concourse.ap import AP
from contextlib import ExitStack

F32 = mybir.dt.float32
BF16 = mybir.dt.bfloat16
AX = mybir.AxisListType.X
ALU = mybir.AluOpType
AF = mybir.ActivationFunctionType

BSZ, SEQ, D = 4, 4096, 2048
NH, HD, WIN = 16, 128, 1024
NT_OWN, HALO = 16, 8
NT_EXT = NT_OWN + HALO
RING = 10
NBLK = 9
NCH = 16
SCALE_Q = float(128.0 ** -0.5)
NEG = -1.0e30
ER = 128
GAW = 8 + ER

DEBUG = False


def bf(x):
    return np.asarray(x, dtype=ml_dtypes.bfloat16)



PARANOID = False


class Prog:
    SEMS = ("qls", "qst", "qlf", "qsf", "pe", "dv", "ac", "po")
    DMA_SEMS = ("qls", "qst", "qlf", "qsf")
    OWNER = {"pe": "pe", "dv": "dv", "ac": "ac", "po": "st", "qls": None,
             "qst": None, "qlf": None, "qsf": None}

    def __init__(self):
        self.stream = {k: [] for k in ("pe", "dv", "ac", "ld", "st")}
        self.cnt = {s: 0 for s in self.SEMS}

    def op(self, eng, fn, waits=(), inc=None, drain=False):
        w = []
        for x in waits:
            if x is None:
                continue
            if isinstance(x, list):
                w += [t for t in x if t is not None]
            else:
                w.append(x)
        if PARANOID:
            w += [(s, self.cnt[s]) for s in ("pe", "dv", "ac")]
        tok = None
        amt = 0
        if inc is not None:
            amt = 16 if inc in self.DMA_SEMS else 1
            self.cnt[inc] += amt
            tok = (inc, self.cnt[inc])
        self.stream[eng].append((fn, w, inc, amt, drain))
        return tok

    def load(self, out, in_, waits=()):
        return self.op("ld", lambda e, o=out, i=in_: e.dma_start(out=o, in_=i),
                       waits=waits, inc="qls")

    def store(self, out, in_, waits=()):
        return self.op("st", lambda e, o=out, i=in_: e.dma_start(out=o, in_=i),
                       waits=waits, inc="qst")

    def lfence(self, kern):
        return self.op("ld", lambda e: e.dma_start(out=kern.fence_sb[:],
                                                   in_=kern.fin_d[:]),
                       inc="qlf")

    def sfence(self, kern):
        return self.op("st", lambda e: e.dma_start(out=kern.fout_d[:],
                                                   in_=kern.fence_sb[:]),
                       inc="qsf")

    def mm(self, out, lhsT, rhs, start, stop, waits=(), inc=False):
        return self.op(
            "pe",
            lambda e, o=out, l=lhsT, r=rhs, s0=start, s1=stop: e.matmul(
                o, l, r, start=s0, stop=s1),
            waits=waits, inc="pe" if inc else None)

    def replay(self, ename, eng, sems):
        seen = {}
        for fn, waits, inc, amt, drain in self.stream[ename]:
            best = {}
            for (s, v) in waits:
                if Prog.OWNER[s] == ename:
                    continue
                if best.get(s, 0) < v:
                    best[s] = v
            for s, v in best.items():
                if seen.get(s, 0) >= v:
                    continue
                eng.wait_ge(sems[s], v)
                seen[s] = v
            inst = fn(eng)
            if drain:
                inst = eng.drain()
            if inc is not None:
                inst.then_inc(sems[inc], amt)


class PsumBlock:
    def __init__(self, alloc, banks, waits):
        self.a = alloc
        self.banks = banks
        self.waits = waits
        self.c0 = banks[0] * 512

    def ap(self, col0, col1):
        return self.a.ps[:, self.c0 + col0:self.c0 + col1]

    def free(self, *toks):
        toks = [t for t in toks if t is not None]
        for b in self.banks:
            self.a.tok[b] = toks


class PsumAlloc:
    def __init__(self, ps):
        self.ps = ps
        self.tok = [[] for _ in range(8)]
        self.ptr = 0

    def alloc(self, nbanks):
        if self.ptr + nbanks > 8:
            self.ptr = 0
        b0 = self.ptr
        self.ptr = (self.ptr + nbanks) % 8
        banks = list(range(b0, b0 + nbanks))
        waits = []
        for b in banks:
            waits += self.tok[b]
        return PsumBlock(self, banks, waits)



def _rope_tables(positions):
    ntiles = len(positions) // 128
    inv = 1.0 / (10000.0 ** (np.arange(0, HD, 2, dtype=np.float64) / HD))
    pos = np.asarray(positions, np.float64)
    fr = pos[:, None] * inv[None, :]
    c = np.cos(fr).reshape(ntiles, 128, 64).transpose(1, 0, 2).reshape(
        128, ntiles * 64)
    s = (-np.sin(fr)).reshape(ntiles, 128, 64).transpose(1, 0, 2).reshape(
        128, ntiles * 64)
    return bf(c), bf(s)


def _host_prepare(x, wq, wk, wv, wo, gate_w, lora_A, lora_B):
    qq = np.arange(128)[:, None]
    kk = np.arange(128)[None, :]
    m_w = bf(np.where(kk > qq, 0.0, NEG))
    m_c = bf(np.where(kk <= qq, 0.0, NEG))
    ident = bf(np.eye(128))
    gA = np.ascontiguousarray(
        np.concatenate([np.asarray(gate_w).T.astype(np.float32, copy=False),
                        np.asarray(lora_A).reshape(ER, D).T.astype(
                            np.float32, copy=False)], axis=1))
    bstack = np.ascontiguousarray(
        np.asarray(lora_B).transpose(0, 2, 1).reshape(ER, D).astype(
            np.float32, copy=False))
    shared = dict(wq=np.asarray(wq, np.float32), wk=np.asarray(wk, np.float32),
                  wv=np.asarray(wv, np.float32), wo=np.asarray(wo, np.float32),
                  gA=gA, bstack=bstack, ident=ident, m_w=m_w, m_c=m_c,
                  ident_f32=np.eye(128, dtype=np.float32))
    x = np.asarray(x, np.float32)
    in_maps = []
    for b in range(BSZ):
        for half in range(2):
            if half == 0:
                xe = np.concatenate(
                    [np.zeros((HALO * 128, D), np.float32), x[b, 0:2048]], axis=0)
                ext_pos = np.concatenate([np.zeros(HALO * 128), np.arange(0, 2048)])
            else:
                xe = np.ascontiguousarray(x[b, 1024:4096])
                ext_pos = np.arange(1024, 4096)
            c2h, s2h = _rope_tables(ext_pos)
            npad = np.zeros((128, NT_OWN), np.float32)
            if half == 0:
                for i in range(NT_OWN):
                    npb = max(0, HALO - i)
                    if npb == 0:
                        continue
                    cnt = np.full(128, 128.0 * (npb - 1))
                    cnt += 127.0 - np.arange(128)
                    npad[:, i] = cnt
            m = dict(shared)
            m.update(x_ext=np.ascontiguousarray(xe.reshape(NT_EXT, 128, D)),
                     c2=c2h, s2=s2h, npad=npad,
                     fence_in=np.zeros((128, 16), np.float32))
            in_maps.append(m)
    return in_maps



class Kern:
    def __init__(self):
        self.nc = bass.Bass("TRN2", target_bir_lowering=False)
        self.ctx = ExitStack()
        self.P = Prog()
        self._declare()
        self.pa = PsumAlloc(self.ps_all)

    def _declare(self):
        nc = self.nc
        dt = lambda n, s, d, k: nc.dram_tensor(n, s, d, kind=k).ap()
        self.xd = dt("x_ext", [NT_EXT, 128, D], F32, "ExternalInput")
        self.wqd = dt("wq", [D, D], F32, "ExternalInput")
        self.wkd = dt("wk", [D, D], F32, "ExternalInput")
        self.wvd = dt("wv", [D, D], F32, "ExternalInput")
        self.wod = dt("wo", [D, D], F32, "ExternalInput")
        self.gAd = dt("gA", [D, GAW], F32, "ExternalInput")
        self.bsd = dt("bstack", [ER, D], F32, "ExternalInput")
        self.c2d = dt("c2", [128, NT_EXT * 64], BF16, "ExternalInput")
        self.s2d = dt("s2", [128, NT_EXT * 64], BF16, "ExternalInput")
        self.npd = dt("npad", [128, NT_OWN], F32, "ExternalInput")
        self.idd = dt("ident", [128, 128], BF16, "ExternalInput")
        self.mwd = dt("m_w", [128, 128], BF16, "ExternalInput")
        self.mcd = dt("m_c", [128, 128], BF16, "ExternalInput")
        self.fin_d = dt("fence_in", [128, 16], F32, "ExternalInput")
        self.idfd = dt("ident_f32", [128, 128], F32, "ExternalInput")
        self.fout_d = dt("fence_out", [128, 16], F32, "Internal")
        self.yd = dt("y", [NT_OWN, 128, D], F32, "ExternalOutput")
        SK = "ExternalOutput" if DEBUG else "Internal"
        self.kTs = dt("kT_s", [NT_EXT, 128, D], BF16, SK)
        self.vss = dt("v_s", [NT_EXT, 128, D], BF16, SK)
        self.us_ = dt("u_s", [NT_OWN, 128, D], F32, SK)
        self.unTs = dt("unT_s", [NT_OWN, 128, D], BF16, SK)
        self.qTs = dt("qT_s", [NT_OWN, 128, 1024], BF16, SK)
        self.oTs = dt("oT_s", [NT_OWN, 128, 1024], BF16, SK)

        sb = lambda n, s, d: self.ctx.enter_context(nc.sbuf_tensor(n, s, d))
        self.fence_sb = sb("fence_sb", [128, 16], F32)
        self.id_sb = sb("id_sb", [128, 128], BF16)
        self.idf_sb = sb("idf_sb", [128, 128], F32)
        self.diag = [sb("diag0", [128, 128], BF16), sb("diag1", [128, 128], BF16)]
        self.mw_sb = sb("mw_sb", [128, 128], BF16)
        self.mc_sb = sb("mc_sb", [128, 128], BF16)
        self.np_sb = sb("np_sb", [128, NT_OWN], F32)
        self.c2_sb = sb("c2_sb", [128, NT_EXT * 64], BF16)
        self.s2_sb = sb("s2_sb", [128, NT_EXT * 64], BF16)
        self.gA_sb = sb("gA_sb", [128, NCH * GAW], BF16)
        self.bs_sb = sb("bs_sb", [128, D], BF16)
        self.w1 = sb("w1", [128, NCH * D], BF16)
        self.w2 = sb("w2", [128, NCH * D], BF16)
        self.xu = [sb("xu0", [128, D], F32), sb("xu1", [128, D], F32)]
        self.xn_sb = sb("xn_sb", [128, D], BF16)
        self.xnT_sb = sb("xnT_sb", [128, D], BF16)
        self.un_sb = sb("un_sb", [128, D], BF16)
        self.unT = [sb("unT0", [128, D], BF16), sb("unT1", [128, D], BF16)]
        self.buf_a = sb("buf_a", [128, D], BF16)
        self.buf_b = sb("buf_b", [128, D], BF16)
        self.kvo = sb("kvo", [128, D], BF16)
        self.p_sb = [sb("p_sb0", [128, NBLK * 128], BF16),
                     sb("p_sb1", [128, NBLK * 128], BF16)]
        self.pt_sb = sb("pt_sb", [128, NBLK * 128], BF16)
        self.oT_sb = sb("oT_sb", [128, D], BF16)
        mk2 = lambda n, w: [sb(n + "0", [128, w], F32), sb(n + "1", [128, w], F32)]
        self.st1 = mk2("st1", 1)
        self.ssq = mk2("ssq", 1)
        self.mean = mk2("mean", 1)
        self.nmean = mk2("nmean", 1)
        self.msq = mk2("msq", 1)
        self.var = mk2("var", 1)
        self.lnv = mk2("lnv", 1)
        self.rstd = mk2("rstd", 1)
        self.g8a = mk2("g8a", 8)
        self.g8b = mk2("g8b", 8)
        self.gm1 = mk2("gm1", 1)
        self.gm2 = mk2("gm2", 1)
        self.gdl = mk2("gdl", 1)
        self.ge = mk2("ge", 1)
        self.gden = mk2("gden", 1)
        self.grcp = mk2("grcp", 1)
        self.comb8 = mk2("comb8", 8)
        self.comb_er = [sb("comber", [128, ER], F32)] * 2
        self.ch_sb = [sb("ch0", [128, ER], BF16), sb("ch1", [128, ER], BF16)]
        self.chT_sb = [sb("chT0", [128, 128], BF16), sb("chT1", [128, 128], BF16)]
        self.nm_all = sb("nm_all", [128, 8], F32)
        self.sum_all = sb("sum_all", [128, 8], F32)
        self.enm = sb("enm", [128, 1], F32)
        self.corr = sb("corr", [128, 1], F32)
        self.rcp_s = sb("rcp_s", [128, 1], F32)
        self.ps_all = self.ctx.enter_context(
            nc.psum_tensor("ps_all", [128, 4096], F32))

    def ring_k(self, phase, slot):
        a = self.w2 if phase == 1 else self.w1
        return a[:, slot * 1024:(slot + 1) * 1024]

    def ring_v(self, phase, slot):
        a = self.w2 if phase == 1 else self.w1
        return a[:, (RING + slot) * 1024:(RING + slot + 1) * 1024]

    def _ring_load(self, phase, e, slot, waits):
        P = self.P
        cols = slice(0, 1024) if phase == 1 else slice(1024, 2048)
        P.load(self.ring_k(phase, slot), self.kTs[e][:, cols], waits=waits)
        return P.load(self.ring_v(phase, slot), self.vss[e][:, cols], waits=())

    def wload(self, wdram, dest, waits):
        P = self.P
        for g in range(4):
            src = wdram[g * 512:(g + 1) * 512, :].rearrange(
                "(c p) d -> p c d", p=128)
            dst = dest[:, g * 4 * D:(g + 1) * 4 * D].rearrange(
                "p (c d) -> p c d", c=4)
            P.op("st", lambda e, o=dst, i=src: e.dma_start(out=o, in_=i),
                 waits=waits, inc="qst")
        return P.sfence(self)

    def emit_ln(self, x_src, out_bf, par, ld_tok, out_war, stats_war,
                sq_war=None):
        P = self.P
        s = lambda b: b[par]
        t_s1 = P.op("dv", lambda e: e.reduce_sum(s(self.st1)[:], x_src[:],
                                                 axis=AX),
                    waits=[ld_tok, stats_war], drain=True, inc="dv")
        t_sq = P.op("ac", lambda e: e.activation(
            self.buf_b[:], x_src[:], AF.Square, accum_out=s(self.ssq)[:, 0:1]),
            waits=[ld_tok, stats_war, sq_war], drain=True, inc="ac")
        P.op("st", lambda e: e.tensor_scalar(
            s(self.mean)[:], s(self.st1)[:], 1.0 / D, None, ALU.mult),
            waits=[t_s1, stats_war])
        P.op("st", lambda e: e.tensor_scalar(
            s(self.nmean)[:], s(self.mean)[:], -1.0, None, ALU.mult))
        P.op("st", lambda e: e.tensor_mul(
            s(self.msq)[:], s(self.mean)[:], s(self.mean)[:]))
        P.op("st", lambda e: e.tensor_scalar(
            s(self.var)[:], s(self.ssq)[:], 1.0 / D, 1e-5, ALU.mult, ALU.add),
            waits=[t_sq])
        t_po = P.op("st", lambda e: e.tensor_sub(
            s(self.var)[:], s(self.var)[:], s(self.msq)[:]), inc="po")
        P.op("ac", lambda e: e.activation(
            s(self.lnv)[:], s(self.var)[:, 0:1], AF.Ln),
            waits=[t_po], drain=True)
        t_r = P.op("ac", lambda e: e.activation(
            s(self.rstd)[:], s(self.lnv)[:, 0:1], AF.Exp, scale=-0.5),
            drain=True, inc="ac")
        t_ap = P.op("dv", lambda e: e.tensor_scalar(
            out_bf[:], x_src[:], s(self.nmean)[:, 0:1], s(self.rstd)[:, 0:1],
            ALU.add, ALU.mult), waits=[t_r, t_po, out_war], drain=True, inc="dv")
        return t_ap, t_sq

    def emit_transposes(self, src_bf, dst_bf, src_tok, dst_war):
        P = self.P
        pe_last = None
        blks = []
        for half in range(2):
            blk = self.pa.alloc(2)
            t_mm = None
            for c in range(8):
                cc = half * 8 + c
                t_mm = P.mm(blk.ap(c * 128, (c + 1) * 128),
                            src_bf[:, cc * 128:(cc + 1) * 128], self.id_sb[:],
                            True, True,
                            waits=([src_tok, self.t_id] + blk.waits)
                            if c == 0 else (),
                            inc=(c == 7))
            blks.append((blk, t_mm))
            pe_last = t_mm
        for half, (blk, t_mm) in enumerate(blks):
            P.op("ac", lambda e, h=half, b=blk: e.activation(
                dst_bf[:, h * 1024:(h + 1) * 1024], b.ap(0, 1024), AF.Copy),
                waits=[t_mm, dst_war])
        ev = P.op("ac", lambda e: e.drain(), inc="ac")
        for blk, _ in blks:
            blk.free(ev)
        return pe_last, ev

    def emit_proj(self, lhs_fn, rhs_sb, lhs_tok, rhs_tok, out_bf=None,
                  out_war=None, x_dst=None, x_tok=None, evac_scale=None):
        P = self.P
        pe_last = None
        blks = []
        for half in range(2):
            blk = self.pa.alloc(2)
            t_mm = None
            for c in range(NCH):
                for s2 in range(2):
                    col = half * 1024 + s2 * 512
                    t_mm = P.mm(
                        blk.ap(s2 * 512, (s2 + 1) * 512), lhs_fn(c),
                        rhs_sb[:, c * D + col:c * D + col + 512],
                        start=(c == 0), stop=(c == NCH - 1),
                        waits=([lhs_tok, rhs_tok] + blk.waits)
                        if (c == 0 and s2 == 0) else (),
                        inc=(c == NCH - 1 and s2 == 1))
            blks.append((blk, t_mm))
            pe_last = t_mm
        eng = "dv" if x_dst is not None else "ac"
        for half, (blk, t_mm) in enumerate(blks):
            if x_dst is not None:
                P.op("dv", lambda e, h=half, b=blk: e.tensor_add(
                    x_dst[:, h * 1024:(h + 1) * 1024],
                    x_dst[:, h * 1024:(h + 1) * 1024], b.ap(0, 1024)),
                    waits=[t_mm, x_tok])
            else:
                kw = {} if evac_scale is None else {"scale": evac_scale}
                P.op("ac", lambda e, h=half, b=blk, kw=kw: e.activation(
                    out_bf[:, h * 1024:(h + 1) * 1024], b.ap(0, 1024),
                    AF.Copy, **kw),
                    waits=[t_mm, out_war])
        out_tok = P.op(eng, lambda e: e.drain(), inc=eng)
        for blk, _ in blks:
            blk.free(out_tok)
        return pe_last, out_tok

    def emit_rope(self, src_bf, tile_e, src_tok):
        P = self.P
        off = tile_e * 64
        csl = self.c2_sb[:, off:off + 64]
        ssl = self.s2_sb[:, off:off + 64]
        cbc = AP(csl.tensor, csl.offset,
                 [list(csl.ap[0]), [0, NH], [0, 2], [1, 64]])
        sbc = AP(ssl.tensor, ssl.offset,
                 [list(ssl.ap[0]), [0, NH], [0, 2], [1, 64]])
        sv = src_bf[:]
        rot = AP(sv.tensor, sv.offset + 64,
                 [list(sv.ap[0]), [128, NH], [-64, 2], [1, 64]])
        t4 = lambda t: t[:].rearrange("p (h x j) -> p h x j", h=NH, x=2)
        P.op("dv", lambda e: e.tensor_tensor(
            t4(self.buf_b), t4(src_bf), cbc, ALU.mult),
            waits=[src_tok, self.t_tab])
        P.op("dv", lambda e: e.tensor_tensor(t4(self.xn_sb), rot, sbc, ALU.mult))
        fh = lambda t: AP(t[:].tensor, t[:].offset,
                          [list(t[:].ap[0]), [128, NH], [1, 64]])
        sh = lambda t: AP(t[:].tensor, t[:].offset + 64,
                          [list(t[:].ap[0]), [128, NH], [1, 64]])
        P.op("dv", lambda e: e.tensor_tensor(
            fh(src_bf), fh(self.buf_b), fh(self.xn_sb), ALU.add))
        return P.op("dv", lambda e: e.tensor_tensor(
            sh(src_bf), sh(self.buf_b), sh(self.xn_sb), ALU.subtract),
            drain=True, inc="dv")

    def emit_ffn(self, actT_sb, actT_tok, x_dst, x_tok, par, war):
        P = self.P
        s = lambda b: b[par]
        gblk = self.pa.alloc(1)
        t_g = None
        for c in range(NCH):
            t_g = P.mm(gblk.ap(0, GAW), actT_sb[:, c * 128:(c + 1) * 128],
                       self.gA_sb[:, c * GAW:(c + 1) * GAW],
                       start=(c == 0), stop=(c == NCH - 1),
                       waits=([actT_tok, self.t_gA] + gblk.waits)
                       if c == 0 else (),
                       inc=(c == NCH - 1))
        lg = gblk.ap(0, 8)
        hh = gblk.ap(8, GAW)
        P.op("dv", lambda e: e.reduce_max(s(self.gm1)[:], lg, axis=AX),
             waits=[t_g, war], drain=True)
        P.op("dv", lambda e: e.tensor_single_scalar(
            s(self.g8a)[:], lg, s(self.gm1)[:, 0:1], ALU.is_ge), drain=True)
        P.op("dv", lambda e: e.scalar_tensor_tensor(
            s(self.g8b)[:], s(self.g8a)[:], NEG, lg, ALU.mult, ALU.add),
            drain=True)
        P.op("dv", lambda e: e.reduce_max(s(self.gm2)[:], s(self.g8b)[:], axis=AX),
             drain=True)
        P.op("dv", lambda e: e.tensor_single_scalar(
            s(self.g8b)[:], s(self.g8b)[:], s(self.gm2)[:, 0:1], ALU.is_ge),
            drain=True)
        t_d = P.op("dv", lambda e: e.tensor_sub(
            s(self.gdl)[:], s(self.gm2)[:], s(self.gm1)[:]), drain=True, inc="dv")
        t_e = P.op("ac", lambda e: e.activation(
            s(self.ge)[:], s(self.gdl)[:, 0:1], AF.Exp),
            waits=[t_d], drain=True, inc="ac")
        P.op("dv", lambda e: e.tensor_scalar(
            s(self.gden)[:], s(self.ge)[:], 1.0, None, ALU.add),
            waits=[t_e], drain=True)
        P.op("dv", lambda e: e.reciprocal(s(self.grcp)[:], s(self.gden)[:]),
             drain=True)
        P.op("dv", lambda e: e.scalar_tensor_tensor(
            s(self.comb8)[:], s(self.g8b)[:], s(self.ge)[:, 0:1], s(self.g8a)[:],
            ALU.mult, ALU.add), drain=True)
        P.op("dv", lambda e: e.tensor_single_scalar(
            s(self.comb8)[:], s(self.comb8)[:], s(self.grcp)[:, 0:1], ALU.mult),
            drain=True)
        c8 = s(self.comb8)[:]
        cbc = AP(c8.tensor, c8.offset, [list(c8.ap[0]), [1, 8], [0, 16]])
        P.op("dv", lambda e: e.tensor_copy(
            s(self.comb_er)[:].rearrange("p (x r) -> p x r", x=8), cbc),
            drain=True)
        t_ch = P.op("dv", lambda e: e.tensor_mul(
            s(self.ch_sb)[:], hh, s(self.comb_er)[:]), drain=True, inc="dv")
        gblk.free(t_ch)
        cblk = self.pa.alloc(1)
        t_ct = P.mm(cblk.ap(0, 128), s(self.ch_sb)[:], self.id_sb[:], True, True,
                    waits=[t_ch] + cblk.waits, inc=True)
        t_ce = P.op("dv", lambda e, b=cblk: e.tensor_copy(
            s(self.chT_sb)[:], b.ap(0, 128)), waits=[t_ct], drain=True, inc="dv")
        cblk.free(t_ce)
        blks = []
        for half in range(2):
            blk = self.pa.alloc(2)
            t_mm = None
            for s2 in range(2):
                col = half * 1024 + s2 * 512
                t_mm = P.mm(blk.ap(s2 * 512, (s2 + 1) * 512), s(self.chT_sb)[:],
                            self.bs_sb[:, col:col + 512], True, True,
                            waits=([t_ce] + blk.waits) if s2 == 0 else (),
                            inc=(s2 == 1))
            blks.append((blk, t_mm))
        for half, (blk, t_mm) in enumerate(blks):
            P.op("dv", lambda e, h=half, b=blk: e.tensor_add(
                x_dst[:, h * 1024:(h + 1) * 1024],
                x_dst[:, h * 1024:(h + 1) * 1024], b.ap(0, 1024)),
                waits=[t_mm, x_tok])
        t_add = P.op("dv", lambda e: e.drain(), inc="dv")
        for blk, _ in blks:
            blk.free(t_add)
        return t_add, t_g

    def emit_attn_tile(self, phase, i, qsrc, q_tok, ring_tok, oT_off, oT_war):
        P = self.P
        ps = self.ps_all
        SREG = (ps[:, 0:1152], ps[:, 1536:2688])
        SLACK = (ps[:, 1216:1344], ps[:, 2752:2880])
        PTA = ps[:, 3072:4096]
        AVR = ps[:, 3072:3200]
        claim = []
        for b in range(8):
            claim += self.pa.tok[b]
            self.pa.tok[b] = []
        s_tok = [None] * 8
        exp_tok = [None] * 8
        oev_tok = [None] * 8
        pt_tok = [None] * 8
        av_tok = [None] * 8
        peb_tok = [None] * 8
        diag_tok = [None] * 8

        def emit_S(h):
            reg = SREG[h % 2]
            w = [q_tok, ring_tok, self.t_mask, self.t_id, self.t_np]
            if h >= 2:
                w += [exp_tok[h - 2]]
                if h >= 4:
                    w += [peb_tok[h - 4]]
            else:
                w = w + claim
            P.mm(reg[:, 0:128], self.id_sb[:], self.mw_sb[:], True, False,
                 waits=w)
            P.mm(reg[:, 8 * 128:9 * 128], self.id_sb[:], self.mc_sb[:],
                 True, False)
            t = None
            for b in range(NBLK):
                slot = (i + b) % RING
                t = P.mm(reg[:, b * 128:(b + 1) * 128],
                         qsrc[:, h * 128:(h + 1) * 128],
                         self.ring_k(phase, slot)[:, h * 128:(h + 1) * 128],
                         start=(b not in (0, 8)), stop=True,
                         inc=(b == NBLK - 1))
            s_tok[h] = t

        emit_S(0)
        emit_S(1)
        pea_tok = [None] * 8

        def stage_A(h):
            reg = SREG[h % 2]
            t_nm = P.op("dv", lambda e, h=h, r=reg: e.reduce_max(
                self.nm_all[:, h:h + 1], r, axis=AX, negate=True),
                waits=[s_tok[h]], drain=True, inc="dv")
            P.op("ac", lambda e, h=h, r=reg: e.activation(
                self.p_sb[h % 2][:], r, AF.Exp, bias=self.nm_all[:, h:h + 1],
                accum_out=self.sum_all[:, h:h + 1]),
                waits=[t_nm, pt_tok[h - 2] if h >= 2 else None])
            t_exp = P.op("ac", lambda e, h=h: e.activation(
                self.enm[:], self.nm_all[:, h:h + 1], AF.Exp),
                drain=True, inc="ac")
            exp_tok[h] = t_exp
            P.op("st", lambda e, i=i: e.tensor_mul(
                self.corr[:], self.np_sb[:, i:i + 1], self.enm[:]),
                waits=[t_exp])
            t_corr = P.op("st", lambda e, h=h: e.tensor_sub(
                self.corr[:], self.sum_all[:, h:h + 1], self.corr[:]),
                inc="po")
            t_rcp = P.op("dv", lambda e: e.reciprocal(
                self.rcp_s[:], self.corr[:]),
                waits=[t_corr], drain=True, inc="dv")
            rb = self.rcp_s[:]
            rbc = AP(rb.tensor, rb.offset, [list(rb.ap[0]), [0, 128]])
            dg = self.diag[h % 2]
            diag_tok[h] = P.op("st", lambda e, d=dg, rbc=rbc: e.tensor_tensor(
                d[:], self.idf_sb[:], rbc, ALU.mult),
                waits=[t_rcp, self.diag_war[h % 2]], inc="po")

        stage_A(0)
        stage_A(1)
        for h in range(8):
            dg = self.diag[h % 2]
            w0 = [diag_tok[h]]
            if h >= 1:
                w0 += [pea_tok[h - 1], oev_tok[h - 1]]
            else:
                w0 = w0 + claim
            t_pta = None
            pp = self.p_sb[h % 2]
            for b in range(8):
                t_pta = P.mm(PTA[:, b * 128:(b + 1) * 128],
                             pp[:, b * 128:(b + 1) * 128], dg[:], True, True,
                             waits=w0 if b == 0 else (), inc=(b == 7))
            eng = "ac" if h % 2 == 0 else "dv"
            if eng == "ac":
                P.op("ac", lambda e: e.activation(
                    self.pt_sb[:, 0:1024], PTA, AF.Copy),
                    waits=[t_pta, av_tok[h - 1] if h >= 1 else None])
            else:
                P.op("dv", lambda e: e.tensor_copy(
                    self.pt_sb[:, 0:1024], PTA),
                    waits=[t_pta, av_tok[h - 1] if h >= 1 else None])
            pea_tok[h] = P.op(eng, lambda e: e.drain(), inc=eng)
            if h + 2 < 8:
                emit_S(h + 2)
            t_ptb = P.mm(SLACK[h % 2], pp[:, 8 * 128:9 * 128], dg[:],
                         True, True,
                         waits=[peb_tok[h - 2]] if h >= 2 else (), inc=True)
            pt_tok[h] = t_ptb
            self.diag_war[h % 2] = t_ptb
            if eng == "ac":
                P.op("ac", lambda e, h=h: e.activation(
                    self.pt_sb[:, 1024:1152], SLACK[h % 2], AF.Copy),
                    waits=[t_ptb, av_tok[h - 1] if h >= 1 else None])
            else:
                P.op("dv", lambda e, h=h: e.tensor_copy(
                    self.pt_sb[:, 1024:1152], SLACK[h % 2]),
                    waits=[t_ptb, av_tok[h - 1] if h >= 1 else None])
            peb_tok[h] = P.op(eng, lambda e: e.drain(), inc=eng)
            if h + 2 < 8:
                stage_A(h + 2)
            t_av = None
            for b in range(NBLK):
                slot = (i + b) % RING
                w_av = ()
                if b == 0:
                    w_av = [pea_tok[h]]
                elif b == NBLK - 1:
                    w_av = [peb_tok[h]]
                t_av = P.mm(AVR,
                            self.ring_v(phase, slot)[:, h * 128:(h + 1) * 128],
                            self.pt_sb[:, b * 128:(b + 1) * 128],
                            start=(b == 0), stop=(b == NBLK - 1),
                            waits=w_av, inc=(b == NBLK - 1))
            av_tok[h] = t_av
            oev_tok[h] = P.op("ac", lambda e, h=h, o=oT_off: e.activation(
                self.oT_sb[:, o + h * 128:o + (h + 1) * 128], AVR, AF.Copy),
                waits=[t_av, oT_war], drain=True, inc="ac")
        for b in (0, 1, 2):
            self.pa.tok[b] = [exp_tok[6], peb_tok[6]]
        for b in (3, 4, 5):
            self.pa.tok[b] = [exp_tok[7], peb_tok[7]]
        for b in (6, 7):
            self.pa.tok[b] = [oev_tok[7], peb_tok[7]]
        self.pa.ptr = 0
        return av_tok[7], oev_tok

    def build(self):
        P = self.P
        P.load(self.fence_sb[:], self.fin_d[:])
        P.load(self.id_sb[:], self.idd[:])
        P.load(self.idf_sb[:], self.idfd[:])
        P.load(self.mw_sb[:], self.mwd[:])
        P.load(self.mc_sb[:], self.mcd[:])
        P.load(self.np_sb[:], self.npd[:])
        P.load(self.c2_sb[:], self.c2d[:])
        P.load(self.s2_sb[:], self.s2d[:])
        self.p_free = None
        self.pt_free = None
        self.diag_war = [None, None]
        xu_free = [[], []]

        src = self.gAd.rearrange("(c p) w -> p c w", p=128)
        dst = self.gA_sb[:].rearrange("p (c w) -> p c w", c=NCH)
        P.op("st", lambda e, o=dst, i=src: e.dma_start(out=o, in_=i),
             inc="qst")
        P.op("st", lambda e: e.dma_start(out=self.bs_sb[:], in_=self.bsd[:]),
             inc="qst")
        self.wload(self.wkd, self.w1, ())
        t_w = self.wload(self.wvd, self.w2, ())
        self.t_gA = t_w
        w1_tok = t_w
        w2_tok = t_w

        x_ld = [None] * NT_EXT
        P.load(self.xu[0][:], self.xd[0])
        P.load(self.xu[1][:], self.xd[1])
        lf0 = P.lfence(self)
        self.t_id = self.t_mask = self.t_np = self.t_tab = lf0
        x_ld[0] = lf0
        x_ld[1] = lf0

        kv_stored = [None] * NT_EXT
        unT_stored = [None] * NT_OWN
        xn_free = None
        xnT_free = None
        un_free = None
        unT_free = [[], []]
        ln_war = [None, None]
        gate_war = [None, None]
        bufa_free = None
        kvo_free = []
        v_pend = None
        w1_last = None
        w2_last = None
        rope_prev = None

        for e in range(NT_EXT):
            sl = e % 2
            t_xn, _ = self.emit_ln(self.xu[sl], self.xn_sb, sl, x_ld[e],
                                   xn_free, ln_war[sl], sq_war=rope_prev)
            t_xnT_pe, t_xnT_ev = self.emit_transposes(
                self.xn_sb, self.xnT_sb, t_xn, xnT_free)
            xn_free = t_xnT_pe
            t_u, t_g = self.emit_ffn(self.xnT_sb, t_xnT_ev, self.xu[sl],
                                     x_ld[e], sl, gate_war[sl])
            xnT_free = t_g
            gate_war[sl] = t_u
            t_usf = None
            if e >= HALO:
                P.store(self.us_[e - HALO], self.xu[sl][:], waits=[t_u])
                t_usf = P.sfence(self)
            if v_pend is not None:
                ve, vsl, vtok = v_pend
                lhsv = lambda c, s=vsl: self.unT[s][:, c * 128:(c + 1) * 128]
                v_pe, t_vev = self.emit_proj(lhsv, self.w2, vtok, w2_tok,
                                             out_bf=self.kvo,
                                             out_war=kvo_free)
                w2_last = v_pe
                unT_free[vsl] = unT_free[vsl] + [v_pe]
                P.store(self.vss[ve], self.kvo[:], waits=[t_vev])
                kv_stored[ve] = P.sfence(self)
                kvo_free = [kv_stored[ve]]
                v_pend = None
            t_un, t_sq2 = self.emit_ln(self.xu[sl], self.un_sb, sl, t_u,
                                       un_free, ln_war[sl], sq_war=rope_prev)
            ln_war[sl] = t_un
            xu_free[sl] = [t_un, t_sq2] + ([t_usf] if t_usf else [])
            if e + 2 < NT_EXT:
                P.load(self.xu[sl][:], self.xd[e + 2], waits=[xu_free[sl]])
                x_ld[e + 2] = P.lfence(self)
            t_unT_pe, t_unT_ev = self.emit_transposes(
                self.un_sb, self.unT[sl], t_un, unT_free[sl])
            un_free = t_unT_pe
            if e >= HALO:
                unT_stored[e - HALO] = P.store(self.unTs[e - HALO],
                                               self.unT[sl][:], waits=[t_unT_ev])
            lhsk = lambda c, s=sl: self.unT[s][:, c * 128:(c + 1) * 128]
            k_pe, t_kev = self.emit_proj(lhsk, self.w1, t_unT_ev, w1_tok,
                                         out_bf=self.buf_a, out_war=bufa_free)
            w1_last = k_pe
            t_kr = self.emit_rope(self.buf_a, e, t_kev)
            rope_prev = t_kr
            t_kt_pe, t_ktev = self.emit_transposes(
                self.buf_a, self.kvo, t_kr, kvo_free)
            bufa_free = t_kt_pe
            P.store(self.kTs[e], self.kvo[:], waits=[t_ktev])
            t_ksf = P.sfence(self)
            kvo_free = [t_ksf]
            unT_free[sl] = [t_unT_pe, k_pe] + \
                ([t_ksf] if e >= HALO else [])
            v_pend = (e, sl, t_unT_ev)
        ve, vsl, vtok = v_pend
        lhsv = lambda c, s=vsl: self.unT[s][:, c * 128:(c + 1) * 128]
        v_pe, t_vev = self.emit_proj(lhsv, self.w2, vtok, w2_tok,
                                     out_bf=self.kvo, out_war=kvo_free)
        w2_last = v_pe
        unT_free[vsl] = unT_free[vsl] + [v_pe]
        P.store(self.vss[ve], self.kvo[:], waits=[t_vev])
        kv_stored[ve] = P.sfence(self)
        kvo_free = [kv_stored[ve]]

        w1_tok = self.wload(self.wqd, self.w1, [w1_last])

        ring_free = [None] * NT_EXT
        ring_f = [None] * NT_EXT
        for e2 in range(HALO):
            self._ring_load(1, e2, e2, [kv_stored[e2], w2_last])
        P.load(self.unT[0][:], self.unTs[0],
               waits=[kv_stored[NT_EXT - 1], unT_free[0]])
        P.load(self.unT[1][:], self.unTs[1], waits=[unT_free[1]])
        lfB1 = P.lfence(self)
        for e2 in range(HALO):
            ring_f[e2] = lfB1
        unTl_f = [None] * NT_OWN
        unTl_f[0] = unTl_f[1] = lfB1

        qT_src_tok = [None] * NT_OWN
        s_last = None
        qT_store = [None] * NT_OWN
        sf1 = [None] * NT_OWN
        attn_prev = None
        self.p_free = None
        self.pt_free = None

        q_pe_by_slot = [None, None]
        for i in range(NT_OWN):
            sl = i % 2
            lhsq = lambda c, s=sl: self.unT[s][:, c * 128:(c + 1) * 128]
            q_pe, t_qev = self.emit_proj(lhsq, self.w1, unTl_f[i], w1_tok,
                                         out_bf=self.buf_a, out_war=bufa_free,
                                         evac_scale=SCALE_Q)
            q_pe_by_slot[sl] = q_pe
            if i + 2 < NT_OWN:
                P.load(self.unT[sl][:], self.unTs[i + 2], waits=[q_pe])
            if i + HALO < NT_EXT:
                self._ring_load(1, i + HALO, (i + HALO) % RING,
                                [ring_free[i - 2] if i >= 2 else None])
            lf1 = P.lfence(self)
            if i + 2 < NT_OWN:
                unTl_f[i + 2] = lf1
            if i + HALO < NT_EXT:
                ring_f[i + HALO] = lf1
            if attn_prev is not None:
                ip = attn_prev
                t_avp, evsp = self.emit_attn_tile(
                    1, ip, self.kvo[:, 0:1024], qT_src_tok[ip],
                    ring_f[ip + HALO], 0, sf1[ip] if ip >= 1 else None)
                ring_free[ip] = t_avp
                P.store(self.oTs[ip], self.oT_sb[:, 0:1024], waits=[evsp[-1]])
                s_last = t_avp
            t_qr = self.emit_rope(self.buf_a, i + HALO, t_qev)
            t_qt_pe, t_qtev = self.emit_transposes(
                self.buf_a, self.kvo, t_qr,
                [s_last, sf1[i - 1] if i >= 1 else kv_stored[NT_EXT - 1]])
            bufa_free = t_qt_pe
            qT_src_tok[i] = t_qtev
            qT_store[i] = P.store(self.qTs[i], self.kvo[:, 1024:2048],
                                  waits=[t_qtev])
            sf1[i] = P.sfence(self)
            attn_prev = i
        ip = attn_prev
        t_avp, evsp = self.emit_attn_tile(1, ip, self.kvo[:, 0:1024],
                                          qT_src_tok[ip], ring_f[ip + HALO],
                                          0, sf1[ip])
        ring_free[ip] = t_avp
        P.store(self.oTs[ip], self.oT_sb[:, 0:1024], waits=[evsp[-1]])
        sf1_final = P.sfence(self)
        b1_ring_last = t_avp
        b1_q_last = q_pe_by_slot[1]

        w2_tok = self.wload(self.wod, self.w2, [b1_ring_last])

        ring_free2 = [None] * NT_EXT
        ring_f2 = [None] * NT_EXT
        for e2 in range(HALO + 1):
            self._ring_load(2, e2, e2, [kv_stored[e2], b1_q_last])
        u_ld = [None] * NT_OWN
        qT_ld = [None] * NT_OWN
        oT_ld = [None] * NT_OWN
        y_stored = [None] * NT_OWN
        P.load(self.xu[0][:], self.us_[0], waits=[xu_free[0], sf1_final])
        P.load(self.xu[1][:], self.us_[1], waits=[xu_free[1]])
        P.load(self.un_sb[:, 0:1024], self.qTs[0], waits=[un_free])
        P.load(self.un_sb[:, 1024:2048], self.qTs[1])
        P.load(self.unT[0][:, 0:1024], self.oTs[0], waits=[q_pe_by_slot[0]])
        P.load(self.unT[1][:, 0:1024], self.oTs[1], waits=[q_pe_by_slot[1]])
        lfB2 = P.lfence(self)
        for e2 in range(HALO + 1):
            ring_f2[e2] = lfB2
        u_ld[0] = u_ld[1] = lfB2
        qT_ld[0] = qT_ld[1] = lfB2
        oT_ld[0] = oT_ld[1] = lfB2

        self.p_free = None
        self.pt_free = None
        oT_war2 = None
        ffn_pend = None
        ln_war2 = [None, None]
        gate_war2 = [None, None]
        self.xn_free2 = None
        self.xnT_free2 = None

        for i in range(NT_OWN):
            sl = i % 2
            if i + 1 + HALO < NT_EXT:
                self._ring_load(2, i + 1 + HALO, (i + 1 + HALO) % RING,
                                [ring_free2[i - 1] if i >= 1 else None])
            if ffn_pend is not None:
                ip2, slp, t_anp = ffn_pend
                t_yf = self._b2_tail(ip2, slp, t_anp, gate_war2)
                y_stored[ip2] = t_yf
                xu_free[slp] = [t_yf]
                if ip2 + 2 < NT_OWN:
                    P.load(self.xu[slp][:], self.us_[ip2 + 2], waits=[t_yf])
                ffn_pend = None
            t_av2, evs2 = self.emit_attn_tile(
                2, i, self.un_sb[:, sl * 1024:(sl + 1) * 1024], qT_ld[i],
                ring_f2[i + HALO], 1024, oT_war2)
            ring_free2[i] = t_av2
            if i + 2 < NT_OWN:
                P.load(self.un_sb[:, sl * 1024:(sl + 1) * 1024],
                       self.qTs[i + 2], waits=[t_av2])
            def lhso(c, s=sl):
                if c < 8:
                    return self.unT[s][:, c * 128:(c + 1) * 128]
                return self.oT_sb[:, 1024 + (c - 8) * 128:1024 + (c - 7) * 128]
            o_pe, t_a = self.emit_proj(lhso, self.w2,
                                       [oT_ld[i], evs2[-1]], w2_tok,
                                       x_dst=self.xu[sl], x_tok=u_ld[i])
            oT_war2 = o_pe
            if i + 2 < NT_OWN:
                P.load(self.unT[sl][:, 0:1024], self.oTs[i + 2], waits=[o_pe])
            lf2 = P.lfence(self)
            if i + 2 < NT_OWN:
                qT_ld[i + 2] = lf2
                oT_ld[i + 2] = lf2
            if i + 1 < NT_OWN:
                u_ld[i + 1] = lf2 if u_ld[i + 1] is None else u_ld[i + 1]
            if i + 1 + HALO < NT_EXT:
                ring_f2[i + 1 + HALO] = lf2
            t_an, _ = self.emit_ln(self.xu[sl], self.xn_sb, sl, t_a,
                                   self.xn_free2, ln_war2[sl])
            ln_war2[sl] = t_an
            ffn_pend = (i, sl, t_an)
        ip2, slp, t_anp = ffn_pend
        y_stored[ip2] = self._b2_tail(ip2, slp, t_anp, gate_war2)

        total = P.cnt["qst"]
        P.op("st", lambda e: e.nop(), waits=[("qst", total)])

        nc = self.nc
        with ExitStack() as bctx:
            sems = {}
            for s in Prog.SEMS:
                sems[s] = bctx.enter_context(nc.semaphore(s))
            blk = bctx.enter_context(nc.Block())

            @blk.tensor
            def _(e):
                P.replay("pe", e, sems)

            @blk.vector
            def _(e):
                P.replay("dv", e, sems)

            @blk.scalar
            def _(e):
                P.replay("ac", e, sems)

            @blk.sync
            def _(e):
                P.replay("ld", e, sems)

            @blk.gpsimd
            def _(e):
                P.replay("st", e, sems)
        self.ctx.close()
        return nc

    def _b2_tail(self, i, sl, t_an, gate_war2):
        P = self.P
        t_anT_pe, t_anT_ev = self.emit_transposes(self.xn_sb, self.xnT_sb,
                                                  t_an, self.xnT_free2)
        self.xn_free2 = t_anT_pe
        t_y, t_g2 = self.emit_ffn(self.xnT_sb, t_anT_ev, self.xu[sl],
                                  t_an, sl, gate_war2[sl])
        self.xnT_free2 = t_g2
        gate_war2[sl] = t_y
        P.store(self.yd[i], self.xu[sl][:], waits=[t_y])
        return P.sfence(self)


def kernel(L0_attn_hidden_states, L0_attn_wq, L0_attn_wk, L0_attn_wv,
           L0_attn_wo, L0_ffn_gate_w, L0_ffn_lora_A, L0_ffn_lora_B):
    x = np.asarray(L0_attn_hidden_states, np.float32)
    in_maps = _host_prepare(x, L0_attn_wq, L0_attn_wk, L0_attn_wv, L0_attn_wo,
                            L0_ffn_gate_w, L0_ffn_lora_A, L0_ffn_lora_B)
    k = Kern()
    nc = k.build()
    res = run_bass_kernel_spmd(nc, in_maps, core_ids=list(range(8)))
    out = np.empty((BSZ, SEQ, D), np.float32)
    for c in range(8):
        b, half = c // 2, c % 2
        y = np.asarray(res.results[c]["y"], np.float32).reshape(2048, D)
        out[b, half * 2048:(half + 1) * 2048, :] = y
    return out


# version 08
# speedup vs baseline: 1.2409x; 1.0578x over previous
import numpy as np
import ml_dtypes
import concourse.bass as bass
import concourse.mybir as mybir
from concourse.bass_utils import run_bass_kernel_spmd
from concourse.ap import AP
from contextlib import ExitStack

F32 = mybir.dt.float32
BF16 = mybir.dt.bfloat16
AX = mybir.AxisListType.X
ALU = mybir.AluOpType
AF = mybir.ActivationFunctionType

BSZ, SEQ, D = 4, 4096, 2048
NH, HD, WIN = 16, 128, 1024
NT_OWN, HALO = 16, 8
NT_EXT = NT_OWN + HALO
RING = 10
NBLK = 9
NCH = 16
SCALE_Q = float(128.0 ** -0.5)
NEG = -1.0e30
ER = 128
GAW = 8 + ER

DEBUG = False


def bf(x):
    return np.asarray(x, dtype=ml_dtypes.bfloat16)



PARANOID = False


class Prog:
    SEMS = ("qls", "qst", "qlf", "qsf", "pe", "dv", "ac", "po")
    DMA_SEMS = ("qls", "qst", "qlf", "qsf")
    OWNER = {"pe": "pe", "dv": "dv", "ac": "ac", "po": "st", "qls": None,
             "qst": None, "qlf": None, "qsf": None}

    def __init__(self):
        self.stream = {k: [] for k in ("pe", "dv", "ac", "ld", "st")}
        self.cnt = {s: 0 for s in self.SEMS}

    def op(self, eng, fn, waits=(), inc=None, drain=False):
        w = []
        for x in waits:
            if x is None:
                continue
            if isinstance(x, list):
                w += [t for t in x if t is not None]
            else:
                w.append(x)
        if PARANOID:
            w += [(s, self.cnt[s]) for s in ("pe", "dv", "ac")]
        tok = None
        amt = 0
        if inc is not None:
            amt = 16 if inc in self.DMA_SEMS else 1
            self.cnt[inc] += amt
            tok = (inc, self.cnt[inc])
        self.stream[eng].append((fn, w, inc, amt, drain))
        return tok

    def load(self, out, in_, waits=()):
        return self.op("ld", lambda e, o=out, i=in_: e.dma_start(out=o, in_=i),
                       waits=waits, inc="qls")

    def store(self, out, in_, waits=()):
        return self.op("st", lambda e, o=out, i=in_: e.dma_start(out=o, in_=i),
                       waits=waits, inc="qst")

    def lfence(self, kern):
        return self.op("ld", lambda e: e.dma_start(out=kern.fence_sb[:],
                                                   in_=kern.fin_d[:]),
                       inc="qlf")

    def sfence(self, kern):
        return self.op("st", lambda e: e.dma_start(out=kern.fout_d[:],
                                                   in_=kern.fence_sb[:]),
                       inc="qsf")

    def mm(self, out, lhsT, rhs, start, stop, waits=(), inc=False):
        return self.op(
            "pe",
            lambda e, o=out, l=lhsT, r=rhs, s0=start, s1=stop: e.matmul(
                o, l, r, start=s0, stop=s1),
            waits=waits, inc="pe" if inc else None)

    def replay(self, ename, eng, sems):
        seen = {}
        for fn, waits, inc, amt, drain in self.stream[ename]:
            best = {}
            for (s, v) in waits:
                if Prog.OWNER[s] == ename:
                    continue
                if best.get(s, 0) < v:
                    best[s] = v
            for s, v in best.items():
                if seen.get(s, 0) >= v:
                    continue
                eng.wait_ge(sems[s], v)
                seen[s] = v
            inst = fn(eng)
            if drain:
                inst = eng.drain()
            if inc is not None:
                inst.then_inc(sems[inc], amt)


class PsumBlock:
    def __init__(self, alloc, banks, waits):
        self.a = alloc
        self.banks = banks
        self.waits = waits
        self.c0 = banks[0] * 512

    def ap(self, col0, col1):
        return self.a.ps[:, self.c0 + col0:self.c0 + col1]

    def free(self, *toks):
        toks = [t for t in toks if t is not None]
        for b in self.banks:
            self.a.tok[b] = toks


class PsumAlloc:
    def __init__(self, ps):
        self.ps = ps
        self.tok = [[] for _ in range(8)]
        self.ptr = 0

    def alloc(self, nbanks):
        if self.ptr + nbanks > 8:
            self.ptr = 0
        b0 = self.ptr
        self.ptr = (self.ptr + nbanks) % 8
        banks = list(range(b0, b0 + nbanks))
        waits = []
        for b in banks:
            waits += self.tok[b]
        return PsumBlock(self, banks, waits)



def _rope_tables(positions):
    ntiles = len(positions) // 128
    inv = 1.0 / (10000.0 ** (np.arange(0, HD, 2, dtype=np.float64) / HD))
    pos = np.asarray(positions, np.float64)
    fr = pos[:, None] * inv[None, :]
    c = np.cos(fr).reshape(ntiles, 128, 64).transpose(1, 0, 2).reshape(
        128, ntiles * 64)
    s = (-np.sin(fr)).reshape(ntiles, 128, 64).transpose(1, 0, 2).reshape(
        128, ntiles * 64)
    return bf(c), bf(s)


def _host_prepare(x, wq, wk, wv, wo, gate_w, lora_A, lora_B):
    qq = np.arange(128)[:, None]
    kk = np.arange(128)[None, :]
    m_w = bf(np.where(kk > qq, 0.0, NEG))
    m_c = bf(np.where(kk <= qq, 0.0, NEG))
    ident = bf(np.eye(128))
    gA = np.ascontiguousarray(
        np.concatenate([np.asarray(gate_w).T.astype(np.float32, copy=False),
                        np.asarray(lora_A).reshape(ER, D).T.astype(
                            np.float32, copy=False)], axis=1))
    bstack = np.ascontiguousarray(
        np.asarray(lora_B).transpose(0, 2, 1).reshape(ER, D).astype(
            np.float32, copy=False))
    shared = dict(wq=np.asarray(wq, np.float32), wk=np.asarray(wk, np.float32),
                  wv=np.asarray(wv, np.float32), wo=np.asarray(wo, np.float32),
                  gA=gA, bstack=bstack, ident=ident, m_w=m_w, m_c=m_c,
                  ident_f32=np.eye(128, dtype=np.float32))
    x = np.asarray(x, np.float32)
    in_maps = []
    for b in range(BSZ):
        for half in range(2):
            if half == 0:
                xe = np.concatenate(
                    [np.zeros((HALO * 128, D), np.float32), x[b, 0:2048]], axis=0)
                ext_pos = np.concatenate([np.zeros(HALO * 128), np.arange(0, 2048)])
            else:
                xe = np.ascontiguousarray(x[b, 1024:4096])
                ext_pos = np.arange(1024, 4096)
            c2h, s2h = _rope_tables(ext_pos)
            npad = np.zeros((128, NT_OWN), np.float32)
            if half == 0:
                for i in range(NT_OWN):
                    npb = max(0, HALO - i)
                    if npb == 0:
                        continue
                    cnt = np.full(128, 128.0 * (npb - 1))
                    cnt += 127.0 - np.arange(128)
                    npad[:, i] = cnt
            m = dict(shared)
            m.update(x_ext=np.ascontiguousarray(xe.reshape(NT_EXT, 128, D)),
                     c2=c2h, s2=s2h, npad=npad,
                     fence_in=np.zeros((128, 16), np.float32))
            in_maps.append(m)
    return in_maps



class Kern:
    def __init__(self):
        self.nc = bass.Bass("TRN2", target_bir_lowering=False)
        self.ctx = ExitStack()
        self.P = Prog()
        self._declare()
        self.pa = PsumAlloc(self.ps_all)

    def _declare(self):
        nc = self.nc
        dt = lambda n, s, d, k: nc.dram_tensor(n, s, d, kind=k).ap()
        self.xd = dt("x_ext", [NT_EXT, 128, D], F32, "ExternalInput")
        self.wqd = dt("wq", [D, D], F32, "ExternalInput")
        self.wkd = dt("wk", [D, D], F32, "ExternalInput")
        self.wvd = dt("wv", [D, D], F32, "ExternalInput")
        self.wod = dt("wo", [D, D], F32, "ExternalInput")
        self.gAd = dt("gA", [D, GAW], F32, "ExternalInput")
        self.bsd = dt("bstack", [ER, D], F32, "ExternalInput")
        self.c2d = dt("c2", [128, NT_EXT * 64], BF16, "ExternalInput")
        self.s2d = dt("s2", [128, NT_EXT * 64], BF16, "ExternalInput")
        self.npd = dt("npad", [128, NT_OWN], F32, "ExternalInput")
        self.idd = dt("ident", [128, 128], BF16, "ExternalInput")
        self.mwd = dt("m_w", [128, 128], BF16, "ExternalInput")
        self.mcd = dt("m_c", [128, 128], BF16, "ExternalInput")
        self.fin_d = dt("fence_in", [128, 16], F32, "ExternalInput")
        self.idfd = dt("ident_f32", [128, 128], F32, "ExternalInput")
        self.fout_d = dt("fence_out", [128, 16], F32, "Internal")
        self.yd = dt("y", [NT_OWN, 128, D], F32, "ExternalOutput")
        SK = "ExternalOutput" if DEBUG else "Internal"
        self.kTs = dt("kT_s", [NT_EXT, 128, D], BF16, SK)
        self.vss = dt("v_s", [NT_EXT, 128, D], BF16, SK)
        self.us_ = dt("u_s", [NT_OWN, 128, D], F32, SK)
        self.unTs = dt("unT_s", [NT_OWN, 128, D], BF16, SK)
        self.qTs = dt("qT_s", [NT_OWN, 128, 1024], BF16, SK)
        self.oTs = dt("oT_s", [NT_OWN, 128, 1024], BF16, SK)

        sb = lambda n, s, d: self.ctx.enter_context(nc.sbuf_tensor(n, s, d))
        self.fence_sb = sb("fence_sb", [128, 16], F32)
        self.id_sb = sb("id_sb", [128, 128], BF16)
        self.idf_sb = sb("idf_sb", [128, 128], F32)
        self.diag = [sb("diag0", [128, 128], BF16), sb("diag1", [128, 128], BF16)]
        self.mw_sb = sb("mw_sb", [128, 128], BF16)
        self.mc_sb = sb("mc_sb", [128, 128], BF16)
        self.np_sb = sb("np_sb", [128, NT_OWN], F32)
        self.c2_sb = sb("c2_sb", [128, NT_EXT * 64], BF16)
        self.s2_sb = sb("s2_sb", [128, NT_EXT * 64], BF16)
        self.gA_sb = sb("gA_sb", [128, NCH * GAW], BF16)
        self.bs_sb = sb("bs_sb", [128, D], BF16)
        self.w1 = sb("w1", [128, NCH * D], BF16)
        self.w2 = sb("w2", [128, NCH * D], BF16)
        self.xu = [sb("xu0", [128, D], F32), sb("xu1", [128, D], F32)]
        self.xnT = [sb("xnT0", [128, D], BF16),
                    sb("xnT1", [128, D], BF16)]
        self.un_sb = sb("un_sb", [128, D], BF16)
        self.unT = [sb("unT0", [128, D], BF16), sb("unT1", [128, D], BF16)]
        self.buf_a = sb("buf_a", [128, D], BF16)
        self.buf_b = sb("buf_b", [128, D], BF16)
        self.kvo = sb("kvo", [128, D], BF16)
        self.p_sb = [sb("p_sb0", [128, NBLK * 128], BF16),
                     sb("p_sb1", [128, NBLK * 128], BF16)]
        self.pt_sb = sb("pt_sb", [128, NBLK * 128], BF16)
        self.oT_sb = sb("oT_sb", [128, D], BF16)
        mk2 = lambda n, w: [sb(n + "0", [128, w], F32), sb(n + "1", [128, w], F32)]
        self.st1 = mk2("st1", 1)
        self.ssq = mk2("ssq", 1)
        self.mean = mk2("mean", 1)
        self.nmean = mk2("nmean", 1)
        self.msq = mk2("msq", 1)
        self.var = mk2("var", 1)
        self.lnv = mk2("lnv", 1)
        self.rstd = mk2("rstd", 1)
        self.g8a = mk2("g8a", 8)
        self.g8b = mk2("g8b", 8)
        self.gm1 = mk2("gm1", 1)
        self.gm2 = mk2("gm2", 1)
        self.gdl = mk2("gdl", 1)
        self.ge = mk2("ge", 1)
        self.gden = mk2("gden", 1)
        self.grcp = mk2("grcp", 1)
        self.comb8 = mk2("comb8", 8)
        self.comb_er = [sb("comber", [128, ER], F32)] * 2
        self.ch_sb = [sb("ch0", [128, ER], BF16), sb("ch1", [128, ER], BF16)]
        self.chT_sb = [sb("chT0", [128, 128], BF16), sb("chT1", [128, 128], BF16)]
        self.nm_all = sb("nm_all", [128, 8], F32)
        self.sum_all = sb("sum_all", [128, 8], F32)
        self.enm = sb("enm", [128, 1], F32)
        self.corr = sb("corr", [128, 1], F32)
        self.rcp_s = sb("rcp_s", [128, 1], F32)
        self.ps_all = self.ctx.enter_context(
            nc.psum_tensor("ps_all", [128, 4096], F32))

    def ring_k(self, phase, slot):
        a = self.w2 if phase == 1 else self.w1
        return a[:, slot * 1024:(slot + 1) * 1024]

    def ring_v(self, phase, slot):
        a = self.w2 if phase == 1 else self.w1
        return a[:, (RING + slot) * 1024:(RING + slot + 1) * 1024]

    def _ring_load(self, phase, e, slot, waits):
        P = self.P
        cols = slice(0, 1024) if phase == 1 else slice(1024, 2048)
        P.load(self.ring_k(phase, slot), self.kTs[e][:, cols], waits=waits)
        return P.load(self.ring_v(phase, slot), self.vss[e][:, cols], waits=())

    def wload(self, wdram, dest, waits):
        P = self.P
        for g in range(4):
            src = wdram[g * 512:(g + 1) * 512, :].rearrange(
                "(c p) d -> p c d", p=128)
            dst = dest[:, g * 4 * D:(g + 1) * 4 * D].rearrange(
                "p (c d) -> p c d", c=4)
            P.op("st", lambda e, o=dst, i=src: e.dma_start(out=o, in_=i),
                 waits=waits, inc="qst")
        return P.sfence(self)

    def emit_ln(self, x_src, out_bf, par, ld_tok, out_war, stats_war,
                sq_war=None):
        P = self.P
        s = lambda b: b[par]
        t_s1 = P.op("dv", lambda e: e.reduce_sum(s(self.st1)[:], x_src[:],
                                                 axis=AX),
                    waits=[ld_tok, stats_war], drain=True, inc="dv")
        t_sq = P.op("ac", lambda e: e.activation(
            self.buf_b[:], x_src[:], AF.Square, accum_out=s(self.ssq)[:, 0:1]),
            waits=[ld_tok, stats_war, sq_war], drain=True, inc="ac")
        P.op("st", lambda e: e.tensor_scalar(
            s(self.mean)[:], s(self.st1)[:], 1.0 / D, None, ALU.mult),
            waits=[t_s1, stats_war])
        P.op("st", lambda e: e.tensor_scalar(
            s(self.nmean)[:], s(self.mean)[:], -1.0, None, ALU.mult))
        P.op("st", lambda e: e.tensor_mul(
            s(self.msq)[:], s(self.mean)[:], s(self.mean)[:]))
        P.op("st", lambda e: e.tensor_scalar(
            s(self.var)[:], s(self.ssq)[:], 1.0 / D, 1e-5, ALU.mult, ALU.add),
            waits=[t_sq])
        t_po = P.op("st", lambda e: e.tensor_sub(
            s(self.var)[:], s(self.var)[:], s(self.msq)[:]), inc="po")
        P.op("ac", lambda e: e.activation(
            s(self.lnv)[:], s(self.var)[:, 0:1], AF.Ln),
            waits=[t_po], drain=True)
        t_r = P.op("ac", lambda e: e.activation(
            s(self.rstd)[:], s(self.lnv)[:, 0:1], AF.Exp, scale=-0.5),
            drain=True, inc="ac")
        t_ap = P.op("dv", lambda e: e.tensor_scalar(
            out_bf[:], x_src[:], s(self.nmean)[:, 0:1], s(self.rstd)[:, 0:1],
            ALU.add, ALU.mult), waits=[t_r, t_po, out_war], drain=True, inc="dv")
        return t_ap, t_sq

    def emit_transposes(self, src_bf, dst_bf, src_tok, dst_war):
        P = self.P
        pe_last = None
        blks = []
        for half in range(2):
            blk = self.pa.alloc(2)
            t_mm = None
            for c in range(8):
                cc = half * 8 + c
                t_mm = P.mm(blk.ap(c * 128, (c + 1) * 128),
                            src_bf[:, cc * 128:(cc + 1) * 128], self.id_sb[:],
                            True, True,
                            waits=([src_tok, self.t_id] + blk.waits)
                            if c == 0 else (),
                            inc=(c == 7))
            blks.append((blk, t_mm))
            pe_last = t_mm
        for half, (blk, t_mm) in enumerate(blks):
            P.op("ac", lambda e, h=half, b=blk: e.activation(
                dst_bf[:, h * 1024:(h + 1) * 1024], b.ap(0, 1024), AF.Copy),
                waits=[t_mm, dst_war])
        ev = P.op("ac", lambda e: e.drain(), inc="ac")
        for blk, _ in blks:
            blk.free(ev)
        return pe_last, ev

    def emit_proj(self, lhs_fn, rhs_sb, lhs_tok, rhs_tok, out_bf=None,
                  out_war=None, x_dst=None, x_tok=None, evac_scale=None):
        P = self.P
        pe_last = None
        blks = []
        for half in range(2):
            blk = self.pa.alloc(2)
            t_mm = None
            for c in range(NCH):
                for s2 in range(2):
                    col = half * 1024 + s2 * 512
                    t_mm = P.mm(
                        blk.ap(s2 * 512, (s2 + 1) * 512), lhs_fn(c),
                        rhs_sb[:, c * D + col:c * D + col + 512],
                        start=(c == 0), stop=(c == NCH - 1),
                        waits=([lhs_tok, rhs_tok] + blk.waits)
                        if (c == 0 and s2 == 0) else (),
                        inc=(c == NCH - 1 and s2 == 1))
            blks.append((blk, t_mm))
            pe_last = t_mm
        eng = "dv" if x_dst is not None else "ac"
        for half, (blk, t_mm) in enumerate(blks):
            if x_dst is not None:
                P.op("dv", lambda e, h=half, b=blk: e.tensor_add(
                    x_dst[:, h * 1024:(h + 1) * 1024],
                    x_dst[:, h * 1024:(h + 1) * 1024], b.ap(0, 1024)),
                    waits=[t_mm, x_tok])
            else:
                kw = {} if evac_scale is None else {"scale": evac_scale}
                P.op("ac", lambda e, h=half, b=blk, kw=kw: e.activation(
                    out_bf[:, h * 1024:(h + 1) * 1024], b.ap(0, 1024),
                    AF.Copy, **kw),
                    waits=[t_mm, out_war])
        out_tok = P.op(eng, lambda e: e.drain(), inc=eng)
        for blk, _ in blks:
            blk.free(out_tok)
        return pe_last, out_tok

    def emit_rope(self, src_bf, tile_e, src_tok):
        P = self.P
        off = tile_e * 64
        csl = self.c2_sb[:, off:off + 64]
        ssl = self.s2_sb[:, off:off + 64]
        cbc = AP(csl.tensor, csl.offset,
                 [list(csl.ap[0]), [0, NH], [0, 2], [1, 64]])
        sbc = AP(ssl.tensor, ssl.offset,
                 [list(ssl.ap[0]), [0, NH], [0, 2], [1, 64]])
        sv = src_bf[:]
        rot = AP(sv.tensor, sv.offset + 64,
                 [list(sv.ap[0]), [128, NH], [-64, 2], [1, 64]])
        t4 = lambda t: t[:].rearrange("p (h x j) -> p h x j", h=NH, x=2)
        P.op("dv", lambda e: e.tensor_tensor(
            t4(self.buf_b), t4(src_bf), cbc, ALU.mult),
            waits=[src_tok, self.t_tab])
        P.op("dv", lambda e: e.tensor_tensor(t4(self.un_sb), rot, sbc, ALU.mult))
        fh = lambda t: AP(t[:].tensor, t[:].offset,
                          [list(t[:].ap[0]), [128, NH], [1, 64]])
        sh = lambda t: AP(t[:].tensor, t[:].offset + 64,
                          [list(t[:].ap[0]), [128, NH], [1, 64]])
        P.op("dv", lambda e: e.tensor_tensor(
            fh(src_bf), fh(self.buf_b), fh(self.un_sb), ALU.add))
        return P.op("dv", lambda e: e.tensor_tensor(
            sh(src_bf), sh(self.buf_b), sh(self.un_sb), ALU.subtract),
            drain=True, inc="dv")

    def emit_ffn(self, actT_sb, actT_tok, x_dst, x_tok, par, war):
        P = self.P
        s = lambda b: b[par]
        gblk = self.pa.alloc(1)
        t_g = None
        for c in range(NCH):
            t_g = P.mm(gblk.ap(0, GAW), actT_sb[:, c * 128:(c + 1) * 128],
                       self.gA_sb[:, c * GAW:(c + 1) * GAW],
                       start=(c == 0), stop=(c == NCH - 1),
                       waits=([actT_tok, self.t_gA] + gblk.waits)
                       if c == 0 else (),
                       inc=(c == NCH - 1))
        lg = gblk.ap(0, 8)
        hh = gblk.ap(8, GAW)
        P.op("dv", lambda e: e.reduce_max(s(self.gm1)[:], lg, axis=AX),
             waits=[t_g, war], drain=True)
        P.op("dv", lambda e: e.tensor_single_scalar(
            s(self.g8a)[:], lg, s(self.gm1)[:, 0:1], ALU.is_ge), drain=True)
        P.op("dv", lambda e: e.scalar_tensor_tensor(
            s(self.g8b)[:], s(self.g8a)[:], NEG, lg, ALU.mult, ALU.add),
            drain=True)
        P.op("dv", lambda e: e.reduce_max(s(self.gm2)[:], s(self.g8b)[:], axis=AX),
             drain=True)
        P.op("dv", lambda e: e.tensor_single_scalar(
            s(self.g8b)[:], s(self.g8b)[:], s(self.gm2)[:, 0:1], ALU.is_ge),
            drain=True)
        t_d = P.op("dv", lambda e: e.tensor_sub(
            s(self.gdl)[:], s(self.gm2)[:], s(self.gm1)[:]), drain=True, inc="dv")
        t_e = P.op("ac", lambda e: e.activation(
            s(self.ge)[:], s(self.gdl)[:, 0:1], AF.Exp),
            waits=[t_d], drain=True, inc="ac")
        P.op("dv", lambda e: e.tensor_scalar(
            s(self.gden)[:], s(self.ge)[:], 1.0, None, ALU.add),
            waits=[t_e], drain=True)
        P.op("dv", lambda e: e.reciprocal(s(self.grcp)[:], s(self.gden)[:]),
             drain=True)
        P.op("dv", lambda e: e.scalar_tensor_tensor(
            s(self.comb8)[:], s(self.g8b)[:], s(self.ge)[:, 0:1], s(self.g8a)[:],
            ALU.mult, ALU.add), drain=True)
        P.op("dv", lambda e: e.tensor_single_scalar(
            s(self.comb8)[:], s(self.comb8)[:], s(self.grcp)[:, 0:1], ALU.mult),
            drain=True)
        c8 = s(self.comb8)[:]
        cbc = AP(c8.tensor, c8.offset, [list(c8.ap[0]), [1, 8], [0, 16]])
        P.op("dv", lambda e: e.tensor_copy(
            s(self.comb_er)[:].rearrange("p (x r) -> p x r", x=8), cbc),
            drain=True)
        t_ch = P.op("dv", lambda e: e.tensor_mul(
            s(self.ch_sb)[:], hh, s(self.comb_er)[:]), drain=True, inc="dv")
        gblk.free(t_ch)
        cblk = self.pa.alloc(1)
        t_ct = P.mm(cblk.ap(0, 128), s(self.ch_sb)[:], self.id_sb[:], True, True,
                    waits=[t_ch] + cblk.waits, inc=True)
        t_ce = P.op("dv", lambda e, b=cblk: e.tensor_copy(
            s(self.chT_sb)[:], b.ap(0, 128)), waits=[t_ct], drain=True, inc="dv")
        cblk.free(t_ce)
        blks = []
        for half in range(2):
            blk = self.pa.alloc(2)
            t_mm = None
            for s2 in range(2):
                col = half * 1024 + s2 * 512
                t_mm = P.mm(blk.ap(s2 * 512, (s2 + 1) * 512), s(self.chT_sb)[:],
                            self.bs_sb[:, col:col + 512], True, True,
                            waits=([t_ce] + blk.waits) if s2 == 0 else (),
                            inc=(s2 == 1))
            blks.append((blk, t_mm))
        for half, (blk, t_mm) in enumerate(blks):
            P.op("dv", lambda e, h=half, b=blk: e.tensor_add(
                x_dst[:, h * 1024:(h + 1) * 1024],
                x_dst[:, h * 1024:(h + 1) * 1024], b.ap(0, 1024)),
                waits=[t_mm, x_tok])
        t_add = P.op("dv", lambda e: e.drain(), inc="dv")
        for blk, _ in blks:
            blk.free(t_add)
        return t_add, t_g

    def emit_attn_tile(self, phase, i, qsrc, q_tok, ring_tok, oT_off, oT_war):
        P = self.P
        ps = self.ps_all
        SREG = (ps[:, 0:1152], ps[:, 1536:2688])
        SLACK = (ps[:, 1216:1344], ps[:, 2752:2880])
        PTA = ps[:, 3072:4096]
        AVR = ps[:, 3072:3200]
        claim = []
        for b in range(8):
            claim += self.pa.tok[b]
            self.pa.tok[b] = []
        s_tok = [None] * 8
        exp_tok = [None] * 8
        oev_tok = [None] * 8
        pt_tok = [None] * 8
        av_tok = [None] * 8
        peb_tok = [None] * 8
        diag_tok = [None] * 8

        def emit_S(h):
            reg = SREG[h % 2]
            w = [q_tok, ring_tok, self.t_mask, self.t_id, self.t_np]
            if h >= 2:
                w += [exp_tok[h - 2]]
                if h >= 4:
                    w += [peb_tok[h - 4]]
            else:
                w = w + claim
            P.mm(reg[:, 0:128], self.id_sb[:], self.mw_sb[:], True, False,
                 waits=w)
            P.mm(reg[:, 8 * 128:9 * 128], self.id_sb[:], self.mc_sb[:],
                 True, False)
            t = None
            for b in range(NBLK):
                slot = (i + b) % RING
                t = P.mm(reg[:, b * 128:(b + 1) * 128],
                         qsrc[:, h * 128:(h + 1) * 128],
                         self.ring_k(phase, slot)[:, h * 128:(h + 1) * 128],
                         start=(b not in (0, 8)), stop=True,
                         inc=(b == NBLK - 1))
            s_tok[h] = t

        emit_S(0)
        emit_S(1)
        pea_tok = [None] * 8

        def stage_A(h):
            reg = SREG[h % 2]
            t_nm = P.op("dv", lambda e, h=h, r=reg: e.reduce_max(
                self.nm_all[:, h:h + 1], r, axis=AX, negate=True),
                waits=[s_tok[h]], drain=True, inc="dv")
            P.op("ac", lambda e, h=h, r=reg: e.activation(
                self.p_sb[h % 2][:], r, AF.Exp, bias=self.nm_all[:, h:h + 1],
                accum_out=self.sum_all[:, h:h + 1]),
                waits=[t_nm, pt_tok[h - 2] if h >= 2 else None])
            t_exp = P.op("ac", lambda e, h=h: e.activation(
                self.enm[:], self.nm_all[:, h:h + 1], AF.Exp),
                drain=True, inc="ac")
            exp_tok[h] = t_exp
            P.op("st", lambda e, i=i: e.tensor_mul(
                self.corr[:], self.np_sb[:, i:i + 1], self.enm[:]),
                waits=[t_exp])
            t_corr = P.op("st", lambda e, h=h: e.tensor_sub(
                self.corr[:], self.sum_all[:, h:h + 1], self.corr[:]),
                inc="po")
            t_rcp = P.op("dv", lambda e: e.reciprocal(
                self.rcp_s[:], self.corr[:]),
                waits=[t_corr], drain=True, inc="dv")
            rb = self.rcp_s[:]
            rbc = AP(rb.tensor, rb.offset, [list(rb.ap[0]), [0, 128]])
            dg = self.diag[h % 2]
            diag_tok[h] = P.op("st", lambda e, d=dg, rbc=rbc: e.tensor_tensor(
                d[:], self.idf_sb[:], rbc, ALU.mult),
                waits=[t_rcp, self.diag_war[h % 2]], inc="po")

        stage_A(0)
        stage_A(1)
        for h in range(8):
            dg = self.diag[h % 2]
            w0 = [diag_tok[h]]
            if h >= 1:
                w0 += [pea_tok[h - 1], oev_tok[h - 1]]
            else:
                w0 = w0 + claim
            t_pta = None
            pp = self.p_sb[h % 2]
            for b in range(8):
                t_pta = P.mm(PTA[:, b * 128:(b + 1) * 128],
                             pp[:, b * 128:(b + 1) * 128], dg[:], True, True,
                             waits=w0 if b == 0 else (), inc=(b == 7))
            eng = "ac" if h % 2 == 0 else "dv"
            if eng == "ac":
                P.op("ac", lambda e: e.activation(
                    self.pt_sb[:, 0:1024], PTA, AF.Copy),
                    waits=[t_pta, av_tok[h - 1] if h >= 1 else None])
            else:
                P.op("dv", lambda e: e.tensor_copy(
                    self.pt_sb[:, 0:1024], PTA),
                    waits=[t_pta, av_tok[h - 1] if h >= 1 else None])
            pea_tok[h] = P.op(eng, lambda e: e.drain(), inc=eng)
            if h + 2 < 8:
                emit_S(h + 2)
            t_ptb = P.mm(SLACK[h % 2], pp[:, 8 * 128:9 * 128], dg[:],
                         True, True,
                         waits=[peb_tok[h - 2]] if h >= 2 else (), inc=True)
            pt_tok[h] = t_ptb
            self.diag_war[h % 2] = t_ptb
            if eng == "ac":
                P.op("ac", lambda e, h=h: e.activation(
                    self.pt_sb[:, 1024:1152], SLACK[h % 2], AF.Copy),
                    waits=[t_ptb, av_tok[h - 1] if h >= 1 else None])
            else:
                P.op("dv", lambda e, h=h: e.tensor_copy(
                    self.pt_sb[:, 1024:1152], SLACK[h % 2]),
                    waits=[t_ptb, av_tok[h - 1] if h >= 1 else None])
            peb_tok[h] = P.op(eng, lambda e: e.drain(), inc=eng)
            if h + 2 < 8:
                stage_A(h + 2)
            t_av = None
            for b in range(NBLK):
                slot = (i + b) % RING
                w_av = ()
                if b == 0:
                    w_av = [pea_tok[h]]
                elif b == NBLK - 1:
                    w_av = [peb_tok[h]]
                t_av = P.mm(AVR,
                            self.ring_v(phase, slot)[:, h * 128:(h + 1) * 128],
                            self.pt_sb[:, b * 128:(b + 1) * 128],
                            start=(b == 0), stop=(b == NBLK - 1),
                            waits=w_av, inc=(b == NBLK - 1))
            av_tok[h] = t_av
            oev_tok[h] = P.op("ac", lambda e, h=h, o=oT_off: e.activation(
                self.oT_sb[:, o + h * 128:o + (h + 1) * 128], AVR, AF.Copy),
                waits=[t_av, oT_war], drain=True, inc="ac")
        for b in (0, 1, 2):
            self.pa.tok[b] = [exp_tok[6], peb_tok[6]]
        for b in (3, 4, 5):
            self.pa.tok[b] = [exp_tok[7], peb_tok[7]]
        for b in (6, 7):
            self.pa.tok[b] = [oev_tok[7], peb_tok[7]]
        self.pa.ptr = 0
        return av_tok[7], oev_tok

    def build(self):
        P = self.P
        P.load(self.fence_sb[:], self.fin_d[:])
        P.load(self.id_sb[:], self.idd[:])
        P.load(self.idf_sb[:], self.idfd[:])
        P.load(self.mw_sb[:], self.mwd[:])
        P.load(self.mc_sb[:], self.mcd[:])
        P.load(self.np_sb[:], self.npd[:])
        P.load(self.c2_sb[:], self.c2d[:])
        P.load(self.s2_sb[:], self.s2d[:])
        self.p_free = None
        self.pt_free = None
        self.diag_war = [None, None]
        xu_free = [[], []]

        src = self.gAd.rearrange("(c p) w -> p c w", p=128)
        dst = self.gA_sb[:].rearrange("p (c w) -> p c w", c=NCH)
        P.op("st", lambda e, o=dst, i=src: e.dma_start(out=o, in_=i),
             inc="qst")
        P.op("st", lambda e: e.dma_start(out=self.bs_sb[:], in_=self.bsd[:]),
             inc="qst")
        self.wload(self.wkd, self.w1, ())
        t_w = self.wload(self.wvd, self.w2, ())
        self.t_gA = t_w
        w1_tok = t_w
        w2_tok = t_w

        x_ld = [None] * NT_EXT
        P.load(self.xu[0][:], self.xd[0])
        P.load(self.xu[1][:], self.xd[1])
        lf0 = P.lfence(self)
        self.t_id = self.t_mask = self.t_np = self.t_tab = lf0
        x_ld[0] = lf0
        x_ld[1] = lf0

        kv_stored = [None] * NT_EXT
        unT_stored = [None] * NT_OWN
        un_free = None
        unT_free = [[], []]
        ln_war = [None, None]
        gate_war = [None, None]
        bufa_free = None
        kvo_free = []
        v_pend = None
        w1_last = None
        w2_last = None
        rope_prev = None

        xnT_ev = [None] * NT_EXT
        gate_free = [None, None]

        def emit_front_ln(e):
            sl = e % 2
            t_xn, _ = self.emit_ln(self.xu[sl], self.xnT[sl], sl, x_ld[e],
                                   gate_free[sl], ln_war[sl], sq_war=rope_prev)
            ln_war[sl] = t_xn
            return (e, t_xn)

        def emit_front_trs(pend):
            e, t_xn = pend
            sl = e % 2
            t_pe, t_ev = self.emit_transposes(self.xnT[sl], self.xnT[sl],
                                              t_xn, None)
            xnT_ev[e] = t_ev

        ln1_pend = emit_front_ln(0)
        kt_pend = None
        for e in range(NT_EXT):
            sl = e % 2
            emit_front_trs(ln1_pend)
            if kt_pend is not None:
                ke, t_kev = kt_pend
                t_kr = self.emit_rope(self.buf_a, ke, t_kev)
                rope_prev = t_kr
                t_kt_pe, t_ktev = self.emit_transposes(
                    self.buf_a, self.kvo, t_kr, kvo_free)
                bufa_free = t_kt_pe
                P.store(self.kTs[ke], self.kvo[:], waits=[t_ktev])
                t_ksf = P.sfence(self)
                kvo_free = [t_ksf]
                unT_free[ke % 2] = unT_free[ke % 2] + [t_ksf]
                kt_pend = None
            t_u, t_g = self.emit_ffn(self.xnT[sl], xnT_ev[e], self.xu[sl],
                                     x_ld[e], sl, gate_war[sl])
            gate_free[sl] = t_g
            gate_war[sl] = t_u
            t_usf = None
            if e >= HALO:
                P.store(self.us_[e - HALO], self.xu[sl][:], waits=[t_u])
                t_usf = P.sfence(self)
            if v_pend is not None:
                ve, vsl, vtok = v_pend
                lhsv = lambda c, s=vsl: self.unT[s][:, c * 128:(c + 1) * 128]
                v_pe, t_vev = self.emit_proj(lhsv, self.w2, vtok, w2_tok,
                                             out_bf=self.kvo,
                                             out_war=kvo_free)
                w2_last = v_pe
                unT_free[vsl] = unT_free[vsl] + [v_pe]
                P.store(self.vss[ve], self.kvo[:], waits=[t_vev])
                kv_stored[ve] = P.sfence(self)
                kvo_free = [kv_stored[ve]]
                v_pend = None
            t_un, t_sq2 = self.emit_ln(self.xu[sl], self.un_sb, sl, t_u,
                                       un_free, ln_war[sl], sq_war=rope_prev)
            ln_war[sl] = t_un
            xu_free[sl] = [t_un, t_sq2] + ([t_usf] if t_usf else [])
            if e + 2 < NT_EXT:
                P.load(self.xu[sl][:], self.xd[e + 2], waits=[xu_free[sl]])
                x_ld[e + 2] = P.lfence(self)
            t_unT_pe, t_unT_ev = self.emit_transposes(
                self.un_sb, self.unT[sl], t_un, unT_free[sl])
            un_free = t_unT_pe
            if e >= HALO:
                unT_stored[e - HALO] = P.store(self.unTs[e - HALO],
                                               self.unT[sl][:], waits=[t_unT_ev])
            if e + 1 < NT_EXT:
                ln1_pend = emit_front_ln(e + 1)
            lhsk = lambda c, s=sl: self.unT[s][:, c * 128:(c + 1) * 128]
            k_pe, t_kev = self.emit_proj(lhsk, self.w1, t_unT_ev, w1_tok,
                                         out_bf=self.buf_a, out_war=bufa_free)
            w1_last = k_pe
            kt_pend = (e, t_kev)
            unT_free[sl] = [t_unT_pe, k_pe]
            v_pend = (e, sl, t_unT_ev)
        ke, t_kev = kt_pend
        t_kr = self.emit_rope(self.buf_a, ke, t_kev)
        rope_prev = t_kr
        t_kt_pe, t_ktev = self.emit_transposes(self.buf_a, self.kvo, t_kr,
                                               kvo_free)
        bufa_free = t_kt_pe
        P.store(self.kTs[ke], self.kvo[:], waits=[t_ktev])
        t_ksf = P.sfence(self)
        kvo_free = [t_ksf]
        ve, vsl, vtok = v_pend
        lhsv = lambda c, s=vsl: self.unT[s][:, c * 128:(c + 1) * 128]
        v_pe, t_vev = self.emit_proj(lhsv, self.w2, vtok, w2_tok,
                                     out_bf=self.kvo, out_war=kvo_free)
        w2_last = v_pe
        unT_free[vsl] = unT_free[vsl] + [v_pe]
        P.store(self.vss[ve], self.kvo[:], waits=[t_vev])
        kv_stored[ve] = P.sfence(self)
        kvo_free = [kv_stored[ve]]

        w1_tok = self.wload(self.wqd, self.w1, [w1_last])

        ring_free = [None] * NT_EXT
        ring_f = [None] * NT_EXT
        for e2 in range(HALO):
            self._ring_load(1, e2, e2, [kv_stored[e2], w2_last])
        P.load(self.unT[0][:], self.unTs[0],
               waits=[kv_stored[NT_EXT - 1], unT_free[0]])
        P.load(self.unT[1][:], self.unTs[1], waits=[unT_free[1]])
        lfB1 = P.lfence(self)
        for e2 in range(HALO):
            ring_f[e2] = lfB1
        unTl_f = [None] * NT_OWN
        unTl_f[0] = unTl_f[1] = lfB1

        qT_src_tok = [None] * NT_OWN
        s_last = None
        qT_store = [None] * NT_OWN
        sf1 = [None] * NT_OWN
        attn_prev = None
        self.p_free = None
        self.pt_free = None

        q_pe_by_slot = [None, None]
        for i in range(NT_OWN):
            sl = i % 2
            lhsq = lambda c, s=sl: self.unT[s][:, c * 128:(c + 1) * 128]
            q_pe, t_qev = self.emit_proj(lhsq, self.w1, unTl_f[i], w1_tok,
                                         out_bf=self.buf_a, out_war=bufa_free,
                                         evac_scale=SCALE_Q)
            q_pe_by_slot[sl] = q_pe
            if i + 2 < NT_OWN:
                P.load(self.unT[sl][:], self.unTs[i + 2], waits=[q_pe])
            if i + HALO < NT_EXT:
                self._ring_load(1, i + HALO, (i + HALO) % RING,
                                [ring_free[i - 2] if i >= 2 else None])
            lf1 = P.lfence(self)
            if i + 2 < NT_OWN:
                unTl_f[i + 2] = lf1
            if i + HALO < NT_EXT:
                ring_f[i + HALO] = lf1
            if attn_prev is not None:
                ip = attn_prev
                t_avp, evsp = self.emit_attn_tile(
                    1, ip, self.kvo[:, 0:1024], qT_src_tok[ip],
                    ring_f[ip + HALO], 0, sf1[ip] if ip >= 1 else None)
                ring_free[ip] = t_avp
                P.store(self.oTs[ip], self.oT_sb[:, 0:1024], waits=[evsp[-1]])
                s_last = t_avp
            t_qr = self.emit_rope(self.buf_a, i + HALO, t_qev)
            t_qt_pe, t_qtev = self.emit_transposes(
                self.buf_a, self.kvo, t_qr,
                [s_last, sf1[i - 1] if i >= 1 else kv_stored[NT_EXT - 1]])
            bufa_free = t_qt_pe
            qT_src_tok[i] = t_qtev
            qT_store[i] = P.store(self.qTs[i], self.kvo[:, 1024:2048],
                                  waits=[t_qtev])
            sf1[i] = P.sfence(self)
            attn_prev = i
        ip = attn_prev
        t_avp, evsp = self.emit_attn_tile(1, ip, self.kvo[:, 0:1024],
                                          qT_src_tok[ip], ring_f[ip + HALO],
                                          0, sf1[ip])
        ring_free[ip] = t_avp
        P.store(self.oTs[ip], self.oT_sb[:, 0:1024], waits=[evsp[-1]])
        sf1_final = P.sfence(self)
        b1_ring_last = t_avp
        b1_q_last = q_pe_by_slot[1]

        w2_tok = self.wload(self.wod, self.w2, [b1_ring_last])

        ring_free2 = [None] * NT_EXT
        ring_f2 = [None] * NT_EXT
        for e2 in range(HALO + 1):
            self._ring_load(2, e2, e2, [kv_stored[e2], b1_q_last])
        u_ld = [None] * NT_OWN
        qT_ld = [None] * NT_OWN
        oT_ld = [None] * NT_OWN
        y_stored = [None] * NT_OWN
        P.load(self.xu[0][:], self.us_[0], waits=[xu_free[0], sf1_final])
        P.load(self.xu[1][:], self.us_[1], waits=[xu_free[1]])
        P.load(self.un_sb[:, 0:1024], self.qTs[0], waits=[un_free])
        P.load(self.un_sb[:, 1024:2048], self.qTs[1])
        P.load(self.unT[0][:, 0:1024], self.oTs[0], waits=[q_pe_by_slot[0]])
        P.load(self.unT[1][:, 0:1024], self.oTs[1], waits=[q_pe_by_slot[1]])
        lfB2 = P.lfence(self)
        for e2 in range(HALO + 1):
            ring_f2[e2] = lfB2
        u_ld[0] = u_ld[1] = lfB2
        qT_ld[0] = qT_ld[1] = lfB2
        oT_ld[0] = oT_ld[1] = lfB2

        self.p_free = None
        self.pt_free = None
        oT_war2 = None
        ffn_pend = None
        ln_war2 = [None, None]
        gate_war2 = [None, None]
        self.xnT_free2 = [None, None]

        for i in range(NT_OWN):
            sl = i % 2
            if i + 1 + HALO < NT_EXT:
                self._ring_load(2, i + 1 + HALO, (i + 1 + HALO) % RING,
                                [ring_free2[i - 1] if i >= 1 else None])
            if ffn_pend is not None:
                ip2, slp, t_anp = ffn_pend
                t_yf = self._b2_tail(ip2, slp, t_anp, gate_war2)
                y_stored[ip2] = t_yf
                xu_free[slp] = [t_yf]
                if ip2 + 2 < NT_OWN:
                    P.load(self.xu[slp][:], self.us_[ip2 + 2], waits=[t_yf])
                ffn_pend = None
            t_av2, evs2 = self.emit_attn_tile(
                2, i, self.un_sb[:, sl * 1024:(sl + 1) * 1024], qT_ld[i],
                ring_f2[i + HALO], 1024, oT_war2)
            ring_free2[i] = t_av2
            if i + 2 < NT_OWN:
                P.load(self.un_sb[:, sl * 1024:(sl + 1) * 1024],
                       self.qTs[i + 2], waits=[t_av2])
            def lhso(c, s=sl):
                if c < 8:
                    return self.unT[s][:, c * 128:(c + 1) * 128]
                return self.oT_sb[:, 1024 + (c - 8) * 128:1024 + (c - 7) * 128]
            o_pe, t_a = self.emit_proj(lhso, self.w2,
                                       [oT_ld[i], evs2[-1]], w2_tok,
                                       x_dst=self.xu[sl], x_tok=u_ld[i])
            oT_war2 = o_pe
            if i + 2 < NT_OWN:
                P.load(self.unT[sl][:, 0:1024], self.oTs[i + 2], waits=[o_pe])
            lf2 = P.lfence(self)
            if i + 2 < NT_OWN:
                qT_ld[i + 2] = lf2
                oT_ld[i + 2] = lf2
            if i + 1 < NT_OWN:
                u_ld[i + 1] = lf2 if u_ld[i + 1] is None else u_ld[i + 1]
            if i + 1 + HALO < NT_EXT:
                ring_f2[i + 1 + HALO] = lf2
            t_an, _ = self.emit_ln(self.xu[sl], self.xnT[sl], sl, t_a,
                                   self.xnT_free2[sl], ln_war2[sl])
            ln_war2[sl] = t_an
            ffn_pend = (i, sl, t_an)
        ip2, slp, t_anp = ffn_pend
        y_stored[ip2] = self._b2_tail(ip2, slp, t_anp, gate_war2)

        total = P.cnt["qst"]
        P.op("st", lambda e: e.nop(), waits=[("qst", total)])

        nc = self.nc
        with ExitStack() as bctx:
            sems = {}
            for s in Prog.SEMS:
                sems[s] = bctx.enter_context(nc.semaphore(s))
            blk = bctx.enter_context(nc.Block())

            @blk.tensor
            def _(e):
                P.replay("pe", e, sems)

            @blk.vector
            def _(e):
                P.replay("dv", e, sems)

            @blk.scalar
            def _(e):
                P.replay("ac", e, sems)

            @blk.sync
            def _(e):
                P.replay("ld", e, sems)

            @blk.gpsimd
            def _(e):
                P.replay("st", e, sems)
        self.ctx.close()
        return nc

    def _b2_tail(self, i, sl, t_an, gate_war2):
        P = self.P
        t_anT_pe, t_anT_ev = self.emit_transposes(self.xnT[sl], self.xnT[sl],
                                                  t_an, None)
        t_y, t_g2 = self.emit_ffn(self.xnT[sl], t_anT_ev, self.xu[sl],
                                  t_an, sl, gate_war2[sl])
        self.xnT_free2[sl] = t_g2
        gate_war2[sl] = t_y
        P.store(self.yd[i], self.xu[sl][:], waits=[t_y])
        return P.sfence(self)


def kernel(L0_attn_hidden_states, L0_attn_wq, L0_attn_wk, L0_attn_wv,
           L0_attn_wo, L0_ffn_gate_w, L0_ffn_lora_A, L0_ffn_lora_B):
    x = np.asarray(L0_attn_hidden_states, np.float32)
    in_maps = _host_prepare(x, L0_attn_wq, L0_attn_wk, L0_attn_wv, L0_attn_wo,
                            L0_ffn_gate_w, L0_ffn_lora_A, L0_ffn_lora_B)
    k = Kern()
    nc = k.build()
    res = run_bass_kernel_spmd(nc, in_maps, core_ids=list(range(8)))
    out = np.empty((BSZ, SEQ, D), np.float32)
    for c in range(8):
        b, half = c // 2, c % 2
        y = np.asarray(res.results[c]["y"], np.float32).reshape(2048, D)
        out[b, half * 2048:(half + 1) * 2048, :] = y
    return out


# version 10
# speedup vs baseline: 1.3333x; 1.0745x over previous
import numpy as np
import ml_dtypes
import concourse.bass as bass
import concourse.mybir as mybir
from concourse.bass_utils import run_bass_kernel_spmd
from concourse.ap import AP
from contextlib import ExitStack

F32 = mybir.dt.float32
BF16 = mybir.dt.bfloat16
AX = mybir.AxisListType.X
ALU = mybir.AluOpType
AF = mybir.ActivationFunctionType

BSZ, SEQ, D = 4, 4096, 2048
NH, HD, WIN = 16, 128, 1024
NT_OWN, HALO = 16, 8
NT_EXT = NT_OWN + HALO
RING = 10
NBLK = 9
NCH = 16
SCALE_Q = float(128.0 ** -0.5)
NEG = -1.0e30
ER = 128
GAW = 8 + ER
SM_SHIFT = 24.0

DEBUG = False


def bf(x):
    return np.asarray(x, dtype=ml_dtypes.bfloat16)



PARANOID = False


class Prog:
    SEMS = ("qls", "qst", "qlf", "qsf", "pe", "dv", "ac", "po")
    DMA_SEMS = ("qls", "qst", "qlf", "qsf")
    OWNER = {"pe": "pe", "dv": "dv", "ac": "ac", "po": "st", "qls": None,
             "qst": None, "qlf": None, "qsf": None}

    def __init__(self):
        self.stream = {k: [] for k in ("pe", "dv", "ac", "ld", "st")}
        self.cnt = {s: 0 for s in self.SEMS}

    def op(self, eng, fn, waits=(), inc=None, drain=False):
        w = []
        for x in waits:
            if x is None:
                continue
            if isinstance(x, list):
                w += [t for t in x if t is not None]
            else:
                w.append(x)
        if PARANOID:
            w += [(s, self.cnt[s]) for s in ("pe", "dv", "ac")]
        tok = None
        amt = 0
        if inc is not None:
            amt = 16 if inc in self.DMA_SEMS else 1
            self.cnt[inc] += amt
            tok = (inc, self.cnt[inc])
        self.stream[eng].append((fn, w, inc, amt, drain))
        return tok

    def load(self, out, in_, waits=()):
        return self.op("ld", lambda e, o=out, i=in_: e.dma_start(out=o, in_=i),
                       waits=waits, inc="qls")

    def store(self, out, in_, waits=()):
        return self.op("st", lambda e, o=out, i=in_: e.dma_start(out=o, in_=i),
                       waits=waits, inc="qst")

    def lfence(self, kern):
        return self.op("ld", lambda e: e.dma_start(out=kern.fence_sb[:],
                                                   in_=kern.fin_d[:]),
                       inc="qlf")

    def sfence(self, kern):
        return self.op("st", lambda e: e.dma_start(out=kern.fout_d[:],
                                                   in_=kern.fence_sb[:]),
                       inc="qsf")

    def mm(self, out, lhsT, rhs, start, stop, waits=(), inc=False):
        return self.op(
            "pe",
            lambda e, o=out, l=lhsT, r=rhs, s0=start, s1=stop: e.matmul(
                o, l, r, start=s0, stop=s1),
            waits=waits, inc="pe" if inc else None)

    def replay(self, ename, eng, sems):
        seen = {}
        for fn, waits, inc, amt, drain in self.stream[ename]:
            best = {}
            for (s, v) in waits:
                if Prog.OWNER[s] == ename:
                    continue
                if best.get(s, 0) < v:
                    best[s] = v
            for s, v in best.items():
                if seen.get(s, 0) >= v:
                    continue
                eng.wait_ge(sems[s], v)
                seen[s] = v
            inst = fn(eng)
            if drain:
                inst = eng.drain()
            if inc is not None:
                inst.then_inc(sems[inc], amt)


class PsumBlock:
    def __init__(self, alloc, banks, waits):
        self.a = alloc
        self.banks = banks
        self.waits = waits
        self.c0 = banks[0] * 512

    def ap(self, col0, col1):
        return self.a.ps[:, self.c0 + col0:self.c0 + col1]

    def free(self, *toks):
        toks = [t for t in toks if t is not None]
        for b in self.banks:
            self.a.tok[b] = toks


class PsumAlloc:
    def __init__(self, ps):
        self.ps = ps
        self.tok = [[] for _ in range(8)]
        self.ptr = 0

    def alloc(self, nbanks):
        if self.ptr + nbanks > 8:
            self.ptr = 0
        b0 = self.ptr
        self.ptr = (self.ptr + nbanks) % 8
        banks = list(range(b0, b0 + nbanks))
        waits = []
        for b in banks:
            waits += self.tok[b]
        return PsumBlock(self, banks, waits)



def _rope_tables(positions):
    ntiles = len(positions) // 128
    inv = 1.0 / (10000.0 ** (np.arange(0, HD, 2, dtype=np.float64) / HD))
    pos = np.asarray(positions, np.float64)
    fr = pos[:, None] * inv[None, :]
    c = np.cos(fr).reshape(ntiles, 128, 64).transpose(1, 0, 2).reshape(
        128, ntiles * 64)
    s = (-np.sin(fr)).reshape(ntiles, 128, 64).transpose(1, 0, 2).reshape(
        128, ntiles * 64)
    return bf(c), bf(s)


def _host_prepare(x, wq, wk, wv, wo, gate_w, lora_A, lora_B):
    qq = np.arange(128)[:, None]
    kk = np.arange(128)[None, :]
    m_w = bf(np.where(kk > qq, 0.0, NEG))
    m_c = bf(np.where(kk <= qq, 0.0, NEG))
    ident = bf(np.eye(128))
    gA = np.ascontiguousarray(
        np.concatenate([np.asarray(gate_w).T.astype(np.float32, copy=False),
                        np.asarray(lora_A).reshape(ER, D).T.astype(
                            np.float32, copy=False)], axis=1))
    bstack = np.ascontiguousarray(
        np.asarray(lora_B).transpose(0, 2, 1).reshape(ER, D).astype(
            np.float32, copy=False))
    shared = dict(wq=np.asarray(wq, np.float32), wk=np.asarray(wk, np.float32),
                  wv=np.asarray(wv, np.float32), wo=np.asarray(wo, np.float32),
                  gA=gA, bstack=bstack, ident=ident, m_w=m_w, m_c=m_c,
                  ident_f32=np.eye(128, dtype=np.float32))
    x = np.asarray(x, np.float32)
    in_maps = []
    for b in range(BSZ):
        for half in range(2):
            if half == 0:
                xe = np.concatenate(
                    [np.zeros((HALO * 128, D), np.float32), x[b, 0:2048]], axis=0)
                ext_pos = np.concatenate([np.zeros(HALO * 128), np.arange(0, 2048)])
            else:
                xe = np.ascontiguousarray(x[b, 1024:4096])
                ext_pos = np.arange(1024, 4096)
            c2h, s2h = _rope_tables(ext_pos)
            npad = np.zeros((128, NT_OWN), np.float32)
            if half == 0:
                for i in range(NT_OWN):
                    npb = max(0, HALO - i)
                    if npb == 0:
                        continue
                    cnt = np.full(128, 128.0 * (npb - 1))
                    cnt += 127.0 - np.arange(128)
                    npad[:, i] = cnt * np.exp(-SM_SHIFT)
            m = dict(shared)
            m.update(x_ext=np.ascontiguousarray(xe.reshape(NT_EXT, 128, D)),
                     c2=c2h, s2=s2h, npad=npad,
                     fence_in=np.zeros((128, 16), np.float32))
            in_maps.append(m)
    return in_maps



class Kern:
    def __init__(self):
        self.nc = bass.Bass("TRN2", target_bir_lowering=False)
        self.ctx = ExitStack()
        self.P = Prog()
        self._declare()
        self.pa = PsumAlloc(self.ps_all)

    def _declare(self):
        nc = self.nc
        dt = lambda n, s, d, k: nc.dram_tensor(n, s, d, kind=k).ap()
        self.xd = dt("x_ext", [NT_EXT, 128, D], F32, "ExternalInput")
        self.wqd = dt("wq", [D, D], F32, "ExternalInput")
        self.wkd = dt("wk", [D, D], F32, "ExternalInput")
        self.wvd = dt("wv", [D, D], F32, "ExternalInput")
        self.wod = dt("wo", [D, D], F32, "ExternalInput")
        self.gAd = dt("gA", [D, GAW], F32, "ExternalInput")
        self.bsd = dt("bstack", [ER, D], F32, "ExternalInput")
        self.c2d = dt("c2", [128, NT_EXT * 64], BF16, "ExternalInput")
        self.s2d = dt("s2", [128, NT_EXT * 64], BF16, "ExternalInput")
        self.npd = dt("npad", [128, NT_OWN], F32, "ExternalInput")
        self.idd = dt("ident", [128, 128], BF16, "ExternalInput")
        self.mwd = dt("m_w", [128, 128], BF16, "ExternalInput")
        self.mcd = dt("m_c", [128, 128], BF16, "ExternalInput")
        self.fin_d = dt("fence_in", [128, 16], F32, "ExternalInput")
        self.idfd = dt("ident_f32", [128, 128], F32, "ExternalInput")
        self.fout_d = dt("fence_out", [128, 16], F32, "Internal")
        self.yd = dt("y", [NT_OWN, 128, D], F32, "ExternalOutput")
        SK = "ExternalOutput" if DEBUG else "Internal"
        self.kTs = dt("kT_s", [NT_EXT, 128, D], BF16, SK)
        self.vss = dt("v_s", [NT_EXT, 128, D], BF16, SK)
        self.us_ = dt("u_s", [NT_OWN, 128, D], F32, SK)
        self.unTs = dt("unT_s", [NT_OWN, 128, D], BF16, SK)
        self.qTs = dt("qT_s", [NT_OWN, 128, 1024], BF16, SK)
        self.oTs = dt("oT_s", [NT_OWN, 128, 1024], BF16, SK)

        sb = lambda n, s, d: self.ctx.enter_context(nc.sbuf_tensor(n, s, d))
        self.fence_sb = sb("fence_sb", [128, 16], F32)
        self.id_sb = sb("id_sb", [128, 128], BF16)
        self.idf_sb = sb("idf_sb", [128, 128], F32)
        self.diag = [sb("diag0", [128, 128], BF16), sb("diag1", [128, 128], BF16)]
        self.mw_sb = sb("mw_sb", [128, 128], BF16)
        self.mc_sb = sb("mc_sb", [128, 128], BF16)
        self.np_sb = sb("np_sb", [128, NT_OWN], F32)
        self.c2_sb = sb("c2_sb", [128, NT_EXT * 64], BF16)
        self.s2_sb = sb("s2_sb", [128, NT_EXT * 64], BF16)
        self.gA_sb = sb("gA_sb", [128, NCH * GAW], BF16)
        self.bs_sb = sb("bs_sb", [128, D], BF16)
        self.w1 = sb("w1", [128, NCH * D], BF16)
        self.w2 = sb("w2", [128, NCH * D], BF16)
        self.xu = [sb("xu0", [128, D], F32), sb("xu1", [128, D], F32)]
        self.xnT = [sb("xnT0", [128, D], BF16),
                    sb("xnT1", [128, D], BF16)]
        self.un_sb = sb("un_sb", [128, D], BF16)
        self.unT = [sb("unT0", [128, D], BF16), sb("unT1", [128, D], BF16)]
        self.buf_a = sb("buf_a", [128, D], BF16)
        self.buf_b = sb("buf_b", [128, D], BF16)
        self.kvo = sb("kvo", [128, D], BF16)
        self.p_sb = [sb("p_sb0", [128, NBLK * 128], BF16),
                     sb("p_sb1", [128, NBLK * 128], BF16)]
        self.pt_sb = sb("pt_sb", [128, NBLK * 128], BF16)
        self.oT_sb = sb("oT_sb", [128, D], BF16)
        mk2 = lambda n, w: [sb(n + "0", [128, w], F32), sb(n + "1", [128, w], F32)]
        self.st1 = mk2("st1", 1)
        self.ssq = mk2("ssq", 1)
        self.mean = mk2("mean", 1)
        self.nmean = mk2("nmean", 1)
        self.msq = mk2("msq", 1)
        self.var = mk2("var", 1)
        self.lnv = mk2("lnv", 1)
        self.rstd = mk2("rstd", 1)
        self.g8a = mk2("g8a", 8)
        self.g8b = mk2("g8b", 8)
        self.gm1 = mk2("gm1", 1)
        self.gm2 = mk2("gm2", 1)
        self.gdl = mk2("gdl", 1)
        self.ge = mk2("ge", 1)
        self.gden = mk2("gden", 1)
        self.grcp = mk2("grcp", 1)
        self.comb8 = mk2("comb8", 8)
        self.comb_er = [sb("comber", [128, ER], F32)] * 2
        self.ch_sb = [sb("ch0", [128, ER], BF16), sb("ch1", [128, ER], BF16)]
        self.chT_sb = [sb("chT0", [128, 128], BF16), sb("chT1", [128, 128], BF16)]
        self.nm_all = sb("nm_all", [128, 8], F32)
        self.sum_all = sb("sum_all", [128, 8], F32)
        self.enm = sb("enm", [128, 1], F32)
        self.corr = sb("corr", [128, 8], F32)
        self.rcp_s = sb("rcp_s", [128, 8], F32)
        self.ps_all = self.ctx.enter_context(
            nc.psum_tensor("ps_all", [128, 4096], F32))

    def ring_k(self, phase, slot):
        a = self.w2 if phase == 1 else self.w1
        return a[:, slot * 1024:(slot + 1) * 1024]

    def ring_v(self, phase, slot):
        a = self.w2 if phase == 1 else self.w1
        return a[:, (RING + slot) * 1024:(RING + slot + 1) * 1024]

    def _ring_load(self, phase, e, slot, waits):
        P = self.P
        cols = slice(0, 1024) if phase == 1 else slice(1024, 2048)
        P.load(self.ring_k(phase, slot), self.kTs[e][:, cols], waits=waits)
        return P.load(self.ring_v(phase, slot), self.vss[e][:, cols], waits=())

    def wload(self, wdram, dest, waits):
        P = self.P
        for g in range(4):
            src = wdram[g * 512:(g + 1) * 512, :].rearrange(
                "(c p) d -> p c d", p=128)
            dst = dest[:, g * 4 * D:(g + 1) * 4 * D].rearrange(
                "p (c d) -> p c d", c=4)
            P.op("st", lambda e, o=dst, i=src: e.dma_start(out=o, in_=i),
                 waits=waits, inc="qst")
        return P.sfence(self)

    def emit_ln(self, x_src, out_bf, par, ld_tok, out_war, stats_war,
                sq_war=None):
        P = self.P
        s = lambda b: b[par]
        t_s1 = P.op("dv", lambda e: e.reduce_sum(s(self.st1)[:], x_src[:],
                                                 axis=AX),
                    waits=[ld_tok, stats_war], drain=True, inc="dv")
        t_sq = P.op("ac", lambda e: e.activation(
            self.buf_b[:], x_src[:], AF.Square, accum_out=s(self.ssq)[:, 0:1]),
            waits=[ld_tok, stats_war, sq_war], drain=True, inc="ac")
        P.op("st", lambda e: e.tensor_scalar(
            s(self.mean)[:], s(self.st1)[:], 1.0 / D, None, ALU.mult),
            waits=[t_s1, stats_war])
        P.op("st", lambda e: e.tensor_scalar(
            s(self.nmean)[:], s(self.mean)[:], -1.0, None, ALU.mult))
        P.op("st", lambda e: e.tensor_mul(
            s(self.msq)[:], s(self.mean)[:], s(self.mean)[:]))
        P.op("st", lambda e: e.tensor_scalar(
            s(self.var)[:], s(self.ssq)[:], 1.0 / D, 1e-5, ALU.mult, ALU.add),
            waits=[t_sq])
        t_po = P.op("st", lambda e: e.tensor_sub(
            s(self.var)[:], s(self.var)[:], s(self.msq)[:]), inc="po")
        P.op("ac", lambda e: e.activation(
            s(self.lnv)[:], s(self.var)[:, 0:1], AF.Ln),
            waits=[t_po], drain=True)
        t_r = P.op("ac", lambda e: e.activation(
            s(self.rstd)[:], s(self.lnv)[:, 0:1], AF.Exp, scale=-0.5),
            drain=True, inc="ac")
        t_ap = P.op("dv", lambda e: e.tensor_scalar(
            out_bf[:], x_src[:], s(self.nmean)[:, 0:1], s(self.rstd)[:, 0:1],
            ALU.add, ALU.mult), waits=[t_r, t_po, out_war], drain=True, inc="dv")
        return t_ap, t_sq

    def emit_transposes(self, src_bf, dst_bf, src_tok, dst_war):
        P = self.P
        pe_last = None
        blks = []
        for half in range(2):
            blk = self.pa.alloc(2)
            t_mm = None
            for c in range(8):
                cc = half * 8 + c
                t_mm = P.mm(blk.ap(c * 128, (c + 1) * 128),
                            src_bf[:, cc * 128:(cc + 1) * 128], self.id_sb[:],
                            True, True,
                            waits=([src_tok, self.t_id] + blk.waits)
                            if c == 0 else (),
                            inc=(c == 7))
            blks.append((blk, t_mm))
            pe_last = t_mm
        for half, (blk, t_mm) in enumerate(blks):
            P.op("ac", lambda e, h=half, b=blk: e.activation(
                dst_bf[:, h * 1024:(h + 1) * 1024], b.ap(0, 1024), AF.Copy),
                waits=[t_mm, dst_war])
        ev = P.op("ac", lambda e: e.drain(), inc="ac")
        for blk, _ in blks:
            blk.free(ev)
        return pe_last, ev

    def emit_proj(self, lhs_fn, rhs_sb, lhs_tok, rhs_tok, out_bf=None,
                  out_war=None, x_dst=None, x_tok=None, evac_scale=None):
        P = self.P
        pe_last = None
        blks = []
        for half in range(2):
            blk = self.pa.alloc(2)
            t_mm = None
            for c in range(NCH):
                for s2 in range(2):
                    col = half * 1024 + s2 * 512
                    t_mm = P.mm(
                        blk.ap(s2 * 512, (s2 + 1) * 512), lhs_fn(c),
                        rhs_sb[:, c * D + col:c * D + col + 512],
                        start=(c == 0), stop=(c == NCH - 1),
                        waits=([lhs_tok, rhs_tok] + blk.waits)
                        if (c == 0 and s2 == 0) else (),
                        inc=(c == NCH - 1 and s2 == 1))
            blks.append((blk, t_mm))
            pe_last = t_mm
        eng = "dv" if x_dst is not None else "ac"
        for half, (blk, t_mm) in enumerate(blks):
            if x_dst is not None:
                P.op("dv", lambda e, h=half, b=blk: e.tensor_add(
                    x_dst[:, h * 1024:(h + 1) * 1024],
                    x_dst[:, h * 1024:(h + 1) * 1024], b.ap(0, 1024)),
                    waits=[t_mm, x_tok])
            else:
                kw = {} if evac_scale is None else {"scale": evac_scale}
                P.op("ac", lambda e, h=half, b=blk, kw=kw: e.activation(
                    out_bf[:, h * 1024:(h + 1) * 1024], b.ap(0, 1024),
                    AF.Copy, **kw),
                    waits=[t_mm, out_war])
        out_tok = P.op(eng, lambda e: e.drain(), inc=eng)
        for blk, _ in blks:
            blk.free(out_tok)
        return pe_last, out_tok

    def emit_rope(self, src_bf, tile_e, src_tok):
        P = self.P
        off = tile_e * 64
        csl = self.c2_sb[:, off:off + 64]
        ssl = self.s2_sb[:, off:off + 64]
        cbc = AP(csl.tensor, csl.offset,
                 [list(csl.ap[0]), [0, NH], [0, 2], [1, 64]])
        sbc = AP(ssl.tensor, ssl.offset,
                 [list(ssl.ap[0]), [0, NH], [0, 2], [1, 64]])
        sv = src_bf[:]
        rot = AP(sv.tensor, sv.offset + 64,
                 [list(sv.ap[0]), [128, NH], [-64, 2], [1, 64]])
        t4 = lambda t: t[:].rearrange("p (h x j) -> p h x j", h=NH, x=2)
        P.op("dv", lambda e: e.tensor_tensor(
            t4(self.buf_b), t4(src_bf), cbc, ALU.mult),
            waits=[src_tok, self.t_tab])
        P.op("dv", lambda e: e.tensor_tensor(t4(self.un_sb), rot, sbc, ALU.mult))
        fh = lambda t: AP(t[:].tensor, t[:].offset,
                          [list(t[:].ap[0]), [128, NH], [1, 64]])
        sh = lambda t: AP(t[:].tensor, t[:].offset + 64,
                          [list(t[:].ap[0]), [128, NH], [1, 64]])
        P.op("dv", lambda e: e.tensor_tensor(
            fh(src_bf), fh(self.buf_b), fh(self.un_sb), ALU.add))
        return P.op("dv", lambda e: e.tensor_tensor(
            sh(src_bf), sh(self.buf_b), sh(self.un_sb), ALU.subtract),
            drain=True, inc="dv")

    def emit_ffn(self, actT_sb, actT_tok, x_dst, x_tok, par, war):
        P = self.P
        s = lambda b: b[par]
        gblk = self.pa.alloc(1)
        t_g = None
        for c in range(NCH):
            t_g = P.mm(gblk.ap(0, GAW), actT_sb[:, c * 128:(c + 1) * 128],
                       self.gA_sb[:, c * GAW:(c + 1) * GAW],
                       start=(c == 0), stop=(c == NCH - 1),
                       waits=([actT_tok, self.t_gA] + gblk.waits)
                       if c == 0 else (),
                       inc=(c == NCH - 1))
        lg = gblk.ap(0, 8)
        hh = gblk.ap(8, GAW)
        P.op("dv", lambda e: e.reduce_max(s(self.gm1)[:], lg, axis=AX),
             waits=[t_g, war], drain=True)
        P.op("dv", lambda e: e.tensor_single_scalar(
            s(self.g8a)[:], lg, s(self.gm1)[:, 0:1], ALU.is_ge), drain=True)
        P.op("dv", lambda e: e.scalar_tensor_tensor(
            s(self.g8b)[:], s(self.g8a)[:], NEG, lg, ALU.mult, ALU.add),
            drain=True)
        P.op("dv", lambda e: e.reduce_max(s(self.gm2)[:], s(self.g8b)[:], axis=AX),
             drain=True)
        P.op("dv", lambda e: e.tensor_single_scalar(
            s(self.g8b)[:], s(self.g8b)[:], s(self.gm2)[:, 0:1], ALU.is_ge),
            drain=True)
        t_d = P.op("dv", lambda e: e.tensor_sub(
            s(self.gdl)[:], s(self.gm2)[:], s(self.gm1)[:]), drain=True, inc="dv")
        t_e = P.op("ac", lambda e: e.activation(
            s(self.ge)[:], s(self.gdl)[:, 0:1], AF.Exp),
            waits=[t_d], drain=True, inc="ac")
        P.op("dv", lambda e: e.tensor_scalar(
            s(self.gden)[:], s(self.ge)[:], 1.0, None, ALU.add),
            waits=[t_e], drain=True)
        P.op("dv", lambda e: e.reciprocal(s(self.grcp)[:], s(self.gden)[:]),
             drain=True)
        P.op("dv", lambda e: e.scalar_tensor_tensor(
            s(self.comb8)[:], s(self.g8b)[:], s(self.ge)[:, 0:1], s(self.g8a)[:],
            ALU.mult, ALU.add), drain=True)
        P.op("dv", lambda e: e.tensor_single_scalar(
            s(self.comb8)[:], s(self.comb8)[:], s(self.grcp)[:, 0:1], ALU.mult),
            drain=True)
        c8 = s(self.comb8)[:]
        cbc = AP(c8.tensor, c8.offset, [list(c8.ap[0]), [1, 8], [0, 16]])
        P.op("dv", lambda e: e.tensor_copy(
            s(self.comb_er)[:].rearrange("p (x r) -> p x r", x=8), cbc),
            drain=True)
        t_ch = P.op("dv", lambda e: e.tensor_mul(
            s(self.ch_sb)[:], hh, s(self.comb_er)[:]), drain=True, inc="dv")
        gblk.free(t_ch)
        cblk = self.pa.alloc(1)
        t_ct = P.mm(cblk.ap(0, 128), s(self.ch_sb)[:], self.id_sb[:], True, True,
                    waits=[t_ch] + cblk.waits, inc=True)
        t_ce = P.op("dv", lambda e, b=cblk: e.tensor_copy(
            s(self.chT_sb)[:], b.ap(0, 128)), waits=[t_ct], drain=True, inc="dv")
        cblk.free(t_ce)
        blks = []
        for half in range(2):
            blk = self.pa.alloc(2)
            t_mm = None
            for s2 in range(2):
                col = half * 1024 + s2 * 512
                t_mm = P.mm(blk.ap(s2 * 512, (s2 + 1) * 512), s(self.chT_sb)[:],
                            self.bs_sb[:, col:col + 512], True, True,
                            waits=([t_ce] + blk.waits) if s2 == 0 else (),
                            inc=(s2 == 1))
            blks.append((blk, t_mm))
        for half, (blk, t_mm) in enumerate(blks):
            P.op("dv", lambda e, h=half, b=blk: e.tensor_add(
                x_dst[:, h * 1024:(h + 1) * 1024],
                x_dst[:, h * 1024:(h + 1) * 1024], b.ap(0, 1024)),
                waits=[t_mm, x_tok])
        t_add = P.op("dv", lambda e: e.drain(), inc="dv")
        for blk, _ in blks:
            blk.free(t_add)
        return t_add, t_g

    def emit_attn_tile(self, phase, i, qsrc, q_tok, ring_tok, oT_off, oT_war):
        P = self.P
        ps = self.ps_all
        SREG = (ps[:, 0:1152], ps[:, 1536:2688])
        SLACK = (ps[:, 1216:1344], ps[:, 2752:2880])
        PTA = ps[:, 3072:4096]
        AVR = ps[:, 3072:3200]
        claim = []
        for b in range(8):
            claim += self.pa.tok[b]
            self.pa.tok[b] = []
        s_tok = [None] * 8
        exp_tok = [None] * 8
        oev_tok = [None] * 8
        pt_tok = [None] * 8
        av_tok = [None] * 8
        peb_tok = [None] * 8
        diag_tok = [None] * 8

        def emit_S(h):
            reg = SREG[h % 2]
            w = [q_tok, ring_tok, self.t_mask, self.t_id, self.t_np]
            if h >= 2:
                w += [exp_tok[h - 2]]
                if h >= 4:
                    w += [peb_tok[h - 4]]
            else:
                w = w + claim
            P.mm(reg[:, 0:128], self.id_sb[:], self.mw_sb[:], True, False,
                 waits=w)
            P.mm(reg[:, 8 * 128:9 * 128], self.id_sb[:], self.mc_sb[:],
                 True, False)
            t = None
            for b in range(NBLK):
                slot = (i + b) % RING
                t = P.mm(reg[:, b * 128:(b + 1) * 128],
                         qsrc[:, h * 128:(h + 1) * 128],
                         self.ring_k(phase, slot)[:, h * 128:(h + 1) * 128],
                         start=(b not in (0, 8)), stop=True,
                         inc=(b == NBLK - 1))
            s_tok[h] = t

        emit_S(0)
        emit_S(1)
        pea_tok = [None] * 8

        rcp_tok = [None] * 8

        def stage_A(h):
            reg = SREG[h % 2]
            t_nm = P.op("dv", lambda e, h=h, r=reg: e.reduce_max(
                self.nm_all[:, 1:2], r[:, 0:128], axis=AX, negate=True),
                waits=[s_tok[h]], drain=True, inc="dv")
            t_exp = P.op("ac", lambda e, h=h, r=reg: e.activation(
                self.p_sb[h % 2][:], r, AF.Exp, bias=self.nm_all[:, 0:1],
                accum_out=self.sum_all[:, h:h + 1]),
                waits=[t_nm, pt_tok[h - 2] if h >= 2 else None,
                       self.t_shift],
                drain=True, inc="ac")
            exp_tok[h] = t_exp
            t_corr = P.op("st", lambda e, h=h, i=i: e.tensor_sub(
                self.corr[:, h:h + 1], self.sum_all[:, h:h + 1],
                self.np_sb[:, i:i + 1]),
                waits=[t_exp], inc="po")
            P.op("dv", lambda e, h=h: e.reciprocal(
                self.rcp_s[:, h:h + 1], self.corr[:, h:h + 1]),
                waits=[t_corr], drain=True, inc="dv")
            rcp_tok[h] = P.op("dv", lambda e, h=h: e.tensor_scalar(
                self.p_sb[h % 2][:], self.p_sb[h % 2][:],
                self.rcp_s[:, h:h + 1], None, ALU.mult),
                drain=True, inc="dv")

        stage_A(0)
        stage_A(1)
        for h in range(8):
            dg = self.id_sb
            w0 = [rcp_tok[h]]
            if h >= 1:
                w0 += [pea_tok[h - 1], oev_tok[h - 1]]
            else:
                w0 = w0 + claim
            t_pta = None
            pp = self.p_sb[h % 2]
            for b in range(8):
                t_pta = P.mm(PTA[:, b * 128:(b + 1) * 128],
                             pp[:, b * 128:(b + 1) * 128], dg[:], True, True,
                             waits=w0 if b == 0 else (), inc=(b == 7))
            eng = "ac" if h % 2 == 0 else "dv"
            if eng == "ac":
                P.op("ac", lambda e: e.activation(
                    self.pt_sb[:, 0:1024], PTA, AF.Copy),
                    waits=[t_pta, av_tok[h - 1] if h >= 1 else None])
            else:
                P.op("dv", lambda e: e.tensor_copy(
                    self.pt_sb[:, 0:1024], PTA),
                    waits=[t_pta, av_tok[h - 1] if h >= 1 else None])
            pea_tok[h] = P.op(eng, lambda e: e.drain(), inc=eng)
            if h + 2 < 8:
                emit_S(h + 2)
            t_ptb = P.mm(SLACK[h % 2], pp[:, 8 * 128:9 * 128], dg[:],
                         True, True,
                         waits=[peb_tok[h - 2]] if h >= 2 else (), inc=True)
            pt_tok[h] = t_ptb
            if eng == "ac":
                P.op("ac", lambda e, h=h: e.activation(
                    self.pt_sb[:, 1024:1152], SLACK[h % 2], AF.Copy),
                    waits=[t_ptb, av_tok[h - 1] if h >= 1 else None])
            else:
                P.op("dv", lambda e, h=h: e.tensor_copy(
                    self.pt_sb[:, 1024:1152], SLACK[h % 2]),
                    waits=[t_ptb, av_tok[h - 1] if h >= 1 else None])
            peb_tok[h] = P.op(eng, lambda e: e.drain(), inc=eng)
            if h + 2 < 8:
                stage_A(h + 2)
            t_av = None
            for b in range(NBLK):
                slot = (i + b) % RING
                w_av = ()
                if b == 0:
                    w_av = [pea_tok[h]]
                    if h >= 1:
                        w_av += [oev_tok[h - 1]]
                elif b == NBLK - 1:
                    w_av = [peb_tok[h]]
                t_av = P.mm(AVR,
                            self.ring_v(phase, slot)[:, h * 128:(h + 1) * 128],
                            self.pt_sb[:, b * 128:(b + 1) * 128],
                            start=(b == 0), stop=(b == NBLK - 1),
                            waits=w_av, inc=(b == NBLK - 1))
            av_tok[h] = t_av
            oev_tok[h] = P.op("ac", lambda e, h=h, o=oT_off: e.activation(
                self.oT_sb[:, o + h * 128:o + (h + 1) * 128], AVR, AF.Copy),
                waits=[t_av, oT_war], drain=True, inc="ac")
        for b in (0, 1, 2):
            self.pa.tok[b] = [exp_tok[6], peb_tok[6]]
        for b in (3, 4, 5):
            self.pa.tok[b] = [exp_tok[7], peb_tok[7]]
        for b in (6, 7):
            self.pa.tok[b] = [oev_tok[7], peb_tok[7]]
        self.pa.ptr = 0
        return av_tok[7], oev_tok

    def build(self):
        P = self.P
        P.load(self.fence_sb[:], self.fin_d[:])
        P.load(self.id_sb[:], self.idd[:])
        P.load(self.idf_sb[:], self.idfd[:])
        P.load(self.mw_sb[:], self.mwd[:])
        P.load(self.mc_sb[:], self.mcd[:])
        P.load(self.np_sb[:], self.npd[:])
        P.load(self.c2_sb[:], self.c2d[:])
        P.load(self.s2_sb[:], self.s2d[:])
        self.p_free = None
        self.pt_free = None
        self.diag_war = [None, None]
        xu_free = [[], []]

        src = self.gAd.rearrange("(c p) w -> p c w", p=128)
        dst = self.gA_sb[:].rearrange("p (c w) -> p c w", c=NCH)
        P.op("st", lambda e, o=dst, i=src: e.dma_start(out=o, in_=i),
             inc="qst")
        P.op("st", lambda e: e.dma_start(out=self.bs_sb[:], in_=self.bsd[:]),
             inc="qst")
        self.wload(self.wkd, self.w1, ())
        t_w = self.wload(self.wvd, self.w2, ())
        self.t_gA = t_w
        w1_tok = t_w
        w2_tok = t_w

        x_ld = [None] * NT_EXT
        P.load(self.xu[0][:], self.xd[0])
        P.load(self.xu[1][:], self.xd[1])
        lf0 = P.lfence(self)
        self.t_id = self.t_mask = self.t_np = self.t_tab = lf0
        self.t_shift = P.op("st", lambda e: e.tensor_scalar(
            self.nm_all[:, 0:1], self.np_sb[:, 0:1], 0.0, -SM_SHIFT,
            ALU.mult, ALU.add), waits=[lf0], inc="po")
        x_ld[0] = lf0
        x_ld[1] = lf0

        kv_stored = [None] * NT_EXT
        unT_stored = [None] * NT_OWN
        un_free = None
        unT_free = [[], []]
        ln_war = [None, None]
        gate_war = [None, None]
        bufa_free = None
        kvo_free = []
        v_pend = None
        w1_last = None
        w2_last = None
        rope_prev = None

        xnT_ev = [None] * NT_EXT
        gate_free = [None, None]

        def emit_front_ln(e):
            sl = e % 2
            t_xn, _ = self.emit_ln(self.xu[sl], self.xnT[sl], sl, x_ld[e],
                                   gate_free[sl], ln_war[sl], sq_war=rope_prev)
            ln_war[sl] = t_xn
            return (e, t_xn)

        def emit_front_trs(pend):
            e, t_xn = pend
            sl = e % 2
            t_pe, t_ev = self.emit_transposes(self.xnT[sl], self.xnT[sl],
                                              t_xn, None)
            xnT_ev[e] = t_ev

        ln1_pend = emit_front_ln(0)
        kt_pend = None
        for e in range(NT_EXT):
            sl = e % 2
            emit_front_trs(ln1_pend)
            if kt_pend is not None:
                ke, t_kev = kt_pend
                t_kr = self.emit_rope(self.buf_a, ke, t_kev)
                rope_prev = t_kr
                t_kt_pe, t_ktev = self.emit_transposes(
                    self.buf_a, self.kvo, t_kr, kvo_free)
                bufa_free = t_kt_pe
                P.store(self.kTs[ke], self.kvo[:], waits=[t_ktev])
                t_ksf = P.sfence(self)
                kvo_free = [t_ksf]
                unT_free[ke % 2] = unT_free[ke % 2] + [t_ksf]
                kt_pend = None
            t_u, t_g = self.emit_ffn(self.xnT[sl], xnT_ev[e], self.xu[sl],
                                     x_ld[e], sl, gate_war[sl])
            gate_free[sl] = t_g
            gate_war[sl] = t_u
            t_usf = None
            if e >= HALO:
                P.store(self.us_[e - HALO], self.xu[sl][:], waits=[t_u])
                t_usf = P.sfence(self)
            if v_pend is not None:
                ve, vsl, vtok = v_pend
                lhsv = lambda c, s=vsl: self.unT[s][:, c * 128:(c + 1) * 128]
                v_pe, t_vev = self.emit_proj(lhsv, self.w2, vtok, w2_tok,
                                             out_bf=self.kvo,
                                             out_war=kvo_free)
                w2_last = v_pe
                unT_free[vsl] = unT_free[vsl] + [v_pe]
                P.store(self.vss[ve], self.kvo[:], waits=[t_vev])
                kv_stored[ve] = P.sfence(self)
                kvo_free = [kv_stored[ve]]
                v_pend = None
            t_un, t_sq2 = self.emit_ln(self.xu[sl], self.un_sb, sl, t_u,
                                       un_free, ln_war[sl], sq_war=rope_prev)
            ln_war[sl] = t_un
            xu_free[sl] = [t_un, t_sq2] + ([t_usf] if t_usf else [])
            if e + 2 < NT_EXT:
                P.load(self.xu[sl][:], self.xd[e + 2], waits=[xu_free[sl]])
                x_ld[e + 2] = P.lfence(self)
            t_unT_pe, t_unT_ev = self.emit_transposes(
                self.un_sb, self.unT[sl], t_un, unT_free[sl])
            un_free = t_unT_pe
            if e >= HALO:
                unT_stored[e - HALO] = P.store(self.unTs[e - HALO],
                                               self.unT[sl][:], waits=[t_unT_ev])
            if e + 1 < NT_EXT:
                ln1_pend = emit_front_ln(e + 1)
            lhsk = lambda c, s=sl: self.unT[s][:, c * 128:(c + 1) * 128]
            k_pe, t_kev = self.emit_proj(lhsk, self.w1, t_unT_ev, w1_tok,
                                         out_bf=self.buf_a, out_war=bufa_free)
            w1_last = k_pe
            kt_pend = (e, t_kev)
            unT_free[sl] = [t_unT_pe, k_pe]
            v_pend = (e, sl, t_unT_ev)
        ke, t_kev = kt_pend
        t_kr = self.emit_rope(self.buf_a, ke, t_kev)
        rope_prev = t_kr
        t_kt_pe, t_ktev = self.emit_transposes(self.buf_a, self.kvo, t_kr,
                                               kvo_free)
        bufa_free = t_kt_pe
        P.store(self.kTs[ke], self.kvo[:], waits=[t_ktev])
        t_ksf = P.sfence(self)
        kvo_free = [t_ksf]
        ve, vsl, vtok = v_pend
        lhsv = lambda c, s=vsl: self.unT[s][:, c * 128:(c + 1) * 128]
        v_pe, t_vev = self.emit_proj(lhsv, self.w2, vtok, w2_tok,
                                     out_bf=self.kvo, out_war=kvo_free)
        w2_last = v_pe
        unT_free[vsl] = unT_free[vsl] + [v_pe]
        P.store(self.vss[ve], self.kvo[:], waits=[t_vev])
        kv_stored[ve] = P.sfence(self)
        kvo_free = [kv_stored[ve]]

        w1_tok = self.wload(self.wqd, self.w1, [w1_last])

        ring_free = [None] * NT_EXT
        ring_f = [None] * NT_EXT
        for e2 in range(HALO):
            self._ring_load(1, e2, e2, [kv_stored[e2], w2_last])
        P.load(self.unT[0][:], self.unTs[0],
               waits=[kv_stored[NT_EXT - 1], unT_free[0]])
        P.load(self.unT[1][:], self.unTs[1], waits=[unT_free[1]])
        lfB1 = P.lfence(self)
        for e2 in range(HALO):
            ring_f[e2] = lfB1
        unTl_f = [None] * NT_OWN
        unTl_f[0] = unTl_f[1] = lfB1

        qT_src_tok = [None] * NT_OWN
        s_last = None
        qT_store = [None] * NT_OWN
        sf1 = [None] * NT_OWN
        attn_prev = None
        self.p_free = None
        self.pt_free = None

        q_pe_by_slot = [None, None]
        for i in range(NT_OWN):
            sl = i % 2
            lhsq = lambda c, s=sl: self.unT[s][:, c * 128:(c + 1) * 128]
            q_pe, t_qev = self.emit_proj(lhsq, self.w1, unTl_f[i], w1_tok,
                                         out_bf=self.buf_a, out_war=bufa_free,
                                         evac_scale=SCALE_Q)
            q_pe_by_slot[sl] = q_pe
            if i + 2 < NT_OWN:
                P.load(self.unT[sl][:], self.unTs[i + 2], waits=[q_pe])
            if i + HALO < NT_EXT:
                self._ring_load(1, i + HALO, (i + HALO) % RING,
                                [ring_free[i - 2] if i >= 2 else None])
            lf1 = P.lfence(self)
            if i + 2 < NT_OWN:
                unTl_f[i + 2] = lf1
            if i + HALO < NT_EXT:
                ring_f[i + HALO] = lf1
            if attn_prev is not None:
                ip = attn_prev
                t_avp, evsp = self.emit_attn_tile(
                    1, ip, self.kvo[:, 0:1024], qT_src_tok[ip],
                    ring_f[ip + HALO], 0, sf1[ip] if ip >= 1 else None)
                ring_free[ip] = t_avp
                P.store(self.oTs[ip], self.oT_sb[:, 0:1024], waits=[evsp[-1]])
                s_last = t_avp
            t_qr = self.emit_rope(self.buf_a, i + HALO, t_qev)
            t_qt_pe, t_qtev = self.emit_transposes(
                self.buf_a, self.kvo, t_qr,
                [s_last, sf1[i - 1] if i >= 1 else kv_stored[NT_EXT - 1]])
            bufa_free = t_qt_pe
            qT_src_tok[i] = t_qtev
            qT_store[i] = P.store(self.qTs[i], self.kvo[:, 1024:2048],
                                  waits=[t_qtev])
            sf1[i] = P.sfence(self)
            attn_prev = i
        ip = attn_prev
        t_avp, evsp = self.emit_attn_tile(1, ip, self.kvo[:, 0:1024],
                                          qT_src_tok[ip], ring_f[ip + HALO],
                                          0, sf1[ip])
        ring_free[ip] = t_avp
        P.store(self.oTs[ip], self.oT_sb[:, 0:1024], waits=[evsp[-1]])
        sf1_final = P.sfence(self)
        b1_ring_last = t_avp
        b1_q_last = q_pe_by_slot[1]

        w2_tok = self.wload(self.wod, self.w2, [b1_ring_last])

        ring_free2 = [None] * NT_EXT
        ring_f2 = [None] * NT_EXT
        for e2 in range(HALO + 1):
            self._ring_load(2, e2, e2, [kv_stored[e2], b1_q_last])
        u_ld = [None] * NT_OWN
        qT_ld = [None] * NT_OWN
        oT_ld = [None] * NT_OWN
        y_stored = [None] * NT_OWN
        P.load(self.xu[0][:], self.us_[0], waits=[xu_free[0], sf1_final])
        P.load(self.xu[1][:], self.us_[1], waits=[xu_free[1]])
        P.load(self.un_sb[:, 0:1024], self.qTs[0], waits=[un_free])
        P.load(self.un_sb[:, 1024:2048], self.qTs[1])
        P.load(self.unT[0][:, 0:1024], self.oTs[0], waits=[q_pe_by_slot[0]])
        P.load(self.unT[1][:, 0:1024], self.oTs[1], waits=[q_pe_by_slot[1]])
        lfB2 = P.lfence(self)
        for e2 in range(HALO + 1):
            ring_f2[e2] = lfB2
        u_ld[0] = u_ld[1] = lfB2
        qT_ld[0] = qT_ld[1] = lfB2
        oT_ld[0] = oT_ld[1] = lfB2

        self.p_free = None
        self.pt_free = None
        oT_war2 = None
        ffn_pend = None
        ln_war2 = [None, None]
        gate_war2 = [None, None]
        self.xnT_free2 = [None, None]

        for i in range(NT_OWN):
            sl = i % 2
            if i + 1 + HALO < NT_EXT:
                self._ring_load(2, i + 1 + HALO, (i + 1 + HALO) % RING,
                                [ring_free2[i - 1] if i >= 1 else None])
            if ffn_pend is not None:
                ip2, slp, t_anp = ffn_pend
                t_yf = self._b2_tail(ip2, slp, t_anp, gate_war2)
                y_stored[ip2] = t_yf
                xu_free[slp] = [t_yf]
                if ip2 + 2 < NT_OWN:
                    P.load(self.xu[slp][:], self.us_[ip2 + 2], waits=[t_yf])
                ffn_pend = None
            t_av2, evs2 = self.emit_attn_tile(
                2, i, self.un_sb[:, sl * 1024:(sl + 1) * 1024], qT_ld[i],
                ring_f2[i + HALO], 1024, oT_war2)
            ring_free2[i] = t_av2
            if i + 2 < NT_OWN:
                P.load(self.un_sb[:, sl * 1024:(sl + 1) * 1024],
                       self.qTs[i + 2], waits=[t_av2])
            def lhso(c, s=sl):
                if c < 8:
                    return self.unT[s][:, c * 128:(c + 1) * 128]
                return self.oT_sb[:, 1024 + (c - 8) * 128:1024 + (c - 7) * 128]
            o_pe, t_a = self.emit_proj(lhso, self.w2,
                                       [oT_ld[i], evs2[-1]], w2_tok,
                                       x_dst=self.xu[sl], x_tok=u_ld[i])
            oT_war2 = o_pe
            if i + 2 < NT_OWN:
                P.load(self.unT[sl][:, 0:1024], self.oTs[i + 2], waits=[o_pe])
            lf2 = P.lfence(self)
            if i + 2 < NT_OWN:
                qT_ld[i + 2] = lf2
                oT_ld[i + 2] = lf2
            if i + 1 < NT_OWN:
                u_ld[i + 1] = lf2 if u_ld[i + 1] is None else u_ld[i + 1]
            if i + 1 + HALO < NT_EXT:
                ring_f2[i + 1 + HALO] = lf2
            t_an, _ = self.emit_ln(self.xu[sl], self.xnT[sl], sl, t_a,
                                   self.xnT_free2[sl], ln_war2[sl])
            ln_war2[sl] = t_an
            ffn_pend = (i, sl, t_an)
        ip2, slp, t_anp = ffn_pend
        y_stored[ip2] = self._b2_tail(ip2, slp, t_anp, gate_war2)

        total = P.cnt["qst"]
        P.op("st", lambda e: e.nop(), waits=[("qst", total)])

        nc = self.nc
        with ExitStack() as bctx:
            sems = {}
            for s in Prog.SEMS:
                sems[s] = bctx.enter_context(nc.semaphore(s))
            blk = bctx.enter_context(nc.Block())

            @blk.tensor
            def _(e):
                P.replay("pe", e, sems)

            @blk.vector
            def _(e):
                P.replay("dv", e, sems)

            @blk.scalar
            def _(e):
                P.replay("ac", e, sems)

            @blk.sync
            def _(e):
                P.replay("ld", e, sems)

            @blk.gpsimd
            def _(e):
                P.replay("st", e, sems)
        self.ctx.close()
        return nc

    def _b2_tail(self, i, sl, t_an, gate_war2):
        P = self.P
        t_anT_pe, t_anT_ev = self.emit_transposes(self.xnT[sl], self.xnT[sl],
                                                  t_an, None)
        t_y, t_g2 = self.emit_ffn(self.xnT[sl], t_anT_ev, self.xu[sl],
                                  t_an, sl, gate_war2[sl])
        self.xnT_free2[sl] = t_g2
        gate_war2[sl] = t_y
        P.store(self.yd[i], self.xu[sl][:], waits=[t_y])
        return P.sfence(self)


def kernel(L0_attn_hidden_states, L0_attn_wq, L0_attn_wk, L0_attn_wv,
           L0_attn_wo, L0_ffn_gate_w, L0_ffn_lora_A, L0_ffn_lora_B):
    x = np.asarray(L0_attn_hidden_states, np.float32)
    in_maps = _host_prepare(x, L0_attn_wq, L0_attn_wk, L0_attn_wv, L0_attn_wo,
                            L0_ffn_gate_w, L0_ffn_lora_A, L0_ffn_lora_B)
    k = Kern()
    nc = k.build()
    res = run_bass_kernel_spmd(nc, in_maps, core_ids=list(range(8)))
    out = np.empty((BSZ, SEQ, D), np.float32)
    for c in range(8):
        b, half = c // 2, c % 2
        y = np.asarray(res.results[c]["y"], np.float32).reshape(2048, D)
        out[b, half * 2048:(half + 1) * 2048, :] = y
    return out


# version 11
# speedup vs baseline: 1.3672x; 1.0255x over previous
import numpy as np
import ml_dtypes
import concourse.bass as bass
import concourse.mybir as mybir
from concourse.bass_utils import run_bass_kernel_spmd
from concourse.ap import AP
from contextlib import ExitStack

F32 = mybir.dt.float32
BF16 = mybir.dt.bfloat16
AX = mybir.AxisListType.X
ALU = mybir.AluOpType
AF = mybir.ActivationFunctionType

BSZ, SEQ, D = 4, 4096, 2048
NH, HD, WIN = 16, 128, 1024
NT_OWN, HALO = 16, 8
NT_EXT = NT_OWN + HALO
RING = 10
NBLK = 9
NCH = 16
SCALE_Q = float(128.0 ** -0.5)
NEG = -1.0e30
ER = 128
GAW = 8 + ER
SM_SHIFT = 24.0

DEBUG = False


def bf(x):
    return np.asarray(x, dtype=ml_dtypes.bfloat16)



PARANOID = False


class Prog:
    SEMS = ("qls", "qst", "qlf", "qsf", "pe", "dv", "ac", "po")
    DMA_SEMS = ("qls", "qst", "qlf", "qsf")
    OWNER = {"pe": "pe", "dv": "dv", "ac": "ac", "po": "st", "qls": None,
             "qst": None, "qlf": None, "qsf": None}

    def __init__(self):
        self.stream = {k: [] for k in ("pe", "dv", "ac", "ld", "st")}
        self.cnt = {s: 0 for s in self.SEMS}

    def op(self, eng, fn, waits=(), inc=None, drain=False):
        w = []
        for x in waits:
            if x is None:
                continue
            if isinstance(x, list):
                w += [t for t in x if t is not None]
            else:
                w.append(x)
        if PARANOID:
            w += [(s, self.cnt[s]) for s in ("pe", "dv", "ac")]
        tok = None
        amt = 0
        if inc is not None:
            amt = 16 if inc in self.DMA_SEMS else 1
            self.cnt[inc] += amt
            tok = (inc, self.cnt[inc])
        self.stream[eng].append((fn, w, inc, amt, drain))
        return tok

    def load(self, out, in_, waits=()):
        return self.op("ld", lambda e, o=out, i=in_: e.dma_start(out=o, in_=i),
                       waits=waits, inc="qls")

    def store(self, out, in_, waits=()):
        return self.op("st", lambda e, o=out, i=in_: e.dma_start(out=o, in_=i),
                       waits=waits, inc="qst")

    def lfence(self, kern):
        return self.op("ld", lambda e: e.dma_start(out=kern.fence_sb[:],
                                                   in_=kern.fin_d[:]),
                       inc="qlf")

    def sfence(self, kern):
        return self.op("st", lambda e: e.dma_start(out=kern.fout_d[:],
                                                   in_=kern.fence_sb[:]),
                       inc="qsf")

    def mm(self, out, lhsT, rhs, start, stop, waits=(), inc=False):
        return self.op(
            "pe",
            lambda e, o=out, l=lhsT, r=rhs, s0=start, s1=stop: e.matmul(
                o, l, r, start=s0, stop=s1),
            waits=waits, inc="pe" if inc else None)

    def replay(self, ename, eng, sems):
        seen = {}
        for fn, waits, inc, amt, drain in self.stream[ename]:
            best = {}
            for (s, v) in waits:
                if Prog.OWNER[s] == ename:
                    continue
                if best.get(s, 0) < v:
                    best[s] = v
            for s, v in best.items():
                if seen.get(s, 0) >= v:
                    continue
                eng.wait_ge(sems[s], v)
                seen[s] = v
            inst = fn(eng)
            if drain:
                inst = eng.drain()
            if inc is not None:
                inst.then_inc(sems[inc], amt)


class PsumBlock:
    def __init__(self, alloc, banks, waits):
        self.a = alloc
        self.banks = banks
        self.waits = waits
        self.c0 = banks[0] * 512

    def ap(self, col0, col1):
        return self.a.ps[:, self.c0 + col0:self.c0 + col1]

    def free(self, *toks):
        toks = [t for t in toks if t is not None]
        for b in self.banks:
            self.a.tok[b] = toks


class PsumAlloc:
    def __init__(self, ps):
        self.ps = ps
        self.tok = [[] for _ in range(8)]
        self.ptr = 0

    def alloc(self, nbanks):
        if self.ptr + nbanks > 8:
            self.ptr = 0
        b0 = self.ptr
        self.ptr = (self.ptr + nbanks) % 8
        banks = list(range(b0, b0 + nbanks))
        waits = []
        for b in banks:
            waits += self.tok[b]
        return PsumBlock(self, banks, waits)



def _rope_tables(positions):
    ntiles = len(positions) // 128
    inv = 1.0 / (10000.0 ** (np.arange(0, HD, 2, dtype=np.float64) / HD))
    pos = np.asarray(positions, np.float64)
    fr = pos[:, None] * inv[None, :]
    c = np.cos(fr).reshape(ntiles, 128, 64).transpose(1, 0, 2).reshape(
        128, ntiles * 64)
    s = (-np.sin(fr)).reshape(ntiles, 128, 64).transpose(1, 0, 2).reshape(
        128, ntiles * 64)
    return bf(c), bf(s)


def _host_prepare(x, wq, wk, wv, wo, gate_w, lora_A, lora_B):
    qq = np.arange(128)[:, None]
    kk = np.arange(128)[None, :]
    m_w = bf(np.where(kk > qq, 0.0, NEG))
    m_c = bf(np.where(kk <= qq, 0.0, NEG))
    ident = bf(np.eye(128))
    gA = np.ascontiguousarray(
        np.concatenate([np.asarray(gate_w).T.astype(np.float32, copy=False),
                        np.asarray(lora_A).reshape(ER, D).T.astype(
                            np.float32, copy=False)], axis=1))
    bstack = np.ascontiguousarray(
        np.asarray(lora_B).transpose(0, 2, 1).reshape(ER, D).astype(
            np.float32, copy=False))
    shared = dict(wq=np.asarray(wq, np.float32), wk=np.asarray(wk, np.float32),
                  wv=np.asarray(wv, np.float32), wo=np.asarray(wo, np.float32),
                  gA=gA, bstack=bstack, ident=ident, m_w=m_w, m_c=m_c,
                  ident_f32=np.eye(128, dtype=np.float32))
    x = np.asarray(x, np.float32)
    in_maps = []
    for b in range(BSZ):
        for half in range(2):
            if half == 0:
                xe = np.concatenate(
                    [np.zeros((HALO * 128, D), np.float32), x[b, 0:2048]], axis=0)
                ext_pos = np.concatenate([np.zeros(HALO * 128), np.arange(0, 2048)])
            else:
                xe = np.ascontiguousarray(x[b, 1024:4096])
                ext_pos = np.arange(1024, 4096)
            c2h, s2h = _rope_tables(ext_pos)
            npad = np.zeros((128, NT_OWN), np.float32)
            if half == 0:
                for i in range(NT_OWN):
                    npb = max(0, HALO - i)
                    if npb == 0:
                        continue
                    cnt = np.full(128, 128.0 * (npb - 1))
                    cnt += 127.0 - np.arange(128)
                    npad[:, i] = cnt * np.exp(-SM_SHIFT)
            m = dict(shared)
            m.update(x_ext=np.ascontiguousarray(xe.reshape(NT_EXT, 128, D)),
                     c2=c2h, s2=s2h, npad=npad,
                     fence_in=np.zeros((128, 16), np.float32))
            in_maps.append(m)
    return in_maps



class Kern:
    def __init__(self):
        self.nc = bass.Bass("TRN2", target_bir_lowering=False)
        self.ctx = ExitStack()
        self.P = Prog()
        self._declare()
        self.pa = PsumAlloc(self.ps_all)

    def _declare(self):
        nc = self.nc
        dt = lambda n, s, d, k: nc.dram_tensor(n, s, d, kind=k).ap()
        self.xd = dt("x_ext", [NT_EXT, 128, D], F32, "ExternalInput")
        self.wqd = dt("wq", [D, D], F32, "ExternalInput")
        self.wkd = dt("wk", [D, D], F32, "ExternalInput")
        self.wvd = dt("wv", [D, D], F32, "ExternalInput")
        self.wod = dt("wo", [D, D], F32, "ExternalInput")
        self.gAd = dt("gA", [D, GAW], F32, "ExternalInput")
        self.bsd = dt("bstack", [ER, D], F32, "ExternalInput")
        self.c2d = dt("c2", [128, NT_EXT * 64], BF16, "ExternalInput")
        self.s2d = dt("s2", [128, NT_EXT * 64], BF16, "ExternalInput")
        self.npd = dt("npad", [128, NT_OWN], F32, "ExternalInput")
        self.idd = dt("ident", [128, 128], BF16, "ExternalInput")
        self.mwd = dt("m_w", [128, 128], BF16, "ExternalInput")
        self.mcd = dt("m_c", [128, 128], BF16, "ExternalInput")
        self.fin_d = dt("fence_in", [128, 16], F32, "ExternalInput")
        self.idfd = dt("ident_f32", [128, 128], F32, "ExternalInput")
        self.fout_d = dt("fence_out", [128, 16], F32, "Internal")
        self.yd = dt("y", [NT_OWN, 128, D], F32, "ExternalOutput")
        SK = "ExternalOutput" if DEBUG else "Internal"
        self.kTs = dt("kT_s", [NT_EXT, 128, D], BF16, SK)
        self.vss = dt("v_s", [NT_EXT, 128, D], BF16, SK)
        self.us_ = dt("u_s", [NT_OWN, 128, D], F32, SK)
        self.unTs = dt("unT_s", [NT_OWN, 128, D], BF16, SK)
        self.qTs = dt("qT_s", [NT_OWN, 128, 1024], BF16, SK)
        self.oTs = dt("oT_s", [NT_OWN, 128, 1024], BF16, SK)

        sb = lambda n, s, d: self.ctx.enter_context(nc.sbuf_tensor(n, s, d))
        self.fence_sb = sb("fence_sb", [128, 16], F32)
        self.id_sb = sb("id_sb", [128, 128], BF16)
        self.idf_sb = sb("idf_sb", [128, 128], F32)
        self.diag = [sb("diag0", [128, 128], BF16), sb("diag1", [128, 128], BF16)]
        self.mw_sb = sb("mw_sb", [128, 128], BF16)
        self.mc_sb = sb("mc_sb", [128, 128], BF16)
        self.np_sb = sb("np_sb", [128, NT_OWN], F32)
        self.c2_sb = sb("c2_sb", [128, NT_EXT * 64], BF16)
        self.s2_sb = sb("s2_sb", [128, NT_EXT * 64], BF16)
        self.gA_sb = sb("gA_sb", [128, NCH * GAW], BF16)
        self.bs_sb = sb("bs_sb", [128, D], BF16)
        self.w1 = sb("w1", [128, NCH * D], BF16)
        self.w2 = sb("w2", [128, NCH * D], BF16)
        self.xu = [sb("xu0", [128, D], F32), sb("xu1", [128, D], F32)]
        self.xnT = [sb("xnT0", [128, D], BF16),
                    sb("xnT1", [128, D], BF16)]
        self.un_sb = sb("un_sb", [128, D], BF16)
        self.unT = [sb("unT0", [128, D], BF16), sb("unT1", [128, D], BF16)]
        self.buf_a = sb("buf_a", [128, D], BF16)
        self.buf_b = sb("buf_b", [128, D], BF16)
        self.kvo = sb("kvo", [128, D], BF16)
        self.p_sb = [sb("p_sb0", [128, NBLK * 128], BF16),
                     sb("p_sb1", [128, NBLK * 128], BF16)]
        self.pt_sb = sb("pt_sb", [128, NBLK * 128], BF16)
        self.oT_sb = sb("oT_sb", [128, D], BF16)
        mk2 = lambda n, w: [sb(n + "0", [128, w], F32), sb(n + "1", [128, w], F32)]
        self.st1 = mk2("st1", 1)
        self.ssq = mk2("ssq", 1)
        self.mean = mk2("mean", 1)
        self.nmean = mk2("nmean", 1)
        self.msq = mk2("msq", 1)
        self.var = mk2("var", 1)
        self.lnv = mk2("lnv", 1)
        self.rstd = mk2("rstd", 1)
        self.g8a = mk2("g8a", 8)
        self.g8b = mk2("g8b", 8)
        self.gm1 = mk2("gm1", 1)
        self.gm2 = mk2("gm2", 1)
        self.gdl = mk2("gdl", 1)
        self.ge = mk2("ge", 1)
        self.gden = mk2("gden", 1)
        self.grcp = mk2("grcp", 1)
        self.comb8 = mk2("comb8", 8)
        self.comb_er = [sb("comber", [128, ER], F32)] * 2
        self.ch_sb = [sb("ch0", [128, ER], BF16), sb("ch1", [128, ER], BF16)]
        self.chT_sb = [sb("chT0", [128, 128], BF16), sb("chT1", [128, 128], BF16)]
        self.nm_all = sb("nm_all", [128, 8], F32)
        self.sum_all = sb("sum_all", [128, 8], F32)
        self.enm = sb("enm", [128, 1], F32)
        self.corr = sb("corr", [128, 8], F32)
        self.rcp_s = sb("rcp_s", [128, 8], F32)
        self.ps_all = self.ctx.enter_context(
            nc.psum_tensor("ps_all", [128, 4096], F32))

    def ring_k(self, phase, slot):
        a = self.w2 if phase == 1 else self.w1
        return a[:, slot * 1024:(slot + 1) * 1024]

    def ring_v(self, phase, slot):
        a = self.w2 if phase == 1 else self.w1
        return a[:, (RING + slot) * 1024:(RING + slot + 1) * 1024]

    def _ring_load(self, phase, e, slot, waits):
        P = self.P
        cols = slice(0, 1024) if phase == 1 else slice(1024, 2048)
        P.load(self.ring_k(phase, slot), self.kTs[e][:, cols], waits=waits)
        return P.load(self.ring_v(phase, slot), self.vss[e][:, cols], waits=())

    def wload(self, wdram, dest, waits):
        P = self.P
        for g in range(4):
            src = wdram[g * 512:(g + 1) * 512, :].rearrange(
                "(c p) d -> p c d", p=128)
            dst = dest[:, g * 4 * D:(g + 1) * 4 * D].rearrange(
                "p (c d) -> p c d", c=4)
            P.op("st", lambda e, o=dst, i=src: e.dma_start(out=o, in_=i),
                 waits=waits, inc="qst")
        return P.sfence(self)

    def emit_ln(self, x_src, out_bf, par, ld_tok, out_war, stats_war,
                sq_war=None):
        P = self.P
        s = lambda b: b[par]
        t_s1 = P.op("dv", lambda e: e.reduce_sum(s(self.st1)[:], x_src[:],
                                                 axis=AX),
                    waits=[ld_tok, stats_war], drain=True, inc="dv")
        t_sq = P.op("ac", lambda e: e.activation(
            self.buf_b[:], x_src[:], AF.Square, accum_out=s(self.ssq)[:, 0:1]),
            waits=[ld_tok, stats_war, sq_war], drain=True, inc="ac")
        P.op("st", lambda e: e.tensor_scalar(
            s(self.mean)[:], s(self.st1)[:], 1.0 / D, None, ALU.mult),
            waits=[t_s1, stats_war])
        P.op("st", lambda e: e.tensor_scalar(
            s(self.nmean)[:], s(self.mean)[:], -1.0, None, ALU.mult))
        P.op("st", lambda e: e.tensor_mul(
            s(self.msq)[:], s(self.mean)[:], s(self.mean)[:]))
        P.op("st", lambda e: e.tensor_scalar(
            s(self.var)[:], s(self.ssq)[:], 1.0 / D, 1e-5, ALU.mult, ALU.add),
            waits=[t_sq])
        t_po = P.op("st", lambda e: e.tensor_sub(
            s(self.var)[:], s(self.var)[:], s(self.msq)[:]), inc="po")
        P.op("ac", lambda e: e.activation(
            s(self.lnv)[:], s(self.var)[:, 0:1], AF.Ln),
            waits=[t_po], drain=True)
        t_r = P.op("ac", lambda e: e.activation(
            s(self.rstd)[:], s(self.lnv)[:, 0:1], AF.Exp, scale=-0.5),
            drain=True, inc="ac")
        t_ap = P.op("dv", lambda e: e.tensor_scalar(
            out_bf[:], x_src[:], s(self.nmean)[:, 0:1], s(self.rstd)[:, 0:1],
            ALU.add, ALU.mult), waits=[t_r, t_po, out_war], inc="dv")
        return t_ap, t_sq

    def emit_transposes(self, src_bf, dst_bf, src_tok, dst_war):
        P = self.P
        pe_last = None
        blks = []
        for half in range(2):
            blk = self.pa.alloc(2)
            t_mm = None
            for c in range(8):
                cc = half * 8 + c
                t_mm = P.mm(blk.ap(c * 128, (c + 1) * 128),
                            src_bf[:, cc * 128:(cc + 1) * 128], self.id_sb[:],
                            True, True,
                            waits=([src_tok, self.t_id] + blk.waits)
                            if c == 0 else (),
                            inc=(c == 7))
            blks.append((blk, t_mm))
            pe_last = t_mm
        ev = None
        for half, (blk, t_mm) in enumerate(blks):
            ev = P.op("ac", lambda e, h=half, b=blk: e.activation(
                dst_bf[:, h * 1024:(h + 1) * 1024], b.ap(0, 1024), AF.Copy),
                waits=[t_mm, dst_war], inc="ac" if half == 1 else None)
        for blk, _ in blks:
            blk.free(ev)
        return pe_last, ev

    def emit_proj(self, lhs_fn, rhs_sb, lhs_tok, rhs_tok, out_bf=None,
                  out_war=None, x_dst=None, x_tok=None, evac_scale=None):
        P = self.P
        pe_last = None
        blks = []
        for half in range(2):
            blk = self.pa.alloc(2)
            t_mm = None
            for c in range(NCH):
                for s2 in range(2):
                    col = half * 1024 + s2 * 512
                    t_mm = P.mm(
                        blk.ap(s2 * 512, (s2 + 1) * 512), lhs_fn(c),
                        rhs_sb[:, c * D + col:c * D + col + 512],
                        start=(c == 0), stop=(c == NCH - 1),
                        waits=([lhs_tok, rhs_tok] + blk.waits)
                        if (c == 0 and s2 == 0) else (),
                        inc=(c == NCH - 1 and s2 == 1))
            blks.append((blk, t_mm))
            pe_last = t_mm
        eng = "dv" if x_dst is not None else "ac"
        out_tok = None
        for half, (blk, t_mm) in enumerate(blks):
            kinc = eng if half == 1 else None
            if x_dst is not None:
                out_tok = P.op("dv", lambda e, h=half, b=blk: e.tensor_add(
                    x_dst[:, h * 1024:(h + 1) * 1024],
                    x_dst[:, h * 1024:(h + 1) * 1024], b.ap(0, 1024)),
                    waits=[t_mm, x_tok], inc=kinc)
            else:
                kw = {} if evac_scale is None else {"scale": evac_scale}
                out_tok = P.op("ac", lambda e, h=half, b=blk, kw=kw: e.activation(
                    out_bf[:, h * 1024:(h + 1) * 1024], b.ap(0, 1024),
                    AF.Copy, **kw),
                    waits=[t_mm, out_war], inc=kinc)
        for blk, _ in blks:
            blk.free(out_tok)
        return pe_last, out_tok

    def emit_rope(self, src_bf, tile_e, src_tok):
        P = self.P
        off = tile_e * 64
        csl = self.c2_sb[:, off:off + 64]
        ssl = self.s2_sb[:, off:off + 64]
        cbc = AP(csl.tensor, csl.offset,
                 [list(csl.ap[0]), [0, NH], [0, 2], [1, 64]])
        sbc = AP(ssl.tensor, ssl.offset,
                 [list(ssl.ap[0]), [0, NH], [0, 2], [1, 64]])
        sv = src_bf[:]
        rot = AP(sv.tensor, sv.offset + 64,
                 [list(sv.ap[0]), [128, NH], [-64, 2], [1, 64]])
        t4 = lambda t: t[:].rearrange("p (h x j) -> p h x j", h=NH, x=2)
        P.op("dv", lambda e: e.tensor_tensor(
            t4(self.buf_b), t4(src_bf), cbc, ALU.mult),
            waits=[src_tok, self.t_tab])
        P.op("dv", lambda e: e.tensor_tensor(t4(self.un_sb), rot, sbc, ALU.mult))
        fh = lambda t: AP(t[:].tensor, t[:].offset,
                          [list(t[:].ap[0]), [128, NH], [1, 64]])
        sh = lambda t: AP(t[:].tensor, t[:].offset + 64,
                          [list(t[:].ap[0]), [128, NH], [1, 64]])
        P.op("dv", lambda e: e.tensor_tensor(
            fh(src_bf), fh(self.buf_b), fh(self.un_sb), ALU.add))
        return P.op("dv", lambda e: e.tensor_tensor(
            sh(src_bf), sh(self.buf_b), sh(self.un_sb), ALU.subtract),
            drain=True, inc="dv")

    def emit_ffn(self, actT_sb, actT_tok, x_dst, x_tok, par, war):
        P = self.P
        s = lambda b: b[par]
        gblk = self.pa.alloc(1)
        t_g = None
        for c in range(NCH):
            t_g = P.mm(gblk.ap(0, GAW), actT_sb[:, c * 128:(c + 1) * 128],
                       self.gA_sb[:, c * GAW:(c + 1) * GAW],
                       start=(c == 0), stop=(c == NCH - 1),
                       waits=([actT_tok, self.t_gA] + gblk.waits)
                       if c == 0 else (),
                       inc=(c == NCH - 1))
        lg = gblk.ap(0, 8)
        hh = gblk.ap(8, GAW)
        P.op("dv", lambda e: e.reduce_max(s(self.gm1)[:], lg, axis=AX),
             waits=[t_g, war], drain=True)
        P.op("dv", lambda e: e.tensor_single_scalar(
            s(self.g8a)[:], lg, s(self.gm1)[:, 0:1], ALU.is_ge), drain=True)
        P.op("dv", lambda e: e.scalar_tensor_tensor(
            s(self.g8b)[:], s(self.g8a)[:], NEG, lg, ALU.mult, ALU.add),
            drain=True)
        P.op("dv", lambda e: e.reduce_max(s(self.gm2)[:], s(self.g8b)[:], axis=AX),
             drain=True)
        P.op("dv", lambda e: e.tensor_single_scalar(
            s(self.g8b)[:], s(self.g8b)[:], s(self.gm2)[:, 0:1], ALU.is_ge),
            drain=True)
        t_d = P.op("dv", lambda e: e.tensor_sub(
            s(self.gdl)[:], s(self.gm2)[:], s(self.gm1)[:]), drain=True, inc="dv")
        t_e = P.op("ac", lambda e: e.activation(
            s(self.ge)[:], s(self.gdl)[:, 0:1], AF.Exp),
            waits=[t_d], drain=True, inc="ac")
        P.op("dv", lambda e: e.tensor_scalar(
            s(self.gden)[:], s(self.ge)[:], 1.0, None, ALU.add),
            waits=[t_e], drain=True)
        P.op("dv", lambda e: e.reciprocal(s(self.grcp)[:], s(self.gden)[:]),
             drain=True)
        P.op("dv", lambda e: e.scalar_tensor_tensor(
            s(self.comb8)[:], s(self.g8b)[:], s(self.ge)[:, 0:1], s(self.g8a)[:],
            ALU.mult, ALU.add), drain=True)
        P.op("dv", lambda e: e.tensor_single_scalar(
            s(self.comb8)[:], s(self.comb8)[:], s(self.grcp)[:, 0:1], ALU.mult),
            drain=True)
        c8 = s(self.comb8)[:]
        cbc = AP(c8.tensor, c8.offset, [list(c8.ap[0]), [1, 8], [0, 16]])
        P.op("dv", lambda e: e.tensor_copy(
            s(self.comb_er)[:].rearrange("p (x r) -> p x r", x=8), cbc),
            drain=True)
        t_ch = P.op("dv", lambda e: e.tensor_mul(
            s(self.ch_sb)[:], hh, s(self.comb_er)[:]), inc="dv")
        gblk.free(t_ch)
        cblk = self.pa.alloc(1)
        t_ct = P.mm(cblk.ap(0, 128), s(self.ch_sb)[:], self.id_sb[:], True, True,
                    waits=[t_ch] + cblk.waits, inc=True)
        t_ce = P.op("dv", lambda e, b=cblk: e.tensor_copy(
            s(self.chT_sb)[:], b.ap(0, 128)), waits=[t_ct], inc="dv")
        cblk.free(t_ce)
        blks = []
        for half in range(2):
            blk = self.pa.alloc(2)
            t_mm = None
            for s2 in range(2):
                col = half * 1024 + s2 * 512
                t_mm = P.mm(blk.ap(s2 * 512, (s2 + 1) * 512), s(self.chT_sb)[:],
                            self.bs_sb[:, col:col + 512], True, True,
                            waits=([t_ce] + blk.waits) if s2 == 0 else (),
                            inc=(s2 == 1))
            blks.append((blk, t_mm))
        t_add = None
        for half, (blk, t_mm) in enumerate(blks):
            t_add = P.op("dv", lambda e, h=half, b=blk: e.tensor_add(
                x_dst[:, h * 1024:(h + 1) * 1024],
                x_dst[:, h * 1024:(h + 1) * 1024], b.ap(0, 1024)),
                waits=[t_mm, x_tok], inc="dv" if half == 1 else None)
        for blk, _ in blks:
            blk.free(t_add)
        return t_add, t_g

    def emit_attn_tile(self, phase, i, qsrc, q_tok, ring_tok, oT_off, oT_war):
        P = self.P
        ps = self.ps_all
        SREG = (ps[:, 0:1152], ps[:, 1536:2688])
        SLACK = (ps[:, 1216:1344], ps[:, 2752:2880])
        PTA = ps[:, 3072:4096]
        AVR = ps[:, 3072:3200]
        claim = []
        for b in range(8):
            claim += self.pa.tok[b]
            self.pa.tok[b] = []
        s_tok = [None] * 8
        exp_tok = [None] * 8
        oev_tok = [None] * 8
        pt_tok = [None] * 8
        av_tok = [None] * 8
        peb_tok = [None] * 8
        diag_tok = [None] * 8

        def emit_S(h):
            reg = SREG[h % 2]
            w = [q_tok, ring_tok, self.t_mask, self.t_id, self.t_np]
            if h >= 2:
                w += [exp_tok[h - 2]]
                if h >= 4:
                    w += [peb_tok[h - 4]]
            else:
                w = w + claim
            P.mm(reg[:, 0:128], self.id_sb[:], self.mw_sb[:], True, False,
                 waits=w)
            P.mm(reg[:, 8 * 128:9 * 128], self.id_sb[:], self.mc_sb[:],
                 True, False)
            t = None
            for b in range(NBLK):
                slot = (i + b) % RING
                t = P.mm(reg[:, b * 128:(b + 1) * 128],
                         qsrc[:, h * 128:(h + 1) * 128],
                         self.ring_k(phase, slot)[:, h * 128:(h + 1) * 128],
                         start=(b not in (0, 8)), stop=True,
                         inc=(b == NBLK - 1))
            s_tok[h] = t

        emit_S(0)
        emit_S(1)
        pea_tok = [None] * 8

        rcp_tok = [None] * 8

        def stage_A(h):
            reg = SREG[h % 2]
            t_nm = P.op("dv", lambda e, h=h, r=reg: e.reduce_max(
                self.nm_all[:, 1:2], r[:, 0:128], axis=AX, negate=True),
                waits=[s_tok[h]], drain=True, inc="dv")
            t_exp = P.op("ac", lambda e, h=h, r=reg: e.activation(
                self.p_sb[h % 2][:], r, AF.Exp, bias=self.nm_all[:, 0:1],
                accum_out=self.sum_all[:, h:h + 1]),
                waits=[t_nm, pt_tok[h - 2] if h >= 2 else None,
                       self.t_shift],
                drain=True, inc="ac")
            exp_tok[h] = t_exp
            t_corr = P.op("st", lambda e, h=h, i=i: e.tensor_sub(
                self.corr[:, h:h + 1], self.sum_all[:, h:h + 1],
                self.np_sb[:, i:i + 1]),
                waits=[t_exp], inc="po")
            P.op("dv", lambda e, h=h: e.reciprocal(
                self.rcp_s[:, h:h + 1], self.corr[:, h:h + 1]),
                waits=[t_corr], drain=True, inc="dv")
            rcp_tok[h] = P.op("dv", lambda e, h=h: e.tensor_scalar(
                self.p_sb[h % 2][:], self.p_sb[h % 2][:],
                self.rcp_s[:, h:h + 1], None, ALU.mult), inc="dv")

        stage_A(0)
        stage_A(1)
        for h in range(8):
            dg = self.id_sb
            w0 = [rcp_tok[h]]
            if h >= 1:
                w0 += [pea_tok[h - 1], oev_tok[h - 1]]
            else:
                w0 = w0 + claim
            t_pta = None
            pp = self.p_sb[h % 2]
            for b in range(8):
                t_pta = P.mm(PTA[:, b * 128:(b + 1) * 128],
                             pp[:, b * 128:(b + 1) * 128], dg[:], True, True,
                             waits=w0 if b == 0 else (), inc=(b == 7))
            eng = "ac" if h % 2 == 0 else "dv"
            if eng == "ac":
                pea_tok[h] = P.op("ac", lambda e: e.activation(
                    self.pt_sb[:, 0:1024], PTA, AF.Copy),
                    waits=[t_pta, av_tok[h - 1] if h >= 1 else None], inc="ac")
            else:
                pea_tok[h] = P.op("dv", lambda e: e.tensor_copy(
                    self.pt_sb[:, 0:1024], PTA),
                    waits=[t_pta, av_tok[h - 1] if h >= 1 else None], inc="dv")
            if h + 2 < 8:
                emit_S(h + 2)
            t_ptb = P.mm(SLACK[h % 2], pp[:, 8 * 128:9 * 128], dg[:],
                         True, True,
                         waits=[peb_tok[h - 2]] if h >= 2 else (), inc=True)
            pt_tok[h] = t_ptb
            if eng == "ac":
                peb_tok[h] = P.op("ac", lambda e, h=h: e.activation(
                    self.pt_sb[:, 1024:1152], SLACK[h % 2], AF.Copy),
                    waits=[t_ptb, av_tok[h - 1] if h >= 1 else None], inc="ac")
            else:
                peb_tok[h] = P.op("dv", lambda e, h=h: e.tensor_copy(
                    self.pt_sb[:, 1024:1152], SLACK[h % 2]),
                    waits=[t_ptb, av_tok[h - 1] if h >= 1 else None], inc="dv")
            if h + 2 < 8:
                stage_A(h + 2)
            t_av = None
            for b in range(NBLK):
                slot = (i + b) % RING
                w_av = ()
                if b == 0:
                    w_av = [pea_tok[h]]
                    if h >= 1:
                        w_av += [oev_tok[h - 1]]
                elif b == NBLK - 1:
                    w_av = [peb_tok[h]]
                t_av = P.mm(AVR,
                            self.ring_v(phase, slot)[:, h * 128:(h + 1) * 128],
                            self.pt_sb[:, b * 128:(b + 1) * 128],
                            start=(b == 0), stop=(b == NBLK - 1),
                            waits=w_av, inc=(b == NBLK - 1))
            av_tok[h] = t_av
            oev_tok[h] = P.op("ac", lambda e, h=h, o=oT_off: e.activation(
                self.oT_sb[:, o + h * 128:o + (h + 1) * 128], AVR, AF.Copy),
                waits=[t_av, oT_war], inc="ac")
        for b in (0, 1, 2):
            self.pa.tok[b] = [exp_tok[6], peb_tok[6]]
        for b in (3, 4, 5):
            self.pa.tok[b] = [exp_tok[7], peb_tok[7]]
        for b in (6, 7):
            self.pa.tok[b] = [oev_tok[7], peb_tok[7]]
        self.pa.ptr = 0
        return av_tok[7], oev_tok

    def build(self):
        P = self.P
        P.load(self.fence_sb[:], self.fin_d[:])
        P.load(self.id_sb[:], self.idd[:])
        P.load(self.idf_sb[:], self.idfd[:])
        P.load(self.mw_sb[:], self.mwd[:])
        P.load(self.mc_sb[:], self.mcd[:])
        P.load(self.np_sb[:], self.npd[:])
        P.load(self.c2_sb[:], self.c2d[:])
        P.load(self.s2_sb[:], self.s2d[:])
        self.p_free = None
        self.pt_free = None
        self.diag_war = [None, None]
        xu_free = [[], []]

        src = self.gAd.rearrange("(c p) w -> p c w", p=128)
        dst = self.gA_sb[:].rearrange("p (c w) -> p c w", c=NCH)
        P.op("st", lambda e, o=dst, i=src: e.dma_start(out=o, in_=i),
             inc="qst")
        P.op("st", lambda e: e.dma_start(out=self.bs_sb[:], in_=self.bsd[:]),
             inc="qst")
        self.wload(self.wkd, self.w1, ())
        t_w = self.wload(self.wvd, self.w2, ())
        self.t_gA = t_w
        w1_tok = t_w
        w2_tok = t_w

        x_ld = [None] * NT_EXT
        P.load(self.xu[0][:], self.xd[0])
        P.load(self.xu[1][:], self.xd[1])
        lf0 = P.lfence(self)
        self.t_id = self.t_mask = self.t_np = self.t_tab = lf0
        self.t_shift = P.op("st", lambda e: e.tensor_scalar(
            self.nm_all[:, 0:1], self.np_sb[:, 0:1], 0.0, -SM_SHIFT,
            ALU.mult, ALU.add), waits=[lf0], inc="po")
        x_ld[0] = lf0
        x_ld[1] = lf0

        kv_stored = [None] * NT_EXT
        unT_stored = [None] * NT_OWN
        un_free = None
        unT_free = [[], []]
        ln_war = [None, None]
        gate_war = [None, None]
        bufa_free = None
        kvo_free = []
        v_pend = None
        w1_last = None
        w2_last = None
        rope_prev = None

        xnT_ev = [None] * NT_EXT
        gate_free = [None, None]

        def emit_front_ln(e):
            sl = e % 2
            t_xn, _ = self.emit_ln(self.xu[sl], self.xnT[sl], sl, x_ld[e],
                                   gate_free[sl], ln_war[sl], sq_war=rope_prev)
            ln_war[sl] = t_xn
            return (e, t_xn)

        def emit_front_trs(pend):
            e, t_xn = pend
            sl = e % 2
            t_pe, t_ev = self.emit_transposes(self.xnT[sl], self.xnT[sl],
                                              t_xn, None)
            xnT_ev[e] = t_ev

        ln1_pend = emit_front_ln(0)
        kt_pend = None
        for e in range(NT_EXT):
            sl = e % 2
            emit_front_trs(ln1_pend)
            if kt_pend is not None:
                ke, t_kev = kt_pend
                t_kr = self.emit_rope(self.buf_a, ke, t_kev)
                rope_prev = t_kr
                t_kt_pe, t_ktev = self.emit_transposes(
                    self.buf_a, self.kvo, t_kr, kvo_free)
                bufa_free = t_kt_pe
                P.store(self.kTs[ke], self.kvo[:], waits=[t_ktev])
                t_ksf = P.sfence(self)
                kvo_free = [t_ksf]
                unT_free[ke % 2] = unT_free[ke % 2] + [t_ksf]
                kt_pend = None
            t_u, t_g = self.emit_ffn(self.xnT[sl], xnT_ev[e], self.xu[sl],
                                     x_ld[e], sl, gate_war[sl])
            gate_free[sl] = t_g
            gate_war[sl] = t_u
            t_usf = None
            if e >= HALO:
                P.store(self.us_[e - HALO], self.xu[sl][:], waits=[t_u])
                t_usf = P.sfence(self)
            if v_pend is not None:
                ve, vsl, vtok = v_pend
                lhsv = lambda c, s=vsl: self.unT[s][:, c * 128:(c + 1) * 128]
                v_pe, t_vev = self.emit_proj(lhsv, self.w2, vtok, w2_tok,
                                             out_bf=self.kvo,
                                             out_war=kvo_free)
                w2_last = v_pe
                unT_free[vsl] = unT_free[vsl] + [v_pe]
                P.store(self.vss[ve], self.kvo[:], waits=[t_vev])
                kv_stored[ve] = P.sfence(self)
                kvo_free = [kv_stored[ve]]
                v_pend = None
            t_un, t_sq2 = self.emit_ln(self.xu[sl], self.un_sb, sl, t_u,
                                       un_free, ln_war[sl], sq_war=rope_prev)
            ln_war[sl] = t_un
            xu_free[sl] = [t_un, t_sq2] + ([t_usf] if t_usf else [])
            if e + 2 < NT_EXT:
                P.load(self.xu[sl][:], self.xd[e + 2], waits=[xu_free[sl]])
                x_ld[e + 2] = P.lfence(self)
            t_unT_pe, t_unT_ev = self.emit_transposes(
                self.un_sb, self.unT[sl], t_un, unT_free[sl])
            un_free = t_unT_pe
            if e >= HALO:
                unT_stored[e - HALO] = P.store(self.unTs[e - HALO],
                                               self.unT[sl][:], waits=[t_unT_ev])
            if e + 1 < NT_EXT:
                ln1_pend = emit_front_ln(e + 1)
            lhsk = lambda c, s=sl: self.unT[s][:, c * 128:(c + 1) * 128]
            k_pe, t_kev = self.emit_proj(lhsk, self.w1, t_unT_ev, w1_tok,
                                         out_bf=self.buf_a, out_war=bufa_free)
            w1_last = k_pe
            kt_pend = (e, t_kev)
            unT_free[sl] = [t_unT_pe, k_pe]
            v_pend = (e, sl, t_unT_ev)
        ke, t_kev = kt_pend
        t_kr = self.emit_rope(self.buf_a, ke, t_kev)
        rope_prev = t_kr
        t_kt_pe, t_ktev = self.emit_transposes(self.buf_a, self.kvo, t_kr,
                                               kvo_free)
        bufa_free = t_kt_pe
        P.store(self.kTs[ke], self.kvo[:], waits=[t_ktev])
        t_ksf = P.sfence(self)
        kvo_free = [t_ksf]
        ve, vsl, vtok = v_pend
        lhsv = lambda c, s=vsl: self.unT[s][:, c * 128:(c + 1) * 128]
        v_pe, t_vev = self.emit_proj(lhsv, self.w2, vtok, w2_tok,
                                     out_bf=self.kvo, out_war=kvo_free)
        w2_last = v_pe
        unT_free[vsl] = unT_free[vsl] + [v_pe]
        P.store(self.vss[ve], self.kvo[:], waits=[t_vev])
        kv_stored[ve] = P.sfence(self)
        kvo_free = [kv_stored[ve]]

        w1_tok = self.wload(self.wqd, self.w1, [w1_last])

        ring_free = [None] * NT_EXT
        ring_f = [None] * NT_EXT
        for e2 in range(HALO):
            self._ring_load(1, e2, e2, [kv_stored[e2], w2_last])
        P.load(self.unT[0][:], self.unTs[0],
               waits=[kv_stored[NT_EXT - 1], unT_free[0]])
        P.load(self.unT[1][:], self.unTs[1], waits=[unT_free[1]])
        lfB1 = P.lfence(self)
        for e2 in range(HALO):
            ring_f[e2] = lfB1
        unTl_f = [None] * NT_OWN
        unTl_f[0] = unTl_f[1] = lfB1

        qT_src_tok = [None] * NT_OWN
        s_last = None
        qT_store = [None] * NT_OWN
        sf1 = [None] * NT_OWN
        attn_prev = None
        self.p_free = None
        self.pt_free = None

        q_pe_by_slot = [None, None]
        for i in range(NT_OWN):
            sl = i % 2
            lhsq = lambda c, s=sl: self.unT[s][:, c * 128:(c + 1) * 128]
            q_pe, t_qev = self.emit_proj(lhsq, self.w1, unTl_f[i], w1_tok,
                                         out_bf=self.buf_a, out_war=bufa_free,
                                         evac_scale=SCALE_Q)
            q_pe_by_slot[sl] = q_pe
            if i + 2 < NT_OWN:
                P.load(self.unT[sl][:], self.unTs[i + 2], waits=[q_pe])
            if i + HALO < NT_EXT:
                self._ring_load(1, i + HALO, (i + HALO) % RING,
                                [ring_free[i - 2] if i >= 2 else None])
            lf1 = P.lfence(self)
            if i + 2 < NT_OWN:
                unTl_f[i + 2] = lf1
            if i + HALO < NT_EXT:
                ring_f[i + HALO] = lf1
            if attn_prev is not None:
                ip = attn_prev
                t_avp, evsp = self.emit_attn_tile(
                    1, ip, self.kvo[:, 0:1024], qT_src_tok[ip],
                    ring_f[ip + HALO], 0, sf1[ip] if ip >= 1 else None)
                ring_free[ip] = t_avp
                P.store(self.oTs[ip], self.oT_sb[:, 0:1024], waits=[evsp[-1]])
                s_last = t_avp
            t_qr = self.emit_rope(self.buf_a, i + HALO, t_qev)
            t_qt_pe, t_qtev = self.emit_transposes(
                self.buf_a, self.kvo, t_qr,
                [s_last, sf1[i - 1] if i >= 1 else kv_stored[NT_EXT - 1]])
            bufa_free = t_qt_pe
            qT_src_tok[i] = t_qtev
            qT_store[i] = P.store(self.qTs[i], self.kvo[:, 1024:2048],
                                  waits=[t_qtev])
            sf1[i] = P.sfence(self)
            attn_prev = i
        ip = attn_prev
        t_avp, evsp = self.emit_attn_tile(1, ip, self.kvo[:, 0:1024],
                                          qT_src_tok[ip], ring_f[ip + HALO],
                                          0, sf1[ip])
        ring_free[ip] = t_avp
        P.store(self.oTs[ip], self.oT_sb[:, 0:1024], waits=[evsp[-1]])
        sf1_final = P.sfence(self)
        b1_ring_last = t_avp
        b1_q_last = q_pe_by_slot[1]

        w2_tok = self.wload(self.wod, self.w2, [b1_ring_last])

        ring_free2 = [None] * NT_EXT
        ring_f2 = [None] * NT_EXT
        for e2 in range(HALO + 1):
            self._ring_load(2, e2, e2, [kv_stored[e2], b1_q_last])
        u_ld = [None] * NT_OWN
        qT_ld = [None] * NT_OWN
        oT_ld = [None] * NT_OWN
        y_stored = [None] * NT_OWN
        P.load(self.xu[0][:], self.us_[0], waits=[xu_free[0], sf1_final])
        P.load(self.xu[1][:], self.us_[1], waits=[xu_free[1]])
        P.load(self.un_sb[:, 0:1024], self.qTs[0], waits=[un_free])
        P.load(self.un_sb[:, 1024:2048], self.qTs[1])
        P.load(self.unT[0][:, 0:1024], self.oTs[0], waits=[q_pe_by_slot[0]])
        P.load(self.unT[1][:, 0:1024], self.oTs[1], waits=[q_pe_by_slot[1]])
        lfB2 = P.lfence(self)
        for e2 in range(HALO + 1):
            ring_f2[e2] = lfB2
        u_ld[0] = u_ld[1] = lfB2
        qT_ld[0] = qT_ld[1] = lfB2
        oT_ld[0] = oT_ld[1] = lfB2

        self.p_free = None
        self.pt_free = None
        oT_war2 = None
        ffn_pend = None
        ln_war2 = [None, None]
        gate_war2 = [None, None]
        self.xnT_free2 = [None, None]

        for i in range(NT_OWN):
            sl = i % 2
            if i + 1 + HALO < NT_EXT:
                self._ring_load(2, i + 1 + HALO, (i + 1 + HALO) % RING,
                                [ring_free2[i - 1] if i >= 1 else None])
            if ffn_pend is not None:
                ip2, slp, t_anp = ffn_pend
                t_yf = self._b2_tail(ip2, slp, t_anp, gate_war2)
                y_stored[ip2] = t_yf
                xu_free[slp] = [t_yf]
                if ip2 + 2 < NT_OWN:
                    P.load(self.xu[slp][:], self.us_[ip2 + 2], waits=[t_yf])
                ffn_pend = None
            t_av2, evs2 = self.emit_attn_tile(
                2, i, self.un_sb[:, sl * 1024:(sl + 1) * 1024], qT_ld[i],
                ring_f2[i + HALO], 1024, oT_war2)
            ring_free2[i] = t_av2
            if i + 2 < NT_OWN:
                P.load(self.un_sb[:, sl * 1024:(sl + 1) * 1024],
                       self.qTs[i + 2], waits=[t_av2])
            def lhso(c, s=sl):
                if c < 8:
                    return self.unT[s][:, c * 128:(c + 1) * 128]
                return self.oT_sb[:, 1024 + (c - 8) * 128:1024 + (c - 7) * 128]
            o_pe, t_a = self.emit_proj(lhso, self.w2,
                                       [oT_ld[i], evs2[-1]], w2_tok,
                                       x_dst=self.xu[sl], x_tok=u_ld[i])
            oT_war2 = o_pe
            if i + 2 < NT_OWN:
                P.load(self.unT[sl][:, 0:1024], self.oTs[i + 2], waits=[o_pe])
            lf2 = P.lfence(self)
            if i + 2 < NT_OWN:
                qT_ld[i + 2] = lf2
                oT_ld[i + 2] = lf2
            if i + 1 < NT_OWN:
                u_ld[i + 1] = lf2 if u_ld[i + 1] is None else u_ld[i + 1]
            if i + 1 + HALO < NT_EXT:
                ring_f2[i + 1 + HALO] = lf2
            t_an, _ = self.emit_ln(self.xu[sl], self.xnT[sl], sl, t_a,
                                   self.xnT_free2[sl], ln_war2[sl])
            ln_war2[sl] = t_an
            ffn_pend = (i, sl, t_an)
        ip2, slp, t_anp = ffn_pend
        y_stored[ip2] = self._b2_tail(ip2, slp, t_anp, gate_war2)

        total = P.cnt["qst"]
        P.op("st", lambda e: e.nop(), waits=[("qst", total)])

        nc = self.nc
        with ExitStack() as bctx:
            sems = {}
            for s in Prog.SEMS:
                sems[s] = bctx.enter_context(nc.semaphore(s))
            blk = bctx.enter_context(nc.Block())

            @blk.tensor
            def _(e):
                P.replay("pe", e, sems)

            @blk.vector
            def _(e):
                P.replay("dv", e, sems)

            @blk.scalar
            def _(e):
                P.replay("ac", e, sems)

            @blk.sync
            def _(e):
                P.replay("ld", e, sems)

            @blk.gpsimd
            def _(e):
                P.replay("st", e, sems)
        self.ctx.close()
        return nc

    def _b2_tail(self, i, sl, t_an, gate_war2):
        P = self.P
        t_anT_pe, t_anT_ev = self.emit_transposes(self.xnT[sl], self.xnT[sl],
                                                  t_an, None)
        t_y, t_g2 = self.emit_ffn(self.xnT[sl], t_anT_ev, self.xu[sl],
                                  t_an, sl, gate_war2[sl])
        self.xnT_free2[sl] = t_g2
        gate_war2[sl] = t_y
        P.store(self.yd[i], self.xu[sl][:], waits=[t_y])
        return P.sfence(self)


def kernel(L0_attn_hidden_states, L0_attn_wq, L0_attn_wk, L0_attn_wv,
           L0_attn_wo, L0_ffn_gate_w, L0_ffn_lora_A, L0_ffn_lora_B):
    x = np.asarray(L0_attn_hidden_states, np.float32)
    in_maps = _host_prepare(x, L0_attn_wq, L0_attn_wk, L0_attn_wv, L0_attn_wo,
                            L0_ffn_gate_w, L0_ffn_lora_A, L0_ffn_lora_B)
    k = Kern()
    nc = k.build()
    res = run_bass_kernel_spmd(nc, in_maps, core_ids=list(range(8)))
    out = np.empty((BSZ, SEQ, D), np.float32)
    for c in range(8):
        b, half = c // 2, c % 2
        y = np.asarray(res.results[c]["y"], np.float32).reshape(2048, D)
        out[b, half * 2048:(half + 1) * 2048, :] = y
    return out


# version 12
# speedup vs baseline: 1.4902x; 1.0899x over previous
import numpy as np
import ml_dtypes
import concourse.bass as bass
import concourse.mybir as mybir
from concourse.bass_utils import run_bass_kernel_spmd
from concourse.ap import AP
from contextlib import ExitStack

F32 = mybir.dt.float32
BF16 = mybir.dt.bfloat16
AX = mybir.AxisListType.X
ALU = mybir.AluOpType
AF = mybir.ActivationFunctionType

BSZ, SEQ, D = 4, 4096, 2048
NH, HD, WIN = 16, 128, 1024
NT_OWN, HALO = 16, 8
NT_EXT = NT_OWN + HALO
RING = 10
NBLK = 9
NCH = 16
SCALE_Q = float(128.0 ** -0.5)
NEG = -1.0e30
ER = 128
GAW = 8 + ER
SM_SHIFT = 24.0

DEBUG = False


def bf(x):
    return np.asarray(x, dtype=ml_dtypes.bfloat16)



PARANOID = False


class Prog:
    SEMS = ("qls", "qst", "qlf", "qsf", "pe", "dv", "ac", "po")
    DMA_SEMS = ("qls", "qst", "qlf", "qsf")
    OWNER = {"pe": "pe", "dv": "dv", "ac": "ac", "po": "st", "qls": None,
             "qst": None, "qlf": None, "qsf": None}

    def __init__(self):
        self.stream = {k: [] for k in ("pe", "dv", "ac", "ld", "st")}
        self.cnt = {s: 0 for s in self.SEMS}

    def op(self, eng, fn, waits=(), inc=None, drain=False):
        w = []
        for x in waits:
            if x is None:
                continue
            if isinstance(x, list):
                w += [t for t in x if t is not None]
            else:
                w.append(x)
        if PARANOID:
            w += [(s, self.cnt[s]) for s in ("pe", "dv", "ac")]
        tok = None
        amt = 0
        if inc is not None:
            amt = 16 if inc in self.DMA_SEMS else 1
            self.cnt[inc] += amt
            tok = (inc, self.cnt[inc])
        self.stream[eng].append((fn, w, inc, amt, drain))
        return tok

    def load(self, out, in_, waits=()):
        return self.op("ld", lambda e, o=out, i=in_: e.dma_start(out=o, in_=i),
                       waits=waits, inc="qls")

    def store(self, out, in_, waits=()):
        return self.op("st", lambda e, o=out, i=in_: e.dma_start(out=o, in_=i),
                       waits=waits, inc="qst")

    def lfence(self, kern):
        return self.op("ld", lambda e: e.dma_start(out=kern.fence_sb[:],
                                                   in_=kern.fin_d[:]),
                       inc="qlf")

    def sfence(self, kern):
        return self.op("st", lambda e: e.dma_start(out=kern.fout_d[:],
                                                   in_=kern.fence_sb[:]),
                       inc="qsf")

    def mm(self, out, lhsT, rhs, start, stop, waits=(), inc=False):
        return self.op(
            "pe",
            lambda e, o=out, l=lhsT, r=rhs, s0=start, s1=stop: e.matmul(
                o, l, r, start=s0, stop=s1),
            waits=waits, inc="pe" if inc else None)

    def replay(self, ename, eng, sems):
        seen = {}
        for fn, waits, inc, amt, drain in self.stream[ename]:
            best = {}
            for (s, v) in waits:
                if Prog.OWNER[s] == ename:
                    continue
                if best.get(s, 0) < v:
                    best[s] = v
            for s, v in best.items():
                if seen.get(s, 0) >= v:
                    continue
                eng.wait_ge(sems[s], v)
                seen[s] = v
            inst = fn(eng)
            if drain:
                inst = eng.drain()
            if inc is not None:
                inst.then_inc(sems[inc], amt)


class PsumBlock:
    def __init__(self, alloc, banks, waits):
        self.a = alloc
        self.banks = banks
        self.waits = waits
        self.c0 = banks[0] * 512

    def ap(self, col0, col1):
        return self.a.ps[:, self.c0 + col0:self.c0 + col1]

    def free(self, *toks):
        toks = [t for t in toks if t is not None]
        for b in self.banks:
            self.a.tok[b] = toks


class PsumAlloc:
    def __init__(self, ps):
        self.ps = ps
        self.tok = [[] for _ in range(8)]
        self.ptr = 0

    def alloc(self, nbanks):
        if self.ptr + nbanks > 8:
            self.ptr = 0
        b0 = self.ptr
        self.ptr = (self.ptr + nbanks) % 8
        banks = list(range(b0, b0 + nbanks))
        waits = []
        for b in banks:
            waits += self.tok[b]
        return PsumBlock(self, banks, waits)



def _rope_tables(positions):
    ntiles = len(positions) // 128
    inv = 1.0 / (10000.0 ** (np.arange(0, HD, 2, dtype=np.float64) / HD))
    pos = np.asarray(positions, np.float64)
    fr = pos[:, None] * inv[None, :]
    c = np.cos(fr).reshape(ntiles, 128, 64).transpose(1, 0, 2).reshape(
        128, ntiles * 64)
    s = (-np.sin(fr)).reshape(ntiles, 128, 64).transpose(1, 0, 2).reshape(
        128, ntiles * 64)
    return bf(c), bf(s)


def _host_prepare(x, wq, wk, wv, wo, gate_w, lora_A, lora_B):
    qq = np.arange(128)[:, None]
    kk = np.arange(128)[None, :]
    m_w = bf(np.where(kk > qq, 0.0, NEG))
    m_c = bf(np.where(kk <= qq, 0.0, NEG))
    ident = bf(np.eye(128))
    gA = np.ascontiguousarray(
        np.concatenate([np.asarray(gate_w).T.astype(np.float32, copy=False),
                        np.asarray(lora_A).reshape(ER, D).T.astype(
                            np.float32, copy=False)], axis=1))
    bstack = np.ascontiguousarray(
        np.asarray(lora_B).transpose(0, 2, 1).reshape(ER, D).astype(
            np.float32, copy=False))
    shared = dict(wq=np.asarray(wq, np.float32), wk=np.asarray(wk, np.float32),
                  wv=np.asarray(wv, np.float32), wo=np.asarray(wo, np.float32),
                  gA=gA, bstack=bstack, ident=ident, m_w=m_w, m_c=m_c,
                  ident_f32=np.eye(128, dtype=np.float32))
    x = np.asarray(x, np.float32)
    in_maps = []
    for b in range(BSZ):
        for half in range(2):
            if half == 0:
                xe = np.concatenate(
                    [np.zeros((HALO * 128, D), np.float32), x[b, 0:2048]], axis=0)
                ext_pos = np.concatenate([np.zeros(HALO * 128), np.arange(0, 2048)])
            else:
                xe = np.ascontiguousarray(x[b, 1024:4096])
                ext_pos = np.arange(1024, 4096)
            c2h, s2h = _rope_tables(ext_pos)
            npad = np.zeros((128, NT_OWN), np.float32)
            if half == 0:
                for i in range(NT_OWN):
                    npb = max(0, HALO - i)
                    if npb == 0:
                        continue
                    cnt = np.full(128, 128.0 * (npb - 1))
                    cnt += 127.0 - np.arange(128)
                    npad[:, i] = cnt * np.exp(-SM_SHIFT)
            m = dict(shared)
            m.update(x_ext=np.ascontiguousarray(xe.reshape(NT_EXT, 128, D)),
                     c2=c2h, s2=s2h, npad=npad,
                     fence_in=np.zeros((128, 16), np.float32))
            in_maps.append(m)
    return in_maps



class Kern:
    def __init__(self):
        self.nc = bass.Bass("TRN2", target_bir_lowering=False)
        self.ctx = ExitStack()
        self.P = Prog()
        self._declare()
        self.pa = PsumAlloc(self.ps_all)

    def _declare(self):
        nc = self.nc
        dt = lambda n, s, d, k: nc.dram_tensor(n, s, d, kind=k).ap()
        self.xd = dt("x_ext", [NT_EXT, 128, D], F32, "ExternalInput")
        self.wqd = dt("wq", [D, D], F32, "ExternalInput")
        self.wkd = dt("wk", [D, D], F32, "ExternalInput")
        self.wvd = dt("wv", [D, D], F32, "ExternalInput")
        self.wod = dt("wo", [D, D], F32, "ExternalInput")
        self.gAd = dt("gA", [D, GAW], F32, "ExternalInput")
        self.bsd = dt("bstack", [ER, D], F32, "ExternalInput")
        self.c2d = dt("c2", [128, NT_EXT * 64], BF16, "ExternalInput")
        self.s2d = dt("s2", [128, NT_EXT * 64], BF16, "ExternalInput")
        self.npd = dt("npad", [128, NT_OWN], F32, "ExternalInput")
        self.idd = dt("ident", [128, 128], BF16, "ExternalInput")
        self.mwd = dt("m_w", [128, 128], BF16, "ExternalInput")
        self.mcd = dt("m_c", [128, 128], BF16, "ExternalInput")
        self.fin_d = dt("fence_in", [128, 16], F32, "ExternalInput")
        self.idfd = dt("ident_f32", [128, 128], F32, "ExternalInput")
        self.fout_d = dt("fence_out", [128, 16], F32, "Internal")
        self.yd = dt("y", [NT_OWN, 128, D], F32, "ExternalOutput")
        SK = "ExternalOutput" if DEBUG else "Internal"
        self.kTs = dt("kT_s", [NT_EXT, 128, D], BF16, SK)
        self.vss = dt("v_s", [NT_EXT, 128, D], BF16, SK)
        self.us_ = dt("u_s", [NT_OWN, 128, D], F32, SK)
        self.unTs = dt("unT_s", [NT_OWN, 128, D], BF16, SK)
        self.qTs = dt("qT_s", [NT_OWN, 128, 1024], BF16, SK)
        self.oTs = dt("oT_s", [NT_OWN, 128, 1024], BF16, SK)

        sb = lambda n, s, d: self.ctx.enter_context(nc.sbuf_tensor(n, s, d))
        self.fence_sb = sb("fence_sb", [128, 16], F32)
        self.id_sb = sb("id_sb", [128, 128], BF16)
        self.idf_sb = sb("idf_sb", [128, 128], F32)
        self.diag = [sb("diag0", [128, 128], BF16), sb("diag1", [128, 128], BF16)]
        self.mw_sb = sb("mw_sb", [128, 128], BF16)
        self.mc_sb = sb("mc_sb", [128, 128], BF16)
        self.np_sb = sb("np_sb", [128, NT_OWN], F32)
        self.c2_sb = sb("c2_sb", [128, NT_EXT * 64], BF16)
        self.s2_sb = sb("s2_sb", [128, NT_EXT * 64], BF16)
        self.gA_sb = sb("gA_sb", [128, NCH * GAW], BF16)
        self.bs_sb = sb("bs_sb", [128, D], BF16)
        self.w1 = sb("w1", [128, NCH * D], BF16)
        self.w2 = sb("w2", [128, NCH * D], BF16)
        self.xu = [sb("xu0", [128, D], F32), sb("xu1", [128, D], F32)]
        self.xnT = [sb("xnT0", [128, D], BF16),
                    sb("xnT1", [128, D], BF16)]
        self.un_sb = sb("un_sb", [128, D], BF16)
        self.unT = [sb("unT0", [128, D], BF16), sb("unT1", [128, D], BF16)]
        self.buf_a = sb("buf_a", [128, D], BF16)
        self.buf_b = sb("buf_b", [128, D], BF16)
        self.kvo = sb("kvo", [128, D], BF16)
        self.p_sb = [sb("p_sb0", [128, NBLK * 128], BF16),
                     sb("p_sb1", [128, NBLK * 128], BF16)]
        self.pt_sb = sb("pt_sb", [128, NBLK * 128], BF16)
        self.oT_sb = sb("oT_sb", [128, D], BF16)
        mk2 = lambda n, w: [sb(n + "0", [128, w], F32), sb(n + "1", [128, w], F32)]
        self.st1 = mk2("st1", 1)
        self.ssq = mk2("ssq", 1)
        self.mean = mk2("mean", 1)
        self.nmean = mk2("nmean", 1)
        self.msq = mk2("msq", 1)
        self.var = mk2("var", 1)
        self.lnv = mk2("lnv", 1)
        self.rstd = mk2("rstd", 1)
        self.g8a = mk2("g8a", 8)
        self.g8b = mk2("g8b", 8)
        self.gm1 = mk2("gm1", 1)
        self.gm2 = mk2("gm2", 1)
        self.gdl = mk2("gdl", 1)
        self.ge = mk2("ge", 1)
        self.gden = mk2("gden", 1)
        self.grcp = mk2("grcp", 1)
        self.comb8 = mk2("comb8", 8)
        self.comb_er = [sb("comber", [128, ER], F32)] * 2
        self.ch_sb = [sb("ch0", [128, ER], BF16), sb("ch1", [128, ER], BF16)]
        self.chT_sb = [sb("chT0", [128, 128], BF16), sb("chT1", [128, 128], BF16)]
        self.nm_all = sb("nm_all", [128, 8], F32)
        self.sum_all = sb("sum_all", [128, 8], F32)
        self.enm = sb("enm", [128, 1], F32)
        self.corr = sb("corr", [128, 8], F32)
        self.rcp_s = sb("rcp_s", [128, 8], F32)
        self.ps_all = self.ctx.enter_context(
            nc.psum_tensor("ps_all", [128, 4096], F32))

    def ring_k(self, phase, slot):
        a = self.w2 if phase == 1 else self.w1
        return a[:, slot * 1024:(slot + 1) * 1024]

    def ring_v(self, phase, slot):
        a = self.w2 if phase == 1 else self.w1
        return a[:, (RING + slot) * 1024:(RING + slot + 1) * 1024]

    def _ring_load(self, phase, e, slot, waits):
        P = self.P
        cols = slice(0, 1024) if phase == 1 else slice(1024, 2048)
        P.load(self.ring_k(phase, slot), self.kTs[e][:, cols], waits=waits)
        return P.load(self.ring_v(phase, slot), self.vss[e][:, cols], waits=())

    def wload(self, wdram, dest, waits):
        P = self.P
        for g in range(4):
            src = wdram[g * 512:(g + 1) * 512, :].rearrange(
                "(c p) d -> p c d", p=128)
            dst = dest[:, g * 4 * D:(g + 1) * 4 * D].rearrange(
                "p (c d) -> p c d", c=4)
            P.op("st", lambda e, o=dst, i=src: e.dma_start(out=o, in_=i),
                 waits=waits, inc="qst")
        return P.sfence(self)

    def emit_ln(self, x_src, out_bf, par, ld_tok, out_war, stats_war,
                sq_war=None):
        P = self.P
        s = lambda b: b[par]
        t_s1 = P.op("dv", lambda e: e.reduce_sum(s(self.st1)[:], x_src[:],
                                                 axis=AX),
                    waits=[ld_tok, stats_war], drain=True, inc="dv")
        t_sq = P.op("ac", lambda e: e.activation(
            out_bf[:], x_src[:], AF.Square, accum_out=s(self.ssq)[:, 0:1]),
            waits=[ld_tok, stats_war, sq_war, out_war], drain=True, inc="ac")
        P.op("st", lambda e: e.tensor_scalar(
            s(self.mean)[:], s(self.st1)[:], 1.0 / D, None, ALU.mult),
            waits=[t_s1, stats_war])
        P.op("st", lambda e: e.tensor_scalar(
            s(self.nmean)[:], s(self.mean)[:], -1.0, None, ALU.mult))
        P.op("st", lambda e: e.tensor_mul(
            s(self.msq)[:], s(self.mean)[:], s(self.mean)[:]))
        P.op("st", lambda e: e.tensor_scalar(
            s(self.var)[:], s(self.ssq)[:], 1.0 / D, 1e-5, ALU.mult, ALU.add),
            waits=[t_sq])
        t_po = P.op("st", lambda e: e.tensor_sub(
            s(self.var)[:], s(self.var)[:], s(self.msq)[:]), inc="po")
        P.op("ac", lambda e: e.activation(
            s(self.lnv)[:], s(self.var)[:, 0:1], AF.Ln),
            waits=[t_po], drain=True)
        t_r = P.op("ac", lambda e: e.activation(
            s(self.rstd)[:], s(self.lnv)[:, 0:1], AF.Exp, scale=-0.5),
            drain=True, inc="ac")
        t_ap = P.op("dv", lambda e: e.tensor_scalar(
            out_bf[:], x_src[:], s(self.nmean)[:, 0:1], s(self.rstd)[:, 0:1],
            ALU.add, ALU.mult), waits=[t_r, t_po, out_war], inc="dv")
        return t_ap, t_sq

    def emit_transposes(self, src_bf, dst_bf, src_tok, dst_war):
        P = self.P
        pe_last = None
        blks = []
        for half in range(2):
            blk = self.pa.alloc(2)
            t_mm = None
            for c in range(8):
                cc = half * 8 + c
                t_mm = P.mm(blk.ap(c * 128, (c + 1) * 128),
                            src_bf[:, cc * 128:(cc + 1) * 128], self.id_sb[:],
                            True, True,
                            waits=([src_tok, self.t_id] + blk.waits)
                            if c == 0 else (),
                            inc=(c == 7))
            blks.append((blk, t_mm))
            pe_last = t_mm
        ev = None
        for half, (blk, t_mm) in enumerate(blks):
            ev = P.op("ac", lambda e, h=half, b=blk: e.activation(
                dst_bf[:, h * 1024:(h + 1) * 1024], b.ap(0, 1024), AF.Copy),
                waits=[t_mm, dst_war], inc="ac" if half == 1 else None)
        for blk, _ in blks:
            blk.free(ev)
        return pe_last, ev

    def emit_proj(self, lhs_fn, rhs_sb, lhs_tok, rhs_tok, out_bf=None,
                  out_war=None, x_dst=None, x_tok=None, evac_scale=None):
        P = self.P
        pe_last = None
        blks = []
        for half in range(2):
            blk = self.pa.alloc(2)
            t_mm = None
            for c in range(NCH):
                for s2 in range(2):
                    col = half * 1024 + s2 * 512
                    t_mm = P.mm(
                        blk.ap(s2 * 512, (s2 + 1) * 512), lhs_fn(c),
                        rhs_sb[:, c * D + col:c * D + col + 512],
                        start=(c == 0), stop=(c == NCH - 1),
                        waits=([lhs_tok, rhs_tok] + blk.waits)
                        if (c == 0 and s2 == 0) else (),
                        inc=(c == NCH - 1 and s2 == 1))
            blks.append((blk, t_mm))
            pe_last = t_mm
        eng = "dv" if x_dst is not None else "ac"
        out_tok = None
        for half, (blk, t_mm) in enumerate(blks):
            kinc = eng if half == 1 else None
            if x_dst is not None:
                out_tok = P.op("dv", lambda e, h=half, b=blk: e.tensor_add(
                    x_dst[:, h * 1024:(h + 1) * 1024],
                    x_dst[:, h * 1024:(h + 1) * 1024], b.ap(0, 1024)),
                    waits=[t_mm, x_tok], inc=kinc)
            else:
                kw = {} if evac_scale is None else {"scale": evac_scale}
                out_tok = P.op("ac", lambda e, h=half, b=blk, kw=kw: e.activation(
                    out_bf[:, h * 1024:(h + 1) * 1024], b.ap(0, 1024),
                    AF.Copy, **kw),
                    waits=[t_mm, out_war], inc=kinc)
        for blk, _ in blks:
            blk.free(out_tok)
        return pe_last, out_tok

    def emit_rope(self, src_bf, tile_e, src_tok):
        P = self.P
        off = tile_e * 64
        csl = self.c2_sb[:, off:off + 64]
        ssl = self.s2_sb[:, off:off + 64]
        cbc = AP(csl.tensor, csl.offset,
                 [list(csl.ap[0]), [0, NH], [0, 2], [1, 64]])
        sbc = AP(ssl.tensor, ssl.offset,
                 [list(ssl.ap[0]), [0, NH], [0, 2], [1, 64]])
        sv = src_bf[:]
        rot = AP(sv.tensor, sv.offset + 64,
                 [list(sv.ap[0]), [128, NH], [-64, 2], [1, 64]])
        t4 = lambda t: t[:].rearrange("p (h x j) -> p h x j", h=NH, x=2)
        P.op("dv", lambda e: e.tensor_tensor(
            t4(self.buf_b), t4(src_bf), cbc, ALU.mult),
            waits=[src_tok, self.t_tab])
        P.op("dv", lambda e: e.tensor_tensor(t4(self.un_sb), rot, sbc, ALU.mult))
        fh = lambda t: AP(t[:].tensor, t[:].offset,
                          [list(t[:].ap[0]), [128, NH], [1, 64]])
        sh = lambda t: AP(t[:].tensor, t[:].offset + 64,
                          [list(t[:].ap[0]), [128, NH], [1, 64]])
        P.op("dv", lambda e: e.tensor_tensor(
            fh(src_bf), fh(self.buf_b), fh(self.un_sb), ALU.add))
        return P.op("dv", lambda e: e.tensor_tensor(
            sh(src_bf), sh(self.buf_b), sh(self.un_sb), ALU.subtract),
            drain=True, inc="dv")

    def emit_ffn(self, actT_sb, actT_tok, x_dst, x_tok, par, war):
        P = self.P
        s = lambda b: b[par]
        gblk = self.pa.alloc(1)
        t_g = None
        for c in range(NCH):
            t_g = P.mm(gblk.ap(0, GAW), actT_sb[:, c * 128:(c + 1) * 128],
                       self.gA_sb[:, c * GAW:(c + 1) * GAW],
                       start=(c == 0), stop=(c == NCH - 1),
                       waits=([actT_tok, self.t_gA] + gblk.waits)
                       if c == 0 else (),
                       inc=(c == NCH - 1))
        lg = gblk.ap(0, 8)
        hh = gblk.ap(8, GAW)
        P.op("dv", lambda e: e.reduce_max(s(self.gm1)[:], lg, axis=AX),
             waits=[t_g, war], drain=True)
        P.op("dv", lambda e: e.tensor_single_scalar(
            s(self.g8a)[:], lg, s(self.gm1)[:, 0:1], ALU.is_ge), drain=True)
        P.op("dv", lambda e: e.scalar_tensor_tensor(
            s(self.g8b)[:], s(self.g8a)[:], NEG, lg, ALU.mult, ALU.add),
            drain=True)
        P.op("dv", lambda e: e.reduce_max(s(self.gm2)[:], s(self.g8b)[:], axis=AX),
             drain=True)
        P.op("dv", lambda e: e.tensor_single_scalar(
            s(self.g8b)[:], s(self.g8b)[:], s(self.gm2)[:, 0:1], ALU.is_ge),
            drain=True)
        t_d = P.op("dv", lambda e: e.tensor_sub(
            s(self.gdl)[:], s(self.gm2)[:], s(self.gm1)[:]), drain=True, inc="dv")
        t_e = P.op("ac", lambda e: e.activation(
            s(self.ge)[:], s(self.gdl)[:, 0:1], AF.Exp),
            waits=[t_d], drain=True, inc="ac")
        P.op("dv", lambda e: e.tensor_scalar(
            s(self.gden)[:], s(self.ge)[:], 1.0, None, ALU.add),
            waits=[t_e], drain=True)
        P.op("dv", lambda e: e.reciprocal(s(self.grcp)[:], s(self.gden)[:]),
             drain=True)
        P.op("dv", lambda e: e.scalar_tensor_tensor(
            s(self.comb8)[:], s(self.g8b)[:], s(self.ge)[:, 0:1], s(self.g8a)[:],
            ALU.mult, ALU.add), drain=True)
        P.op("dv", lambda e: e.tensor_single_scalar(
            s(self.comb8)[:], s(self.comb8)[:], s(self.grcp)[:, 0:1], ALU.mult),
            drain=True)
        c8 = s(self.comb8)[:]
        cbc = AP(c8.tensor, c8.offset, [list(c8.ap[0]), [1, 8], [0, 16]])
        P.op("dv", lambda e: e.tensor_copy(
            s(self.comb_er)[:].rearrange("p (x r) -> p x r", x=8), cbc),
            drain=True)
        t_ch = P.op("dv", lambda e: e.tensor_mul(
            s(self.ch_sb)[:], hh, s(self.comb_er)[:]), inc="dv")
        gblk.free(t_ch)
        cblk = self.pa.alloc(1)
        t_ct = P.mm(cblk.ap(0, 128), s(self.ch_sb)[:], self.id_sb[:], True, True,
                    waits=[t_ch] + cblk.waits, inc=True)
        t_ce = P.op("dv", lambda e, b=cblk: e.tensor_copy(
            s(self.chT_sb)[:], b.ap(0, 128)), waits=[t_ct], inc="dv")
        cblk.free(t_ce)
        blks = []
        for half in range(2):
            blk = self.pa.alloc(2)
            t_mm = None
            for s2 in range(2):
                col = half * 1024 + s2 * 512
                t_mm = P.mm(blk.ap(s2 * 512, (s2 + 1) * 512), s(self.chT_sb)[:],
                            self.bs_sb[:, col:col + 512], True, True,
                            waits=([t_ce] + blk.waits) if s2 == 0 else (),
                            inc=(s2 == 1))
            blks.append((blk, t_mm))
        t_add = None
        for half, (blk, t_mm) in enumerate(blks):
            t_add = P.op("dv", lambda e, h=half, b=blk: e.tensor_add(
                x_dst[:, h * 1024:(h + 1) * 1024],
                x_dst[:, h * 1024:(h + 1) * 1024], b.ap(0, 1024)),
                waits=[t_mm, x_tok], inc="dv" if half == 1 else None)
        for blk, _ in blks:
            blk.free(t_add)
        return t_add, t_g

    def emit_attn_tile(self, phase, i, qsrc, q_tok, ring_tok, oT_off, oT_war):
        P = self.P
        ps = self.ps_all
        SREG = (ps[:, 0:1152], ps[:, 1536:2688])
        SLACK = (ps[:, 1216:1344], ps[:, 2752:2880])
        PTA = ps[:, 3072:4096]
        AVR = ps[:, 3072:3200]
        claim = []
        for b in range(8):
            claim += self.pa.tok[b]
            self.pa.tok[b] = []
        s_tok = [None] * 8
        exp_tok = [None] * 8
        oev_tok = [None] * 8
        pt_tok = [None] * 8
        av_tok = [None] * 8
        peb_tok = [None] * 8
        diag_tok = [None] * 8

        def emit_S(h):
            reg = SREG[h % 2]
            w = [q_tok, ring_tok, self.t_mask, self.t_id, self.t_np]
            if h >= 2:
                w += [exp_tok[h - 2]]
                if h >= 4:
                    w += [peb_tok[h - 4]]
            else:
                w = w + claim
            P.mm(reg[:, 0:128], self.id_sb[:], self.mw_sb[:], True, False,
                 waits=w)
            P.mm(reg[:, 8 * 128:9 * 128], self.id_sb[:], self.mc_sb[:],
                 True, False)
            t = None
            for b in range(NBLK):
                slot = (i + b) % RING
                t = P.mm(reg[:, b * 128:(b + 1) * 128],
                         qsrc[:, h * 128:(h + 1) * 128],
                         self.ring_k(phase, slot)[:, h * 128:(h + 1) * 128],
                         start=(b not in (0, 8)), stop=True,
                         inc=(b == NBLK - 1))
            s_tok[h] = t

        emit_S(0)
        emit_S(1)
        pea_tok = [None] * 8

        rcp_tok = [None] * 8

        def stage_A(h):
            reg = SREG[h % 2]
            t_nm = P.op("dv", lambda e, h=h, r=reg: e.reduce_max(
                self.nm_all[:, 1:2], r[:, 0:128], axis=AX, negate=True),
                waits=[s_tok[h]], drain=True, inc="dv")
            t_exp = P.op("ac", lambda e, h=h, r=reg: e.activation(
                self.p_sb[h % 2][:], r, AF.Exp, bias=self.nm_all[:, 0:1],
                accum_out=self.sum_all[:, h:h + 1]),
                waits=[t_nm, pt_tok[h - 2] if h >= 2 else None,
                       self.t_shift],
                drain=True, inc="ac")
            exp_tok[h] = t_exp
            t_corr = P.op("st", lambda e, h=h, i=i: e.tensor_sub(
                self.corr[:, h:h + 1], self.sum_all[:, h:h + 1],
                self.np_sb[:, i:i + 1]),
                waits=[t_exp], inc="po")
            P.op("dv", lambda e, h=h: e.reciprocal(
                self.rcp_s[:, h:h + 1], self.corr[:, h:h + 1]),
                waits=[t_corr], drain=True, inc="dv")
            rcp_tok[h] = P.op("dv", lambda e, h=h: e.tensor_scalar(
                self.p_sb[h % 2][:], self.p_sb[h % 2][:],
                self.rcp_s[:, h:h + 1], None, ALU.mult), inc="dv")

        stage_A(0)
        stage_A(1)
        for h in range(8):
            dg = self.id_sb
            w0 = [rcp_tok[h]]
            if h >= 1:
                w0 += [pea_tok[h - 1], oev_tok[h - 1]]
            else:
                w0 = w0 + claim
            t_pta = None
            pp = self.p_sb[h % 2]
            for b in range(8):
                t_pta = P.mm(PTA[:, b * 128:(b + 1) * 128],
                             pp[:, b * 128:(b + 1) * 128], dg[:], True, True,
                             waits=w0 if b == 0 else (), inc=(b == 7))
            eng = "ac" if h % 2 == 0 else "dv"
            if eng == "ac":
                pea_tok[h] = P.op("ac", lambda e: e.activation(
                    self.pt_sb[:, 0:1024], PTA, AF.Copy),
                    waits=[t_pta, av_tok[h - 1] if h >= 1 else None], inc="ac")
            else:
                pea_tok[h] = P.op("dv", lambda e: e.tensor_copy(
                    self.pt_sb[:, 0:1024], PTA),
                    waits=[t_pta, av_tok[h - 1] if h >= 1 else None], inc="dv")
            if h + 2 < 8:
                emit_S(h + 2)
            t_ptb = P.mm(SLACK[h % 2], pp[:, 8 * 128:9 * 128], dg[:],
                         True, True,
                         waits=[peb_tok[h - 2]] if h >= 2 else (), inc=True)
            pt_tok[h] = t_ptb
            if eng == "ac":
                peb_tok[h] = P.op("ac", lambda e, h=h: e.activation(
                    self.pt_sb[:, 1024:1152], SLACK[h % 2], AF.Copy),
                    waits=[t_ptb, av_tok[h - 1] if h >= 1 else None], inc="ac")
            else:
                peb_tok[h] = P.op("dv", lambda e, h=h: e.tensor_copy(
                    self.pt_sb[:, 1024:1152], SLACK[h % 2]),
                    waits=[t_ptb, av_tok[h - 1] if h >= 1 else None], inc="dv")
            if h + 2 < 8:
                stage_A(h + 2)
            t_av = None
            for b in range(NBLK):
                slot = (i + b) % RING
                w_av = ()
                if b == 0:
                    w_av = [pea_tok[h]]
                    if h >= 1:
                        w_av += [oev_tok[h - 1]]
                elif b == NBLK - 1:
                    w_av = [peb_tok[h]]
                t_av = P.mm(AVR,
                            self.ring_v(phase, slot)[:, h * 128:(h + 1) * 128],
                            self.pt_sb[:, b * 128:(b + 1) * 128],
                            start=(b == 0), stop=(b == NBLK - 1),
                            waits=w_av, inc=(b == NBLK - 1))
            av_tok[h] = t_av
            oev_tok[h] = P.op("ac", lambda e, h=h, o=oT_off: e.activation(
                self.oT_sb[:, o + h * 128:o + (h + 1) * 128], AVR, AF.Copy),
                waits=[t_av, oT_war], inc="ac")
        for b in (0, 1, 2):
            self.pa.tok[b] = [exp_tok[6], peb_tok[6]]
        for b in (3, 4, 5):
            self.pa.tok[b] = [exp_tok[7], peb_tok[7]]
        for b in (6, 7):
            self.pa.tok[b] = [oev_tok[7], peb_tok[7]]
        self.pa.ptr = 0
        return av_tok[7], oev_tok

    def build(self):
        P = self.P
        P.load(self.fence_sb[:], self.fin_d[:])
        P.load(self.id_sb[:], self.idd[:])
        P.load(self.idf_sb[:], self.idfd[:])
        P.load(self.mw_sb[:], self.mwd[:])
        P.load(self.mc_sb[:], self.mcd[:])
        P.load(self.np_sb[:], self.npd[:])
        P.load(self.c2_sb[:], self.c2d[:])
        P.load(self.s2_sb[:], self.s2d[:])
        self.p_free = None
        self.pt_free = None
        self.diag_war = [None, None]
        xu_free = [[], []]

        src = self.gAd.rearrange("(c p) w -> p c w", p=128)
        dst = self.gA_sb[:].rearrange("p (c w) -> p c w", c=NCH)
        P.op("st", lambda e, o=dst, i=src: e.dma_start(out=o, in_=i),
             inc="qst")
        P.op("st", lambda e: e.dma_start(out=self.bs_sb[:], in_=self.bsd[:]),
             inc="qst")
        self.wload(self.wkd, self.w1, ())
        t_w = self.wload(self.wvd, self.w2, ())
        self.t_gA = t_w
        w1_tok = t_w
        w2_tok = t_w

        x_ld = [None] * NT_EXT
        P.load(self.xu[0][:], self.xd[0])
        P.load(self.xu[1][:], self.xd[1])
        lf0 = P.lfence(self)
        self.t_id = self.t_mask = self.t_np = self.t_tab = lf0
        self.t_shift = P.op("st", lambda e: e.tensor_scalar(
            self.nm_all[:, 0:1], self.np_sb[:, 0:1], 0.0, -SM_SHIFT,
            ALU.mult, ALU.add), waits=[lf0], inc="po")
        x_ld[0] = lf0
        x_ld[1] = lf0

        kv_stored = [None] * NT_EXT
        unT_stored = [None] * NT_OWN
        un_free = None
        unT_free = [[], []]
        ln_war = [None, None]
        gate_war = [None, None]
        bufa_free = None
        kvo_free = []
        v_pend = None
        w1_last = None
        w2_last = None
        rope_prev = None

        xnT_ev = [None] * NT_EXT
        gate_free = [None, None]

        def emit_front_ln(e):
            sl = e % 2
            t_xn, _ = self.emit_ln(self.xu[sl], self.xnT[sl], sl, x_ld[e],
                                   gate_free[sl], ln_war[sl], sq_war=rope_prev)
            ln_war[sl] = t_xn
            return (e, t_xn)

        def emit_front_trs(pend):
            e, t_xn = pend
            sl = e % 2
            t_pe, t_ev = self.emit_transposes(self.xnT[sl], self.xnT[sl],
                                              t_xn, None)
            xnT_ev[e] = t_ev

        vstore_f = []

        def emit_v_stage(ve, vsl, vtok):
            lhsv = lambda c, s=vsl: self.unT[s][:, c * 128:(c + 1) * 128]
            v_pe, t_vev = self.emit_proj(lhsv, self.w2, vtok, w2_tok,
                                         out_bf=self.oT_sb, out_war=vstore_f)
            unT_free[vsl] = unT_free[vsl] + [v_pe]
            P.store(self.vss[ve], self.oT_sb[:], waits=[t_vev])
            kv_stored[ve] = P.sfence(self)
            return v_pe, [kv_stored[ve]]

        def emit_kt_stage(ke, t_kev, kvw):
            t_kr = self.emit_rope(self.buf_a, ke, t_kev)
            t_kt_pe, t_ktev = self.emit_transposes(
                self.buf_a, self.kvo, t_kr, kvw)
            P.store(self.kTs[ke], self.kvo[:], waits=[t_ktev])
            t_ksf = P.sfence(self)
            return t_kr, t_kt_pe, [t_ksf]

        def emit_k_stage(ke, ksl, ktok, baw):
            lhsk = lambda c, s=ksl: self.unT[s][:, c * 128:(c + 1) * 128]
            k_pe, t_kev = self.emit_proj(lhsk, self.w1, ktok, w1_tok,
                                         out_bf=self.buf_a, out_war=baw)
            unT_free[ksl] = unT_free[ksl] + [k_pe]
            return k_pe, t_kev

        ln1_pend = emit_front_ln(0)
        kt_pend = None
        k_pend = None
        for e in range(NT_EXT):
            sl = e % 2
            self.pa.ptr = (2 * e) % 8
            if v_pend is not None:
                ve, vsl, vtok = v_pend
                w2_last, vstore_f = emit_v_stage(ve, vsl, vtok)
                v_pend = None
            emit_front_trs(ln1_pend)
            if kt_pend is not None:
                ke, t_kev = kt_pend
                rope_prev, bufa_free, kvo_free = emit_kt_stage(
                    ke, t_kev, kvo_free)
                kt_pend = None
            t_u, t_g = self.emit_ffn(self.xnT[sl], xnT_ev[e], self.xu[sl],
                                     x_ld[e], sl, gate_war[sl])
            gate_free[sl] = t_g
            gate_war[sl] = t_u
            t_usf = None
            if e >= HALO:
                P.store(self.us_[e - HALO], self.xu[sl][:], waits=[t_u])
                t_usf = P.sfence(self)
            t_un, t_sq2 = self.emit_ln(self.xu[sl], self.un_sb, sl, t_u,
                                       un_free, ln_war[sl])
            ln_war[sl] = t_un
            xu_free[sl] = [t_un, t_sq2] + ([t_usf] if t_usf else [])
            if e + 2 < NT_EXT:
                P.load(self.xu[sl][:], self.xd[e + 2], waits=[xu_free[sl]])
                x_ld[e + 2] = P.lfence(self)
            if k_pend is not None:
                ke, ksl, ktok = k_pend
                w1_last, t_kev = emit_k_stage(ke, ksl, ktok, bufa_free)
                kt_pend = (ke, t_kev)
                k_pend = None
            t_unT_pe, t_unT_ev = self.emit_transposes(
                self.un_sb, self.unT[sl], t_un, unT_free[sl])
            un_free = t_unT_pe
            if e >= HALO:
                unT_stored[e - HALO] = P.store(self.unTs[e - HALO],
                                               self.unT[sl][:], waits=[t_unT_ev])
            if e + 1 < NT_EXT:
                ln1_pend = emit_front_ln(e + 1)
            unT_free[sl] = [t_unT_pe]
            k_pend = (e, sl, t_unT_ev)
            v_pend = (e, sl, t_unT_ev)
        ke, ksl, ktok = k_pend
        w1_last, t_kev23 = emit_k_stage(ke, ksl, ktok, bufa_free)
        rope_prev, bufa_free, kvo_free = emit_kt_stage(*kt_pend, kvo_free)
        ve, vsl, vtok = v_pend
        w2_last, vstore_f = emit_v_stage(ve, vsl, vtok)
        rope_prev, bufa_free, kvo_free = emit_kt_stage(ke, t_kev23, kvo_free)

        w1_tok = self.wload(self.wqd, self.w1, [w1_last])

        ring_free = [None] * NT_EXT
        ring_f = [None] * NT_EXT
        for e2 in range(HALO):
            self._ring_load(1, e2, e2, [kv_stored[e2], w2_last])
        P.load(self.unT[0][:], self.unTs[0],
               waits=[kv_stored[NT_EXT - 1], unT_free[0]])
        P.load(self.unT[1][:], self.unTs[1], waits=[unT_free[1]])
        lfB1 = P.lfence(self)
        for e2 in range(HALO):
            ring_f[e2] = lfB1
        unTl_f = [None] * NT_OWN
        unTl_f[0] = unTl_f[1] = lfB1

        qT_src_tok = [None] * NT_OWN
        s_last = None
        qT_store = [None] * NT_OWN
        sf1 = [None] * NT_OWN
        attn_prev = None
        self.p_free = None
        self.pt_free = None

        q_pe_by_slot = [None, None]
        for i in range(NT_OWN):
            sl = i % 2
            lhsq = lambda c, s=sl: self.unT[s][:, c * 128:(c + 1) * 128]
            q_pe, t_qev = self.emit_proj(lhsq, self.w1, unTl_f[i], w1_tok,
                                         out_bf=self.buf_a, out_war=bufa_free,
                                         evac_scale=SCALE_Q)
            q_pe_by_slot[sl] = q_pe
            if i + 2 < NT_OWN:
                P.load(self.unT[sl][:], self.unTs[i + 2], waits=[q_pe])
            if i + HALO < NT_EXT:
                self._ring_load(1, i + HALO, (i + HALO) % RING,
                                [ring_free[i - 2] if i >= 2 else None])
            lf1 = P.lfence(self)
            if i + 2 < NT_OWN:
                unTl_f[i + 2] = lf1
            if i + HALO < NT_EXT:
                ring_f[i + HALO] = lf1
            if attn_prev is not None:
                ip = attn_prev
                t_avp, evsp = self.emit_attn_tile(
                    1, ip, self.kvo[:, 0:1024], qT_src_tok[ip],
                    ring_f[ip + HALO], 0,
                    sf1[ip] if ip >= 1 else kv_stored[NT_EXT - 1])
                ring_free[ip] = t_avp
                P.store(self.oTs[ip], self.oT_sb[:, 0:1024], waits=[evsp[-1]])
                s_last = t_avp
            t_qr = self.emit_rope(self.buf_a, i + HALO, t_qev)
            t_qt_pe, t_qtev = self.emit_transposes(
                self.buf_a, self.kvo, t_qr,
                [s_last, sf1[i - 1] if i >= 1 else kvo_free[0]])
            bufa_free = t_qt_pe
            qT_src_tok[i] = t_qtev
            qT_store[i] = P.store(self.qTs[i], self.kvo[:, 1024:2048],
                                  waits=[t_qtev])
            sf1[i] = P.sfence(self)
            attn_prev = i
        ip = attn_prev
        t_avp, evsp = self.emit_attn_tile(1, ip, self.kvo[:, 0:1024],
                                          qT_src_tok[ip], ring_f[ip + HALO],
                                          0, sf1[ip])
        ring_free[ip] = t_avp
        P.store(self.oTs[ip], self.oT_sb[:, 0:1024], waits=[evsp[-1]])
        sf1_final = P.sfence(self)
        b1_ring_last = t_avp
        b1_q_last = q_pe_by_slot[1]

        w2_tok = self.wload(self.wod, self.w2, [b1_ring_last])

        ring_free2 = [None] * NT_EXT
        ring_f2 = [None] * NT_EXT
        for e2 in range(HALO + 1):
            self._ring_load(2, e2, e2, [kv_stored[e2], b1_q_last])
        u_ld = [None] * NT_OWN
        qT_ld = [None] * NT_OWN
        oT_ld = [None] * NT_OWN
        y_stored = [None] * NT_OWN
        P.load(self.xu[0][:], self.us_[0], waits=[xu_free[0], sf1_final])
        P.load(self.xu[1][:], self.us_[1], waits=[xu_free[1]])
        P.load(self.un_sb[:, 0:1024], self.qTs[0], waits=[un_free])
        P.load(self.un_sb[:, 1024:2048], self.qTs[1])
        P.load(self.unT[0][:, 0:1024], self.oTs[0], waits=[q_pe_by_slot[0]])
        P.load(self.unT[1][:, 0:1024], self.oTs[1], waits=[q_pe_by_slot[1]])
        lfB2 = P.lfence(self)
        for e2 in range(HALO + 1):
            ring_f2[e2] = lfB2
        u_ld[0] = u_ld[1] = lfB2
        qT_ld[0] = qT_ld[1] = lfB2
        oT_ld[0] = oT_ld[1] = lfB2

        self.p_free = None
        self.pt_free = None
        oT_war2 = None
        ffn_pend = None
        ln_war2 = [None, None]
        gate_war2 = [None, None]
        self.xnT_free2 = [None, None]

        for i in range(NT_OWN):
            sl = i % 2
            if i + 1 + HALO < NT_EXT:
                self._ring_load(2, i + 1 + HALO, (i + 1 + HALO) % RING,
                                [ring_free2[i - 1] if i >= 1 else None])
            if ffn_pend is not None:
                ip2, slp, t_anp = ffn_pend
                t_yf = self._b2_tail(ip2, slp, t_anp, gate_war2)
                y_stored[ip2] = t_yf
                xu_free[slp] = [t_yf]
                if ip2 + 2 < NT_OWN:
                    P.load(self.xu[slp][:], self.us_[ip2 + 2], waits=[t_yf])
                ffn_pend = None
            t_av2, evs2 = self.emit_attn_tile(
                2, i, self.un_sb[:, sl * 1024:(sl + 1) * 1024], qT_ld[i],
                ring_f2[i + HALO], 1024, oT_war2)
            ring_free2[i] = t_av2
            if i + 2 < NT_OWN:
                P.load(self.un_sb[:, sl * 1024:(sl + 1) * 1024],
                       self.qTs[i + 2], waits=[t_av2])
            def lhso(c, s=sl):
                if c < 8:
                    return self.unT[s][:, c * 128:(c + 1) * 128]
                return self.oT_sb[:, 1024 + (c - 8) * 128:1024 + (c - 7) * 128]
            o_pe, t_a = self.emit_proj(lhso, self.w2,
                                       [oT_ld[i], evs2[-1]], w2_tok,
                                       x_dst=self.xu[sl], x_tok=u_ld[i])
            oT_war2 = o_pe
            if i + 2 < NT_OWN:
                P.load(self.unT[sl][:, 0:1024], self.oTs[i + 2], waits=[o_pe])
            lf2 = P.lfence(self)
            if i + 2 < NT_OWN:
                qT_ld[i + 2] = lf2
                oT_ld[i + 2] = lf2
            if i + 1 < NT_OWN:
                u_ld[i + 1] = lf2 if u_ld[i + 1] is None else u_ld[i + 1]
            if i + 1 + HALO < NT_EXT:
                ring_f2[i + 1 + HALO] = lf2
            t_an, _ = self.emit_ln(self.xu[sl], self.xnT[sl], sl, t_a,
                                   self.xnT_free2[sl], ln_war2[sl])
            ln_war2[sl] = t_an
            ffn_pend = (i, sl, t_an)
        ip2, slp, t_anp = ffn_pend
        y_stored[ip2] = self._b2_tail(ip2, slp, t_anp, gate_war2)

        total = P.cnt["qst"]
        P.op("st", lambda e: e.nop(), waits=[("qst", total)])

        nc = self.nc
        with ExitStack() as bctx:
            sems = {}
            for s in Prog.SEMS:
                sems[s] = bctx.enter_context(nc.semaphore(s))
            blk = bctx.enter_context(nc.Block())

            @blk.tensor
            def _(e):
                P.replay("pe", e, sems)

            @blk.vector
            def _(e):
                P.replay("dv", e, sems)

            @blk.scalar
            def _(e):
                P.replay("ac", e, sems)

            @blk.sync
            def _(e):
                P.replay("ld", e, sems)

            @blk.gpsimd
            def _(e):
                P.replay("st", e, sems)
        self.ctx.close()
        return nc

    def _b2_tail(self, i, sl, t_an, gate_war2):
        P = self.P
        t_anT_pe, t_anT_ev = self.emit_transposes(self.xnT[sl], self.xnT[sl],
                                                  t_an, None)
        t_y, t_g2 = self.emit_ffn(self.xnT[sl], t_anT_ev, self.xu[sl],
                                  t_an, sl, gate_war2[sl])
        self.xnT_free2[sl] = t_g2
        gate_war2[sl] = t_y
        P.store(self.yd[i], self.xu[sl][:], waits=[t_y])
        return P.sfence(self)


def kernel(L0_attn_hidden_states, L0_attn_wq, L0_attn_wk, L0_attn_wv,
           L0_attn_wo, L0_ffn_gate_w, L0_ffn_lora_A, L0_ffn_lora_B):
    x = np.asarray(L0_attn_hidden_states, np.float32)
    in_maps = _host_prepare(x, L0_attn_wq, L0_attn_wk, L0_attn_wv, L0_attn_wo,
                            L0_ffn_gate_w, L0_ffn_lora_A, L0_ffn_lora_B)
    k = Kern()
    nc = k.build()
    res = run_bass_kernel_spmd(nc, in_maps, core_ids=list(range(8)))
    out = np.empty((BSZ, SEQ, D), np.float32)
    for c in range(8):
        b, half = c // 2, c % 2
        y = np.asarray(res.results[c]["y"], np.float32).reshape(2048, D)
        out[b, half * 2048:(half + 1) * 2048, :] = y
    return out
